# version 18
# baseline (speedup 1.0000x reference)
import numpy as np
from contextlib import ExitStack

import concourse.bass as bass
import concourse.mybir as mybir
from concourse.bass_utils import run_bass_kernel_spmd

F32 = mybir.dt.float32
BF16 = mybir.dt.bfloat16
OP = mybir.AluOpType
ACT = mybir.ActivationFunctionType
AX = mybir.AxisListType

C = 1024
H = 16
NDIM = 64
TILE = 512
NCC = 8
TPT = 4
NT = 4
TE = 2048
EPS_LN = 1e-5
EPS_GN = 1e-5 * 64
RING = 2
STOP = None
DBG_Y = "both"
DBG_TAP = None


class Sem:
    def __init__(self, st, nc, name):
        self.h = st.enter_context(nc.semaphore(name))
        self.v = 0

    def tick(self, n=1):
        self.v += n
        return (self, self.v)


def _inc(instr, sem, n=1):
    instr.then_inc(sem.h, n)
    return sem.tick(n)


def _wt(eng, evt):
    if evt is not None:
        eng.wait_ge(evt[0].h, evt[1])


class Banks:
    def __init__(self, ps):
        self.ps = ps
        self.rel = [None] * 8

    def ap(self, b, lo=0, hi=512):
        return self.ps[:, b * 512 + lo: b * 512 + hi]

    def span(self, b0, nbank):
        return self.ps[:, b0 * 512:(b0 + nbank) * 512]

    def acquire(self, eng, b):
        _wt(eng, self.rel[b])
        self.rel[b] = None

    def release(self, b, evt):
        self.rel[b] = evt


def build_nc(nt=NT):
    _stage = [0]

    def go():
        _stage[0] += 1
        return STOP is None or _stage[0] <= STOP

    nc = bass.Bass("TRN2", target_bir_lowering=False,
                   detect_race_conditions=False,
                   dynamic_dma_scratch_size=8192)

    te = nt * TILE
    x_slab = nc.dram_tensor("x_slab", [2 * te, C], F32, kind="ExternalInput").ap()
    wT = {}
    for w in ("wk", "wv", "wr", "wg", "wo", "wfr", "wfk", "wfv"):
        wT[w] = nc.dram_tensor(w, [C, C], F32, kind="ExternalInput").ap()
    vec_d = {}
    for m in ("maak", "maav", "maar", "maag", "mur", "muk"):
        vec_d[m] = nc.dram_tensor(m, [128, 8], F32, kind="ExternalInput").ap()
    dec_d = nc.dram_tensor("dec_row", [1, 16], F32, kind="ExternalInput").ap()
    faa_d = nc.dram_tensor("faa_row", [1, 16], F32, kind="ExternalInput").ap()
    lnw_d = nc.dram_tensor("lnw_row", [1, C], F32, kind="ExternalInput").ap()
    lnb_d = nc.dram_tensor("lnb_row", [1, C], F32, kind="ExternalInput").ap()
    out_d = nc.dram_tensor("out", [te, C], F32, kind="ExternalOutput").ap()
    wsp_d = {w: nc.dram_tensor("wsp_" + w, [NCC * 128, 2 * C], BF16,
                               kind="Internal").ap() for w in
             ("wk", "wv", "wr", "wg", "wo", "wfk", "wfr", "wfv")}

    jj, ii = np.meshgrid(np.arange(128), np.arange(128), indexing="ij")
    c_grid = nc.inline_tensor((jj + 1 - ii).astype(np.float32), name="c_gridt")
    c_tri = nc.inline_tensor((ii > jj).astype(np.float32), name="c_trit")
    c_diag = nc.inline_tensor((ii == jj).astype(np.float32), name="c_diagt")
    c_rampi = nc.inline_tensor(
        np.broadcast_to(-np.arange(128, dtype=np.float32), (128, 128)).copy(),
        name="c_rampi")
    c_rampj = nc.inline_tensor(
        np.broadcast_to((np.arange(128, dtype=np.float32) - 127.0)[:, None],
                        (128, 64)).copy(), name="c_rampj")
    c_ident = nc.inline_tensor(np.eye(128, dtype=np.float32), name="c_ident")

    A = nc.alloc_sbuf_tensor
    XNT = A("XNT", [128, NCC * TILE], F32)
    XNB = A("XNB", [128, 8], F32)
    BUFA = A("BUFA", [128, 4096], F32)
    D = A("D", [128, NCC * TILE], F32)
    MIX0H = A("MIX0H", [128, NCC * TILE], BF16)
    MIX0L = A("MIX0L", [128, NCC * TILE], BF16)
    MIX1H = A("MIX1H", [128, NCC * TILE], BF16)
    MIX1L = A("MIX1L", [128, NCC * TILE], BF16)
    SETA = A("SETA", [128, 1024], F32)
    MIXT = SETA[:, 0:1024]
    KFM = A("KFM", [128, NCC * TILE], F32)
    VTM = A("VTM", [128, TPT * C], F32)
    RFM = A("RFM", [128, NCC * TILE], F32)
    ACCT = A("ACCT", [128, TPT * C], F32)
    KTIL = A("KTIL", [128, C], F32)
    SSCR = A("SSCR", [128, 512], F32)
    RS = A("RS", [128, C], F32)
    APR = A("APR", [128, H * 128], F32)
    WF32 = APR[:, 0:2048]
    _KTILB = KTIL.bitcast(BF16)
    _RSB = RS.bitcast(BF16)
    def _wsphl(k):
        buf = _KTILB if (k % 4) < 2 else _RSB
        base = (k % 2) * 1024
        return buf[:, base:base + 1024]
    S_ALL = A("S_ALL", [128, 512], F32)
    S_ALL2 = A("S_ALL2", [128, 512], F32)
    XIO = A("XIO", [128, 2 * C], F32)
    LNS = A("LNS", [128, 32], F32)
    GNS = A("GNS", [128, 320], F32)
    IDENT = A("IDENT", [128, 128], F32)
    GRIDT = SETA[:, 0:128]
    TRIT = SETA[:, 128:256]
    DIAGT = SETA[:, 256:384]
    RAMPI = SETA[:, 384:512]
    RAMPJ = SETA[:, 512:576]
    MWT = A("MWT", [128, H * 128], F32)
    MLAM = A("MLAM", [128, NCC * 128], F32)
    MK = A("MK", [128, C], F32)
    WSBC = A("WSBC", [128, 512], F32)
    MGAM = A("MGAM", [128, C], F32)
    MBET = A("MBET", [128, C], F32)
    DECB = A("DECB", [128, 16], F32)
    FAAB = A("FAAB", [128, 16], F32)
    LNW = A("LNW", [128, 16], F32)
    WS128 = A("WS128", [128, 16], F32)
    ONES64 = A("ONES64", [128, 64], F32)
    NEGHALF = A("NEGHALF", [128, 1], F32)
    INVC = A("INVC", [128, 1], F32)
    NEGINVC = A("NEGINVC", [128, 1], F32)
    NEGEPSC = A("NEGEPSC", [128, 1], F32)
    INVN = A("INVN", [128, 1], F32)
    NEGEPSN = A("NEGEPSN", [128, 1], F32)
    VEC = {m: A("t_" + m, [128, 8], F32) for m in
           ("maak", "maav", "maar", "maag", "mur", "muk")}
    OMR = A("OMR", [128, 8], F32)
    OMK = A("OMK", [128, 8], F32)
    MSCR = SETA[:, 576:704]
    WRINGB = A("WRINGB", [128, RING * 2 * C], BF16)

    PS = nc.alloc_psum_tensor("PS", [128, 8 * 512], F32)
    banks = Banks(PS)

    st = ExitStack()
    with st:
        s_t = Sem(st, nc, "s_t")
        s_v = Sem(st, nc, "s_v")
        s_s = Sem(st, nc, "s_s")
        s_cst = Sem(st, nc, "s_cst")
        s_x = [Sem(st, nc, f"s_x{i}") for i in range(2)]
        s_xr = [Sem(st, nc, f"s_xr{i}") for i in range(2)]
        s_w = [Sem(st, nc, f"s_w{i}") for i in range(RING)]
        s_gst = Sem(st, nc, "s_gst")
        s_wf = Sem(st, nc, "s_wf")
        s_wsp = Sem(st, nc, "s_wsp")

        S = {
            "xio_free": [None, None],
            "xio_loaded": [None, None],
            "x_cnt": [0, 0],
            "xr_cnt": [0, 0],
            "mix_free": {0: None, 1: None},
            "mix_ready": {0: None, 1: None},
            "mixt_free": None,
            "d_ready": None,
            "xnt_ready": None,
            "xnt_free": None,
            "bufa_free": None,
            "kfm_free": None,
            "vtm_free": None,
            "rfm_free": None,
            "acct_free": None,
            "supd": None,
            "yread": None,
            "setup_done": None,
        }

        def vbar(v, ins):
            ev = _inc(ins, s_v)
            _wt(v, ev)
            return ev

        _scnt = [0]

        def state_buf():
            return (S_ALL, S_ALL2)[_scnt[0] % 2]

        def state_dst():
            return (S_ALL, S_ALL2)[(_scnt[0] + 1) % 2]

        wring = {"n": 0, "scount": [0] * RING, "freed": [], "loadevt": []}

        def wring_slot_ap(i):
            s = i % RING
            return WRINGB[:, s * 2 * C:(s + 1) * 2 * C]

        def wring_load(sync, dram_ap, gate=None):
            i = wring["n"]
            s = i % RING
            if i >= RING:
                _wt(sync, wring["freed"][i - RING])
            _wt(sync, gate)
            ins = sync.dma_start(out=wring_slot_ap(i), in_=dram_ap)
            wring["scount"][s] += 1
            ins.then_inc(s_w[s].h, 16)
            s_w[s].v += 16
            wring["loadevt"].append((s_w[s], 16 * wring["scount"][s]))
            wring["freed"].append(None)
            wring["n"] += 1
            return i

        with nc.Block() as block:
            @block.sync
            def _(sync):
                for ap_, dst in ((c_ident.ap(), IDENT), (c_grid.ap(), GRIDT),
                                 (c_tri.ap(), TRIT), (c_diag.ap(), DIAGT),
                                 (c_rampi.ap(), RAMPI), (c_rampj.ap(), RAMPJ)):
                    _inc(sync.dma_start(out=dst[:, :], in_=ap_[:, :]), s_cst, 16)
                for m in VEC:
                    _inc(sync.dma_start(out=VEC[m][:, :], in_=vec_d[m][:, :]),
                         s_cst, 16)
                _inc(sync.dma_start(out=DECB[:, :],
                                    in_=dec_d.broadcast_to([128, 16])), s_cst, 16)
                _inc(sync.dma_start(out=FAAB[:, :],
                                    in_=faa_d.broadcast_to([128, 16])), s_cst, 16)
                _inc(sync.dma_start(out=MGAM[:, :],
                                    in_=lnw_d.broadcast_to([128, C])), s_cst, 16)
                _inc(sync.dma_start(out=MBET[:, :],
                                    in_=lnb_d.broadcast_to([128, C])), s_cst, 16)

            n_cst = s_cst.v

            @block.vector
            def _(v):
                v.wait_ge(s_cst.h, n_cst)
                v.memset(XNB[:, :], 0.0)
                v.memset(S_ALL[:, :], 0.0)
                v.memset(ONES64[:, :], 1.0)
                v.memset(NEGHALF[:, :], -0.5)
                v.memset(INVC[:, :], 1.0 / C)
                v.memset(NEGINVC[:, :], -1.0 / C)
                v.memset(NEGEPSC[:, :], -EPS_LN)
                v.memset(INVN[:, :], 1.0 / NDIM)
                v.memset(NEGEPSN[:, :], -EPS_GN)
                v.tensor_scalar(OMR[:, :], VEC["mur"][:, :], -1.0, 1.0,
                                OP.mult, OP.add)
                ev = _inc(v.tensor_scalar(OMK[:, :], VEC["muk"][:, :], -1.0, 1.0,
                                          OP.mult, OP.add), s_v)
                S["vsetup"] = ev

            @block.scalar
            def _(s):
                s.wait_ge(s_cst.h, n_cst)
                _wt(s, S["vsetup"])
                ev = _inc(s.activation(LNW[:, :], DECB[:, :], ACT.Exp), s_s)
                S["expd"] = ev

            @block.vector
            def _(v):
                _wt(v, S["expd"])
                ev = _inc(v.tensor_scalar_mul(WS128[:, :], LNW[:, :], -128.0),
                          s_v)
                S["ex128"] = ev

            @block.scalar
            def _(s):
                _wt(s, S["ex128"])
                s.activation(WS128[:, :], WS128[:, :], ACT.Exp)
                ev = None
                for h in range(H):
                    hcol = (h % 2) * 1024 + (h // 2) * 128
                    s.activation(MWT[:, hcol:hcol + 128], GRIDT[:, :],
                                 ACT.Exp, scale=LNW[:, h:h + 1])
                    s.activation(MK[:, h * 64:(h + 1) * 64], RAMPJ[:, :],
                                 ACT.Exp, scale=LNW[:, h:h + 1])
                    bp = (h % 2) * 64
                    s.activation(WSBC[bp:bp + 64,
                                      (h // 2) * 64:(h // 2) * 64 + 64],
                                 ONES64[0:64, :], ACT.Identity,
                                 scale=WS128[bp:bp + 64, h:h + 1])
                for cc in range(NCC):
                    s.activation(MLAM[0:64, cc * 128:(cc + 1) * 128],
                                 RAMPI[0:64, :], ACT.Exp,
                                 scale=LNW[0:64, 2 * cc:2 * cc + 1])
                    ev = s.activation(MLAM[64:128, cc * 128:(cc + 1) * 128],
                                      RAMPI[64:128, :], ACT.Exp,
                                      scale=LNW[64:128, 2 * cc + 1:2 * cc + 2])
                S["ssetup"] = _inc(ev, s_s)

            @block.vector
            def _(v):
                _wt(v, S["ssetup"])
                ev = None
                for h in range(H):
                    hcol = (h % 2) * 1024 + (h // 2) * 128
                    blk = MWT[:, hcol:hcol + 128]
                    v.tensor_scalar_mul(MSCR[:, :], DIAGT[:, :],
                                        FAAB[:, h:h + 1])
                    vbar(v, v.tensor_tensor(blk, blk, TRIT[:, :], OP.mult))
                    ev = v.tensor_tensor(blk, blk, MSCR[:, :], OP.add)
                S["setup_done"] = _inc(ev, s_v)

            SPLIT_ORDER = ("wk", "wv", "wr", "wg", "wo", "wfk", "wfr", "wfv")
            split_done = {w: [None] * NCC for w in SPLIT_ORDER}
            sp = {"next": 0, "ld": {}, "tt": {}, "st": {}}
            sp_list = [(w, cc, h) for w in SPLIT_ORDER for cc in range(NCC)
                       for h in range(2)]

            def sp_slot(k, buf, width):
                p = (k % 4) * width
                return buf[:, p:p + width]

            def emit_split_pair():
                k0 = sp["next"]
                if k0 >= len(sp_list):
                    return
                ks = [k for k in (k0, k0 + 1) if k < len(sp_list)]
                sp["next"] = k0 + len(ks)

                @block.sync
                def _(sync):
                    for k in ks:
                        w, cc, h = sp_list[k]
                        _wt(sync, sp["tt"].get(k - 4))
                        ins = sync.dma_start(
                            out=sp_slot(k, WF32, 512),
                            in_=wT[w][cc * 128:(cc + 1) * 128,
                                      h * 512:(h + 1) * 512])
                        sp["ld"][k] = _inc(ins, s_wf, 16)

                @block.vector
                def _(v):
                    cpev = {}
                    for k in ks:
                        _wt(v, sp["ld"][k])
                        _wt(v, sp["st"].get(k - 4))
                        hl = _wsphl(k)
                        cpev[k] = _inc(v.tensor_copy(hl[:, 0:512],
                                                     sp_slot(k, WF32, 512)),
                                       s_v)
                    for k in ks:
                        _wt(v, cpev[k])
                        hl = _wsphl(k)
                        ins = v.tensor_tensor(
                            hl[:, 512:1024],
                            sp_slot(k, WF32, 512),
                            hl[:, 0:512], OP.subtract)
                        sp["tt"][k] = _inc(ins, s_v)

                @block.scalar
                def _(s):
                    for k in ks:
                        w, cc, h = sp_list[k]
                        _wt(s, sp["tt"][k])
                        rows = slice(cc * 128, (cc + 1) * 128)
                        ins = s.dma_start(
                            out=wsp_d[w][rows, h * 1024:(h + 1) * 1024],
                            in_=_wsphl(k))
                        ev = _inc(ins, s_wsp, 16)
                        sp["st"][k] = ev
                        if h == 1:
                            split_done[w][cc] = ev

            def emit_one_split():
                emit_split_pair()

            def pump_splits(n=2):
                for _ in range(n):
                    emit_one_split()

            def ensure_splits(w):
                while split_done[w][NCC - 1] is None:
                    emit_one_split()

            def emit_ln_stats(kind, ti):
                row0 = (0 if kind == "warm" else te) + ti * TILE
                sqev = {}
                for tch in range(TPT):
                    pump_splits(1)
                    sl = tch % 2

                    @block.sync
                    def _(sync, tch=tch, sl=sl, row0=row0):
                        _wt(sync, S["xio_free"][sl])
                        ins = sync.dma_start(
                            out=XIO[:, sl * C:(sl + 1) * C],
                            in_=x_slab[row0 + tch * 128: row0 + (tch + 1) * 128, :])
                        ins.then_inc(s_x[sl].h, 16)
                        s_x[sl].v += 16
                        S["x_cnt"][sl] += 1
                        S["xio_loaded"][sl] = (s_x[sl], 16 * S["x_cnt"][sl])

                    ld_evt = S["xio_loaded"][sl]

                    @block.vector
                    def _(v, tch=tch, sl=sl, ld_evt=ld_evt, sqev=sqev):
                        _wt(v, ld_evt)
                        if tch == 0:
                            _wt(v, S["setup_done"])
                            _wt(v, S["bufa_free"])
                        xio = XIO[:, sl * C:(sl + 1) * C]
                        v.tensor_copy(BUFA[:, tch * C:(tch + 1) * C], xio)
                        v.tensor_reduce(LNS[:, tch:tch + 1], xio, AX.X, OP.add)
                        ev = _inc(v.tensor_tensor(
                            D[:, tch * C:(tch + 1) * C], xio, xio, OP.mult),
                            s_v)
                        sqev[tch] = ev
                        S["xio_free"][sl] = ev
                        if tch > 0:
                            _wt(v, sqev[tch - 1])
                            _inc(v.tensor_reduce(
                                LNS[:, 4 + tch - 1:4 + tch],
                                D[:, (tch - 1) * C:tch * C],
                                AX.X, OP.add), s_v)

                @block.vector
                def _(v, sqev=sqev):
                    _wt(v, sqev[TPT - 1])
                    _inc(v.tensor_reduce(
                        LNS[:, 4 + TPT - 1:4 + TPT],
                        D[:, (TPT - 1) * C:TPT * C],
                        AX.X, OP.add), s_v)

                red_evt = (s_v, s_v.v)

                @block.scalar
                def _(s, red_evt=red_evt):
                    _wt(s, red_evt)
                    s.activation(LNS[:, 8:12], LNS[:, 0:4], ACT.Identity,
                                 scale=INVC[:, :])
                    s.activation(LNS[:, 12:16], LNS[:, 0:4], ACT.Identity,
                                 scale=NEGINVC[:, :])
                    s.activation(LNS[:, 16:20], LNS[:, 8:12], ACT.Square)
                    _inc(s.activation(LNS[:, 16:20], LNS[:, 16:20],
                                      ACT.Identity, bias=NEGEPSC[:, :]), s_s)
                msq_evt = (s_s, s_s.v)

                @block.vector
                def _(v, msq_evt=msq_evt):
                    _wt(v, msq_evt)
                    _inc(v.scalar_tensor_tensor(
                        LNS[:, 20:24], LNS[:, 4:8], 1.0 / C,
                        LNS[:, 16:20], OP.mult, OP.subtract), s_v)
                var_evt = (s_v, s_v.v)

                @block.scalar
                def _(s, var_evt=var_evt):
                    _wt(s, var_evt)
                    s.activation(LNS[:, 24:28], LNS[:, 20:24], ACT.Ln)
                    _inc(s.activation(LNS[:, 28:32], LNS[:, 24:28], ACT.Exp,
                                      scale=NEGHALF[:, :]), s_s)
                    ev = None
                    for tch in range(TPT):
                        dst = BUFA[:, tch * C:(tch + 1) * C]
                        s.activation(dst, dst, ACT.Identity,
                                     bias=LNS[:, 12 + tch:13 + tch])
                        ev = s.activation(dst, dst, ACT.Identity,
                                          scale=LNS[:, 28 + tch:29 + tch])
                    _inc(ev, s_s)

                S["norm_evt"] = (s_s, s_s.v)

            def emit_ln_finish():
                norm_evt = S["norm_evt"]
                last_tr = None
                for cc in range(NCC):
                    bk = 6 + (cc % 2)

                    @block.tensor
                    def _(t, cc=cc, bk=bk, norm_evt=norm_evt):
                        _wt(t, norm_evt)
                        banks.acquire(t, bk)
                        for tch in range(TPT):
                            _inc(t.transpose(
                                banks.ap(bk, tch * 128, (tch + 1) * 128),
                                BUFA[:, tch * C + cc * 128:
                                     tch * C + (cc + 1) * 128],
                                IDENT[:, :]), s_t)
                    tr_evt = (s_t, s_t.v)
                    last_tr = tr_evt

                    @block.scalar
                    def _(s, cc=cc, bk=bk, tr_evt=tr_evt):
                        _wt(s, tr_evt)
                        _wt(s, S["xnt_free"])
                        ev = _inc(s.activation(XNT[:, cc * TILE:(cc + 1) * TILE],
                                               banks.ap(bk), ACT.Identity), s_s)
                        banks.release(bk, ev)

                S["xnt_ready"] = (s_s, s_s.v)
                S["xnt_free"] = None
                S["bufa_free"] = last_tr

                @block.vector
                def _(v):
                    _wt(v, S["xnt_ready"])
                    xv3 = XNT[:, :].rearrange("p (c t) -> p c t", t=TILE)
                    dv3 = D[:, :].rearrange("p (c t) -> p c t", t=TILE)
                    v.tensor_tensor(dv3[:, :, 0:1],
                                    XNB[:, :].rearrange("p (c o) -> p c o", o=1),
                                    xv3[:, :, 0:1], OP.subtract)
                    ev = _inc(v.tensor_tensor(dv3[:, :, 1:TILE],
                                              xv3[:, :, 0:TILE - 1],
                                              xv3[:, :, 1:TILE], OP.subtract),
                              s_v)
                    S["d_ready"] = ev

            def emit_mix(mslot, vtile):
                pump_splits(2)
                bh = MIX0H if mslot == 0 else MIX1H
                bl = MIX0L if mslot == 0 else MIX1L
                evs = {}

                def mslot_ap(cc):
                    p = (cc % 2) * 512
                    return MIXT[:, p:p + 512]

                @block.vector
                def _(v, vtile=vtile, mslot=mslot, bh=bh, bl=bl):
                    def stt(cc):
                        sl = slice(cc * TILE, (cc + 1) * TILE)
                        if cc == 0:
                            _wt(v, S["d_ready"])
                            _wt(v, S["mix_free"][mslot])
                            _wt(v, S["setup_done"])
                            _wt(v, S["mixt_free"])
                        _wt(v, evs.get(("lo", cc - 2)))
                        evs[("t", cc)] = _inc(v.scalar_tensor_tensor(
                            mslot_ap(cc), D[:, sl], vtile[:, cc:cc + 1],
                            XNT[:, sl], OP.mult, OP.add), s_v)

                    def cp(cc):
                        sl = slice(cc * TILE, (cc + 1) * TILE)
                        _wt(v, evs[("t", cc)])
                        evs[("hi", cc)] = _inc(
                            v.tensor_copy(bh[:, sl], mslot_ap(cc)), s_v)

                    def lo(cc):
                        sl = slice(cc * TILE, (cc + 1) * TILE)
                        _wt(v, evs[("hi", cc)])
                        evs[("lo", cc)] = _inc(
                            v.tensor_tensor(bl[:, sl], mslot_ap(cc),
                                            bh[:, sl], OP.subtract), s_v)

                    for k in range(0, NCC, 2):
                        stt(k)
                        stt(k + 1)
                        cp(k)
                        cp(k + 1)
                        lo(k)
                        lo(k + 1)
                    S["mixt_free"] = evs[("lo", NCC - 1)]
                    S["mix_ready"][mslot] = evs[("lo", NCC - 1)]

            def emit_proj_wstat(wname, mslot, dst, out_evts, dst_free_key,
                                evac_act=None):
                pump_splits(2)
                ensure_splits(wname)
                mh = MIX0H if mslot == 0 else MIX1H
                ml = MIX0L if mslot == 0 else MIX1L
                loads = [None] * NCC
                bank_evts = [None] * 8

                @block.sync
                def _(sync):
                    for cc in range(RING):
                        loads[cc] = wring_load(sync,
                                               wsp_d[wname][cc * 128:(cc + 1) * 128, :],
                                               gate=split_done[wname][cc])

                for cc in range(NCC):
                    @block.tensor
                    def _(t, cc=cc):
                        _wt(t, wring["loadevt"][loads[cc]])
                        if cc == 0:
                            _wt(t, S["mix_ready"][mslot])
                            for d in range(8):
                                banks.acquire(t, d)
                        slot = wring_slot_ap(loads[cc])
                        msl = slice(cc * TILE, (cc + 1) * TILE)
                        ins = None
                        for d in range(8):
                            base = (d // 4) * 1024 + (d % 4) * 128
                            w_h = slot[:, base:base + 128]
                            w_l = slot[:, base + 512:base + 640]
                            ins = t.matmul(banks.ap(d), w_h, mh[:, msl],
                                           start=(cc == 0), stop=False)
                            ins = t.matmul(banks.ap(d), w_h, ml[:, msl],
                                           start=False, stop=False)
                            ins = t.matmul(banks.ap(d), w_l, mh[:, msl],
                                           start=False, stop=(cc == NCC - 1))
                            if cc == NCC - 1:
                                bank_evts[d] = _inc(ins, s_t)
                        if cc < NCC - 1:
                            wring["freed"][loads[cc]] = _inc(ins, s_t)
                        else:
                            wring["freed"][loads[cc]] = bank_evts[7]
                            S["mix_free"][mslot] = bank_evts[7]

                    if cc + RING < NCC:
                        @block.sync
                        def _(sync, cc=cc):
                            loads[cc + RING] = wring_load(
                                sync,
                                wsp_d[wname][(cc + RING) * 128:(cc + RING + 1) * 128, :],
                                gate=split_done[wname][cc + RING])

                dst_free = S[dst_free_key]
                S[dst_free_key] = None
                for d in range(8):
                    if evac_act is not None or d % 2 == 0:
                        @block.scalar
                        def _(s, d=d):
                            _wt(s, bank_evts[d])
                            _wt(s, dst_free)
                            ev = _inc(s.activation(
                                dst[:, d * TILE:(d + 1) * TILE], banks.ap(d),
                                evac_act if evac_act is not None
                                else ACT.Identity), s_s)
                            banks.release(d, ev)
                            out_evts[d] = ev
                    else:
                        @block.vector
                        def _(v, d=d):
                            _wt(v, bank_evts[d])
                            _wt(v, dst_free)
                            ev = _inc(v.tensor_copy(
                                dst[:, d * TILE:(d + 1) * TILE], banks.ap(d)),
                                s_v)
                            banks.release(d, ev)
                            out_evts[d] = ev

            def emit_proj_xstat(wname, stats, stat_ready_fn, evac_fn,
                                mix_free_key=None):
                pump_splits(2)
                ensure_splits(wname)
                sh_buf, sl_buf = stats
                loads = [None] * NCC
                bank_evts = [None] * 8

                @block.sync
                def _(sync):
                    for cc in range(RING):
                        loads[cc] = wring_load(sync,
                                               wsp_d[wname][cc * 128:(cc + 1) * 128, :],
                                               gate=split_done[wname][cc])

                for cc in range(NCC):
                    @block.tensor
                    def _(t, cc=cc):
                        _wt(t, wring["loadevt"][loads[cc]])
                        if cc == 0:
                            _wt(t, stat_ready_fn())
                            for b in range(8):
                                banks.acquire(t, b)
                        slot = wring_slot_ap(loads[cc])
                        ins = None
                        for tch in range(TPT):
                            ssl = slice(cc * TILE + tch * 128,
                                        cc * TILE + (tch + 1) * 128)
                            for dh in range(2):
                                b = 2 * tch + dh
                                movh = slot[:, dh * 1024:dh * 1024 + 512]
                                movl = slot[:, dh * 1024 + 512:
                                            (dh + 1) * 1024]
                                ins = t.matmul(banks.ap(b), sh_buf[:, ssl],
                                               movh, start=(cc == 0),
                                               stop=False)
                                ins = t.matmul(banks.ap(b), sh_buf[:, ssl],
                                               movl, start=False, stop=False)
                                ins = t.matmul(banks.ap(b), sl_buf[:, ssl],
                                               movh, start=False,
                                               stop=(cc == NCC - 1))
                                if cc == NCC - 1:
                                    bank_evts[b] = _inc(ins, s_t)
                        if cc < NCC - 1:
                            wring["freed"][loads[cc]] = _inc(ins, s_t)
                        else:
                            wring["freed"][loads[cc]] = bank_evts[7]
                            if mix_free_key is not None:
                                S["mix_free"][mix_free_key] = bank_evts[7]

                    if cc + RING < NCC:
                        @block.sync
                        def _(sync, cc=cc):
                            loads[cc + RING] = wring_load(
                                sync,
                                wsp_d[wname][(cc + RING) * 128:(cc + RING + 1) * 128, :],
                                gate=split_done[wname][cc + RING])

                for tch in range(TPT):
                    for dh in range(2):
                        evac_fn(2 * tch + dh, tch, dh, bank_evts[2 * tch + dh])

            def evac_simple(dst, dst_free_key=None, mask=None, act=None,
                            evts_out=None):
                holder = {"df": S[dst_free_key] if dst_free_key else None}
                if dst_free_key:
                    S[dst_free_key] = None

                def f(b, tch, dh, evt):
                    sl = slice(tch * C + dh * 512, tch * C + (dh + 1) * 512)
                    if mask is not None:
                        @block.vector
                        def _(v, b=b, sl=sl, dh=dh, evt=evt):
                            _wt(v, evt)
                            _wt(v, holder["df"])
                            ev = _inc(v.tensor_tensor(
                                dst[:, sl], banks.ap(b),
                                mask[:, dh * 512:(dh + 1) * 512], OP.mult), s_v)
                            banks.release(b, ev)
                            if evts_out is not None:
                                evts_out[b] = ev
                    elif act is not None:
                        @block.scalar
                        def _(s, b=b, sl=sl, evt=evt):
                            _wt(s, evt)
                            _wt(s, holder["df"])
                            ev = _inc(s.activation(dst[:, sl], banks.ap(b), act),
                                      s_s)
                            banks.release(b, ev)
                            if evts_out is not None:
                                evts_out[b] = ev
                    else:
                        if b % 2 == 0:
                            @block.scalar
                            def _(s, b=b, sl=sl, evt=evt):
                                _wt(s, evt)
                                _wt(s, holder["df"])
                                ev = _inc(s.activation(dst[:, sl],
                                                       banks.ap(b),
                                                       ACT.Identity), s_s)
                                banks.release(b, ev)
                                if evts_out is not None:
                                    evts_out[b] = ev
                        else:
                            @block.vector
                            def _(v, b=b, sl=sl, evt=evt):
                                _wt(v, evt)
                                _wt(v, holder["df"])
                                ev = _inc(v.tensor_copy(dst[:, sl],
                                                        banks.ap(b)), s_v)
                                banks.release(b, ev)
                                if evts_out is not None:
                                    evts_out[b] = ev
                return f

            def rf_sl(b):
                if b < 2:
                    return KTIL[:, b * 512:(b + 1) * 512]
                if b < 4:
                    return RS[:, (b - 2) * 512:(b - 1) * 512]
                return APR[:, (b - 4) * 512:(b - 3) * 512]

            def emit_save_xnb():
                @block.vector
                def _(v):
                    xl = XNT[:, :].rearrange("p (c t) -> p c t", t=TILE)
                    ev = _inc(v.tensor_copy(
                        XNB[:, :].rearrange("p (c o) -> p c o", o=1),
                        xl[:, :, TILE - 1:TILE]), s_v)
                    S["xnt_free"] = ev

            emit_ln_stats("warm", 0)
            emit_ln_finish()
            for wt_i in range(nt):
                if not go():
                    break
                if not go():
                    break
                if wt_i == 0:
                    emit_mix(0, VEC["maak"])
                emit_mix(1, VEC["maav"])
                if wt_i + 1 < nt:
                    emit_ln_stats("warm", wt_i + 1)
                else:
                    emit_ln_stats("emit", 0)

                kevts = [None] * 8
                if not go():
                    break
                emit_proj_xstat("wk", (MIX0H, MIX0L), lambda: S["mix_ready"][0],
                                evac_simple(KFM, "kfm_free", mask=MK,
                                            evts_out=kevts), mix_free_key=0)
                vevts = [None] * 8
                if not go():
                    break
                emit_proj_xstat("wv", (MIX1H, MIX1L), lambda: S["mix_ready"][1],
                                evac_simple(VTM, "vtm_free", evts_out=vevts),
                                mix_free_key=1)
                pump_splits(12)
                emit_save_xnb()
                emit_ln_finish()
                emit_mix(0, VEC["maak"])
                if not go():
                    break

                for ch in range(TPT):
                    ube = (ch % 2) * 2
                    ubo = ube + 1

                    @block.tensor
                    def _(t, ch=ch, ube=ube, ubo=ubo):
                        _wt(t, kevts[2 * ch])
                        _wt(t, kevts[2 * ch + 1])
                        _wt(t, vevts[2 * ch])
                        _wt(t, vevts[2 * ch + 1])
                        banks.acquire(t, ube)
                        banks.acquire(t, ubo)
                        for par, ub in ((0, ube), (1, ubo)):
                            bp = par * 64
                            ins = None
                            for hh in range(8):
                                h = 2 * hh + par
                                ins = t.matmul(
                                    PS[bp:bp + 64,
                                       ub * 512 + hh * 64:
                                       ub * 512 + hh * 64 + 64],
                                    KFM[:, ch * C + h * 64: ch * C + (h + 1) * 64],
                                    VTM[:, ch * C + h * 64: ch * C + (h + 1) * 64],
                                    start=True, stop=True,
                                    tile_position=(0, bp))
                            S[f"u_mm{par}"] = _inc(ins, s_t)

                    @block.vector
                    def _(v, ube=ube, ubo=ubo):
                        _wt(v, S["u_mm0"])
                        _wt(v, S["u_mm1"])
                        sa, sd = state_buf(), state_dst()
                        _scnt[0] += 1
                        vbar(v, v.tensor_tensor(SSCR[:, :], sa[:, :],
                                                WSBC[:, :], OP.mult))
                        v.tensor_tensor(sd[0:64, :], SSCR[0:64, :],
                                        banks.ap(ube)[0:64, :], OP.add)
                        ev = _inc(v.tensor_tensor(sd[64:128, :],
                                                  SSCR[64:128, :],
                                                  banks.ap(ubo)[64:128, :],
                                                  OP.add), s_v)
                        S["supd"] = ev
                        banks.release(ube, ev)
                        banks.release(ubo, ev)

                S["kfm_free"] = S["u_mm1"]
                S["vtm_free"] = S["u_mm1"]
                pump_splits(12)

            ensure_splits("wfv")
            S["split_fin"] = (s_wsp, s_wsp.v)

            for ti in range(nt):
                if not go():
                    break
                if not go():
                    break
                emit_mix(1, VEC["maav"])

                kevts = [None] * 8
                if not go():
                    break
                emit_proj_wstat("wk", 0, KFM, kevts, "kfm_free")
                vevts = [None] * 8
                if not go():
                    break
                emit_proj_xstat("wv", (MIX1H, MIX1L), lambda: S["mix_ready"][1],
                                evac_simple(VTM, "vtm_free", evts_out=vevts),
                                mix_free_key=1)
                emit_mix(0, VEC["maar"])
                revts = [None] * 8
                if not go():
                    break
                emit_proj_wstat("wr", 0, RFM, revts, "rfm_free")
                if not go():
                    break

                yn_evts = [None] * TPT

                def emit_scan_prep(ch):
                    @block.tensor
                    def _(t, ch=ch):
                        for e in kevts:
                            _wt(t, e)
                        banks.acquire(t, 6)
                        banks.acquire(t, 7)
                        ins = None
                        for cc in range(NCC):
                            bk = 6 + cc // 4
                            ins = t.transpose(
                                banks.ap(bk, (cc % 4) * 128, (cc % 4 + 1) * 128),
                                KFM[:, cc * TILE + ch * 128:
                                    cc * TILE + (ch + 1) * 128],
                                IDENT[:, :])
                            _inc(ins, s_t)
                        S["ktr"] = (s_t, s_t.v)

                    @block.vector
                    def _(v, ch=ch, ti=ti):
                        _wt(v, S["ktr"])
                        if ti == 0 and ch == 0:
                            _wt(v, S["split_fin"])
                        v.tensor_tensor(KTIL[:, 0:512], banks.ap(6),
                                        MK[:, 0:512], OP.mult)
                        ev = _inc(v.tensor_tensor(KTIL[:, 512:1024],
                                                  banks.ap(7), MK[:, 512:1024],
                                                  OP.mult), s_v)
                        banks.release(6, ev)
                        banks.release(7, ev)
                        rsv = RS[:, :].rearrange("p (c t) -> p c t", t=128)
                        rfm3 = RFM[:, :].rearrange("p (c t) -> p c t", t=TILE)
                        for e in revts:
                            _wt(v, e)
                        ev = _inc(v.tensor_tensor(
                            rsv, rfm3[:, :, ch * 128:(ch + 1) * 128],
                            MLAM[:, :].rearrange("p (c t) -> p c t", t=128),
                            OP.mult), s_v)
                        S["rs"] = ev

                    @block.tensor
                    def _(t, ch=ch):
                        for e in revts:
                            _wt(t, e)
                        for b in range(4):
                            banks.acquire(t, b)
                        ins = None
                        for h in range(H):
                            cc = h // 2
                            bp = (h % 2) * 64
                            hcol = (h % 2) * 1024 + (h // 2) * 128
                            tok = slice(cc * TILE + ch * 128,
                                        cc * TILE + (ch + 1) * 128)
                            ins = t.matmul(
                                banks.span(0, 4)[:, hcol:hcol + 128],
                                KFM[bp:bp + 64, tok],
                                RFM[bp:bp + 64, tok],
                                start=True, stop=True,
                                tile_position=(bp, 0))
                        S["amm"] = _inc(ins, s_t)

                    @block.vector
                    def _(v, ch=ch, ti=ti):
                        _wt(v, S["amm"])
                        if ti == 0 and ch == 0:
                            _wt(v, S["split_fin"])
                        v.tensor_tensor(APR[:, 0:1024], banks.span(0, 2),
                                        MWT[:, 0:1024], OP.mult)
                        ev = _inc(v.tensor_tensor(APR[:, 1024:2048],
                                                  banks.span(2, 2),
                                                  MWT[:, 1024:2048], OP.mult),
                                  s_v)
                        for b in range(4):
                            banks.release(b, ev)
                        S["apr"] = ev

                emit_scan_prep(0)
                for ch in range(TPT):
                    if not go():
                        break

                    @block.tensor
                    def _(t, ch=ch):
                        _wt(t, S["apr"])
                        _wt(t, S["supd"])
                        _wt(t, S["rs"])
                        banks.acquire(t, 4)
                        banks.acquire(t, 5)
                        sa = state_buf()
                        ins = None
                        for h in range(H):
                            bp = (h % 2) * 64
                            cc = h // 2
                            hcol = (h % 2) * 1024 + (h // 2) * 128
                            ysl = banks.span(4, 2)[:, h * 64:(h + 1) * 64]
                            ins = t.matmul(ysl, APR[:, hcol:hcol + 128],
                                           VTM[:, ch * C + h * 64:
                                               ch * C + (h + 1) * 64],
                                           start=True, stop=False)
                            ins = t.matmul(
                                ysl,
                                RS[bp:bp + 64, cc * 128:(cc + 1) * 128],
                                sa[bp:bp + 64,
                                   (h // 2) * 64:(h // 2) * 64 + 64],
                                start=False, stop=True,
                                tile_position=(bp, 0))
                        S["yread"] = _inc(ins, s_t)

                    @block.tensor
                    def _(t, ch=ch):
                        banks.acquire(t, 2)
                        banks.acquire(t, 3)
                        for par, ub in ((0, 2), (1, 3)):
                            bp = par * 64
                            ins = None
                            for hh in range(8):
                                h = 2 * hh + par
                                ins = t.matmul(
                                    PS[bp:bp + 64,
                                       ub * 512 + hh * 64:
                                       ub * 512 + hh * 64 + 64],
                                    KTIL[:, h * 64:(h + 1) * 64],
                                    VTM[:, ch * C + h * 64: ch * C + (h + 1) * 64],
                                    start=True, stop=True,
                                    tile_position=(0, bp))
                            S[f"u_mm{par}"] = _inc(ins, s_t)

                    @block.vector
                    def _(v, ch=ch):
                        _wt(v, S["u_mm0"])
                        _wt(v, S["u_mm1"])
                        sa, sd = state_buf(), state_dst()
                        _scnt[0] += 1
                        vbar(v, v.tensor_tensor(SSCR[:, :], sa[:, :],
                                                WSBC[:, :], OP.mult))
                        v.tensor_tensor(sd[0:64, :], SSCR[0:64, :],
                                        banks.ap(2)[0:64, :], OP.add)
                        ev = _inc(v.tensor_tensor(sd[64:128, :],
                                                  SSCR[64:128, :],
                                                  banks.ap(3)[64:128, :],
                                                  OP.add), s_v)
                        S["supd"] = ev
                        banks.release(2, ev)
                        banks.release(3, ev)

                    @block.vector
                    def _(v, ch=ch):
                        _wt(v, S["yread"])
                        ev = _inc(v.tensor_copy(BUFA[:, ch * C:(ch + 1) * C],
                                                banks.span(4, 2)), s_v)
                        banks.release(4, ev)
                        banks.release(5, ev)
                        yn_evts[ch] = ev

                    if ch + 1 < TPT:
                        emit_scan_prep(ch + 1)

                if not go():
                    break
                emit_mix(1, VEC["maag"])

                @block.vector
                def _(v):
                    for e in yn_evts:
                        _wt(v, e)
                    y3 = BUFA[:, :].rearrange("p (g n) -> p g n", n=NDIM)
                    _inc(v.tensor_reduce(GNS[:, 0:64], y3, AX.X, OP.add), s_v)
                mean_evt = (s_v, s_v.v)

                @block.scalar
                def _(s, mean_evt=mean_evt):
                    _wt(s, mean_evt)
                    _wt(s, S["acct_free"])
                    S["acct_free"] = None
                    _inc(s.activation(ACCT[:, :], BUFA[:, :], ACT.Square), s_s)
                sq_evt = (s_s, s_s.v)

                @block.vector
                def _(v, sq_evt=sq_evt):
                    _wt(v, sq_evt)
                    sq3 = ACCT[:, :].rearrange("p (g n) -> p g n", n=NDIM)
                    _inc(v.tensor_reduce(GNS[:, 64:128], sq3, AX.X, OP.add),
                         s_v)
                red2_evt = (s_v, s_v.v)

                @block.scalar
                def _(s, red2_evt=red2_evt):
                    _wt(s, red2_evt)
                    s.activation(GNS[:, 128:192], GNS[:, 0:64], ACT.Identity,
                                 scale=INVN[:, :])
                    s.activation(GNS[:, 0:64], GNS[:, 128:192], ACT.Square)
                    s.activation(GNS[:, 0:64], GNS[:, 0:64], ACT.Identity,
                                 bias=NEGEPSN[:, :])
                    _inc(s.activation(GNS[:, 64:128], GNS[:, 64:128],
                                      ACT.Identity), s_s)
                gst_evt = (s_s, s_s.v)

                @block.vector
                def _(v, gst_evt=gst_evt):
                    _wt(v, gst_evt)
                    _inc(v.scalar_tensor_tensor(
                        GNS[:, 192:256], GNS[:, 64:128], 1.0 / NDIM,
                        GNS[:, 0:64], OP.mult, OP.subtract), s_v)
                var_evt = (s_v, s_v.v)

                @block.scalar
                def _(s, var_evt=var_evt):
                    _wt(s, var_evt)
                    s.activation(GNS[:, 64:128], GNS[:, 192:256], ACT.Ln)
                    _inc(s.activation(GNS[:, 256:320], GNS[:, 64:128],
                                      ACT.Exp, scale=NEGHALF[:, :]), s_s)
                sd_evt = (s_s, s_s.v)

                @block.vector
                def _(v, sd_evt=sd_evt):
                    _wt(v, sd_evt)
                    mb = GNS[:, 128:192].rearrange(
                        "p (g o) -> p g o", o=1).broadcast_to([128, 64, NDIM])
                    rb = GNS[:, 256:320].rearrange(
                        "p (g o) -> p g o", o=1).broadcast_to([128, 64, NDIM])
                    y3 = BUFA[:, :].rearrange("p (g n) -> p g n", n=NDIM)
                    vbar(v, v.tensor_tensor(y3, y3, mb, OP.subtract))
                    vbar(v, v.tensor_tensor(y3, y3, rb, OP.mult))
                    yc = BUFA[:, :].rearrange("p (a c) -> p a c", c=C)
                    gg = MGAM[:, :].rearrange(
                        "p (a c) -> p a c", a=1).broadcast_to([128, TPT, C])
                    bb = MBET[:, :].rearrange(
                        "p (a c) -> p a c", a=1).broadcast_to([128, TPT, C])
                    vbar(v, v.tensor_tensor(yc, yc, gg, OP.mult))
                    ev = _inc(v.tensor_tensor(yc, yc, bb, OP.add), s_v)
                    S["yn_all"] = ev

                gevts = [None] * 8
                emit_proj_xstat("wg", (MIX1H, MIX1L),
                                lambda: S["mix_ready"][1],
                                evac_simple(KFM, "kfm_free", act=ACT.Gelu,
                                            evts_out=gevts), mix_free_key=1)

                if not go():
                    break
                @block.vector
                def _(v):
                    for e in gevts:
                        _wt(v, e)
                    _wt(v, S["yn_all"])
                    _wt(v, S["vtm_free"])
                    ev = _inc(v.tensor_tensor(VTM[:, :], BUFA[:, :], KFM[:, :],
                                              OP.mult), s_v)
                    S["z_tm"] = ev
                    S["bufa_free"] = ev
                    S["vtm_free"] = None

                zf_free = S["mix_free"][0]
                S["mix_free"][0] = None
                for p in range(0, NCC, 2):
                    tr_evt = {}
                    for cc in (p, p + 1):
                        bk = 6 + (cc % 2)

                        @block.tensor
                        def _(t, cc=cc, bk=bk):
                            _wt(t, S["z_tm"])
                            banks.acquire(t, bk)
                            for tch in range(TPT):
                                _inc(t.transpose(
                                    banks.ap(bk, tch * 128, (tch + 1) * 128),
                                    VTM[:, tch * C + cc * 128:
                                        tch * C + (cc + 1) * 128],
                                    IDENT[:, :]), s_t)
                        tr_evt[cc] = (s_t, s_t.v)

                    @block.vector
                    def _(v, p=p, tr_evt=tr_evt, zf_free=zf_free):
                        cpev = {}
                        for cc in (p, p + 1):
                            _wt(v, tr_evt[cc])
                            if cc == 0:
                                _wt(v, zf_free)
                            cpev[cc] = _inc(v.tensor_copy(
                                MIX0H[:, cc * TILE:(cc + 1) * TILE],
                                banks.ap(6 + (cc % 2))), s_v)
                        for cc in (p, p + 1):
                            _wt(v, cpev[cc])
                            ev = _inc(v.tensor_tensor(
                                MIX0L[:, cc * TILE:(cc + 1) * TILE],
                                banks.ap(6 + (cc % 2)),
                                MIX0H[:, cc * TILE:(cc + 1) * TILE],
                                OP.subtract), s_v)
                            banks.release(6 + (cc % 2), ev)
                            if cc == NCC - 1:
                                S["zf"] = ev

                if not go():
                    break
                emit_mix(1, OMR)
                xres_ld = [None] * TPT

                def emit_xres_load(tch, ti=ti):
                    sl = tch % 2

                    @block.gpsimd
                    def _(g, tch=tch, sl=sl, ti=ti):
                        _wt(g, S["xio_free"][sl])
                        row0 = te + ti * TILE + tch * 128
                        ins = g.dma_start(out=XIO[:, sl * C:(sl + 1) * C],
                                          in_=x_slab[row0:row0 + 128, :])
                        ins.then_inc(s_xr[sl].h, 16)
                        s_xr[sl].v += 16
                        S["xr_cnt"][sl] += 1
                        xres_ld[tch] = (s_xr[sl], 16 * S["xr_cnt"][sl])

                emit_xres_load(0)
                emit_xres_load(1)

                oevts = [None] * 8

                def o_evac(b, tch, dh, evt):
                    sl = tch % 2

                    @block.vector
                    def _(v, b=b, tch=tch, dh=dh, evt=evt, sl=sl):
                        _wt(v, evt)
                        _wt(v, xres_ld[tch])
                        _wt(v, S["acct_free"])
                        S["acct_free"] = None
                        ev = _inc(v.tensor_tensor(
                            ACCT[:, tch * C + dh * 512:
                                 tch * C + (dh + 1) * 512],
                            banks.ap(b),
                            XIO[:, sl * C + dh * 512: sl * C + (dh + 1) * 512],
                            OP.add), s_v)
                        banks.release(b, ev)
                        oevts[b] = ev
                        if dh == 1:
                            S["xio_free"][sl] = ev
                    if dh == 1 and tch + 2 < TPT:
                        emit_xres_load(tch + 2)

                emit_proj_xstat("wo", (MIX0H, MIX0L), lambda: S["zf"], o_evac,
                                mix_free_key=0)

                if not go():
                    break
                emit_mix(0, OMK)
                emit_save_xnb()
                if ti + 1 < nt:
                    emit_ln_stats("emit", ti + 1)

                frevts = [None] * 8

                def fr_evac(b, tch, dh, evt):
                    @block.scalar
                    def _(s, b=b, evt=evt):
                        _wt(s, evt)
                        ev = _inc(s.activation(rf_sl(b), banks.ap(b),
                                               ACT.Sigmoid), s_s)
                        banks.release(b, ev)
                        frevts[b] = ev

                emit_proj_xstat("wfr", (MIX1H, MIX1L),
                                lambda: S["mix_ready"][1], fr_evac,
                                mix_free_key=1)

                if not go():
                    break
                fkevts = [None] * 8
                emit_proj_wstat("wfk", 0, KFM, fkevts, "kfm_free",
                                evac_act=ACT.Relu)
                if ti + 1 < nt:
                    emit_ln_finish()

                if not go():
                    break

                @block.vector
                def _(v):
                    for e in fkevts:
                        _wt(v, e)
                    ev = _inc(v.tensor_tensor(KFM[:, :], KFM[:, :], KFM[:, :],
                                              OP.mult), s_v)
                    _wt(v, ev)
                    _wt(v, S["mix_free"][1])
                    vbar(v, v.tensor_copy(MIX1H[:, :], KFM[:, :]))
                    ev = vbar(v, v.tensor_tensor(MIX1L[:, :], KFM[:, :],
                                                 MIX1H[:, :], OP.subtract))
                    S["kact"] = ev
                    S["kfm_free"] = ev

                if ti + 1 < nt:
                    emit_mix(0, VEC["maak"])

                out_evts = [None] * 8

                def fv_evac(b, tch, dh, evt):
                    @block.vector
                    def _(v, b=b, tch=tch, dh=dh, evt=evt):
                        _wt(v, evt)
                        _wt(v, frevts[b])
                        _wt(v, oevts[b])
                        sl = slice(tch * C + dh * 512, tch * C + (dh + 1) * 512)
                        vbar(v, v.tensor_tensor(VTM[:, sl], banks.ap(b),
                                                rf_sl(b), OP.mult))
                        ev = _inc(v.tensor_tensor(VTM[:, sl], VTM[:, sl],
                                                  ACCT[:, sl], OP.add), s_v)
                        banks.release(b, ev)
                        out_evts[b] = ev

                emit_proj_xstat("wfv", (MIX1H, MIX1L), lambda: S["kact"],
                                fv_evac, mix_free_key=1)

                for tch in range(TPT):
                    @block.gpsimd
                    def _(g, tch=tch, ti=ti):
                        _wt(g, out_evts[2 * tch])
                        _wt(g, out_evts[2 * tch + 1])
                        row0 = ti * TILE + tch * 128
                        ins = g.dma_start(out=out_d[row0:row0 + 128, :],
                                          in_=VTM[:, tch * C:(tch + 1) * C])
                        ins.then_inc(s_gst.h, 16)
                        s_gst.v += 16

                S["acct_free"] = out_evts[7]
                S["rfm_free"] = S["rs"]
                S["vtm_free"] = (s_gst, s_gst.v)

            _BUFS = {"XNT": XNT, "BUFA": BUFA, "D": D,
                     "KFM": KFM, "VTM": VTM, "RFM": RFM,
                     "ACCT": ACCT, "KTIL": KTIL, "RS": RS, "APR": APR,
                     "S_ALL": S_ALL, "MWT": MWT, "MLAM": MLAM, "MK": MK,
                     "WSBC": WSBC, "MGAM": MGAM, "XNB": XNB, "LNW": LNW, "LNS": LNS, "XIO": XIO,
                     "WS128": WS128}
            _BF16S = {"MIX0S": (MIX0H, MIX0L), "MIX1S": (MIX1H, MIX1L),
                      "MIX0HH": (MIX0H, None), "MIX0LL": (MIX0L, None),
                      "MIX1HH": (MIX1H, None), "MIX1LL": (MIX1L, None)}
            if DBG_TAP in _BF16S:
                bh_, bl_ = _BF16S[DBG_TAP]

                @block.scalar
                def _(s):
                    _wt(s, (s_v, s_v.v))
                    _wt(s, (s_t, s_t.v))
                    _inc(s.activation(ACCT[:, :], bh_[:, :], ACT.Identity), s_s)

                if bl_ is not None:
                    @block.vector
                    def _(v):
                        _wt(v, (s_s, s_s.v))
                        _inc(v.tensor_tensor(ACCT[:, :], ACCT[:, :], bl_[:, :],
                                             OP.add), s_v)
                _BUFS[DBG_TAP] = ACCT
            if DBG_TAP is not None or s_gst.v == 0:
                @block.gpsimd
                def _(g):
                    _wt(g, (s_v, s_v.v))
                    _wt(g, (s_s, s_s.v))
                    _wt(g, (s_t, s_t.v))
                    if DBG_TAP is None:
                        ins = g.dma_start(out=out_d[0:128, 0:8], in_=XNB[:, :])
                        ins.then_inc(s_gst.h, 16)
                        s_gst.v += 16
                    else:
                        buf = _BUFS[DBG_TAP]
                        ncol = buf.shape[1]
                        nrow = min((ncol + C - 1) // C, te // 128)
                        for r in range(nrow):
                            w = min(C, ncol - r * C)
                            ins = g.dma_start(
                                out=out_d[r * 128:(r + 1) * 128, 0:w],
                                in_=buf[:, r * C:r * C + w])
                            ins.then_inc(s_gst.h, 16)
                            s_gst.v += 16

            total_st = s_gst.v

            @block.gpsimd
            def _(g):
                g.wait_ge(s_gst.h, total_st)

    return nc


def kernel(**inputs):
    ins = {k: np.asarray(v, dtype=np.float32) for k, v in inputs.items()}
    x = ins["L0_attn_x"]
    B, T, _ = x.shape

    def vec_tile(v):
        return np.ascontiguousarray(v.reshape(8, 128).T)

    common = {
        "wk": np.ascontiguousarray(ins["L0_attn_Wk"].T),
        "wv": np.ascontiguousarray(ins["L0_attn_Wv"].T),
        "wr": np.ascontiguousarray(ins["L0_attn_Wr"].T),
        "wg": np.ascontiguousarray(ins["L0_attn_Wg"].T),
        "wo": np.ascontiguousarray(ins["L0_attn_Wo"].T),
        "wfr": np.ascontiguousarray(ins["L0_ffn_W_r"].T),
        "wfk": np.ascontiguousarray(ins["L0_ffn_W_k"].T),
        "wfv": np.ascontiguousarray(ins["L0_ffn_W_v"].T),
        "maak": vec_tile(ins["L0_attn_time_maa_k"].reshape(-1)),
        "maav": vec_tile(ins["L0_attn_time_maa_v"].reshape(-1)),
        "maar": vec_tile(ins["L0_attn_time_maa_r"].reshape(-1)),
        "maag": vec_tile(ins["L0_attn_time_maa_g"].reshape(-1)),
        "mur": vec_tile(ins["L0_ffn_mu_r"].reshape(-1)),
        "muk": vec_tile(ins["L0_ffn_mu_k"].reshape(-1)),
        "dec_row": np.ascontiguousarray(ins["L0_attn_time_decay"].reshape(1, 16)),
        "faa_row": np.ascontiguousarray(ins["L0_attn_time_faaaa"].reshape(1, 16)),
        "lnw_row": np.ascontiguousarray(ins["L0_attn_ln_w"].reshape(1, C)),
        "lnb_row": np.ascontiguousarray(ins["L0_attn_ln_b"].reshape(1, C)),
    }

    in_maps = []
    for core in range(8):
        b, half = core // 2, core % 2
        if half == 0:
            slab = np.concatenate(
                [np.zeros((TE, C), np.float32), x[b, 0:TE]], axis=0)
        else:
            slab = np.ascontiguousarray(x[b, 0:2 * TE])
        m = dict(common)
        m["x_slab"] = slab
        in_maps.append(m)

    nc = build_nc(NT)
    res = run_bass_kernel_spmd(nc, in_maps, core_ids=list(range(8)))

    out = np.empty((B, T, C), np.float32)
    for core in range(8):
        b, half = core // 2, core % 2
        out[b, half * TE:(half + 1) * TE] = res.results[core]["out"]
    return out


# version 19
# speedup vs baseline: 1.0071x; 1.0071x over previous
import numpy as np
from contextlib import ExitStack

import concourse.bass as bass
import concourse.mybir as mybir
from concourse.bass_utils import run_bass_kernel_spmd

F32 = mybir.dt.float32
BF16 = mybir.dt.bfloat16
OP = mybir.AluOpType
ACT = mybir.ActivationFunctionType
AX = mybir.AxisListType

C = 1024
H = 16
NDIM = 64
TILE = 512
NCC = 8
TPT = 4
NT = 4
TE = 2048
EPS_LN = 1e-5
EPS_GN = 1e-5 * 64
RING = 2
STOP = None
DBG_Y = "both"
DBG_TAP = None


class Sem:
    def __init__(self, st, nc, name):
        self.h = st.enter_context(nc.semaphore(name))
        self.v = 0

    def tick(self, n=1):
        self.v += n
        return (self, self.v)


def _inc(instr, sem, n=1):
    instr.then_inc(sem.h, n)
    return sem.tick(n)


def _wt(eng, evt):
    if evt is not None:
        eng.wait_ge(evt[0].h, evt[1])


class Banks:
    def __init__(self, ps):
        self.ps = ps
        self.rel = [None] * 8

    def ap(self, b, lo=0, hi=512):
        return self.ps[:, b * 512 + lo: b * 512 + hi]

    def span(self, b0, nbank):
        return self.ps[:, b0 * 512:(b0 + nbank) * 512]

    def acquire(self, eng, b):
        _wt(eng, self.rel[b])
        self.rel[b] = None

    def release(self, b, evt):
        self.rel[b] = evt


def build_nc(nt=NT):
    _stage = [0]

    def go():
        _stage[0] += 1
        return STOP is None or _stage[0] <= STOP

    nc = bass.Bass("TRN2", target_bir_lowering=False,
                   detect_race_conditions=False,
                   dynamic_dma_scratch_size=8192)

    te = nt * TILE
    x_slab = nc.dram_tensor("x_slab", [2 * te, C], F32, kind="ExternalInput").ap()
    wT = {}
    for w in ("wk", "wv", "wr", "wg", "wo", "wfr", "wfk", "wfv"):
        wT[w] = nc.dram_tensor(w, [C, C], F32, kind="ExternalInput").ap()
    vec_d = {}
    for m in ("maak", "maav", "maar", "maag", "mur", "muk"):
        vec_d[m] = nc.dram_tensor(m, [128, 8], F32, kind="ExternalInput").ap()
    dec_d = nc.dram_tensor("dec_row", [1, 16], F32, kind="ExternalInput").ap()
    faa_d = nc.dram_tensor("faa_row", [1, 16], F32, kind="ExternalInput").ap()
    lnw_d = nc.dram_tensor("lnw_row", [1, C], F32, kind="ExternalInput").ap()
    lnb_d = nc.dram_tensor("lnb_row", [1, C], F32, kind="ExternalInput").ap()
    out_d = nc.dram_tensor("out", [te, C], F32, kind="ExternalOutput").ap()
    wsp_d = {w: nc.dram_tensor("wsp_" + w, [NCC * 128, 2 * C], BF16,
                               kind="Internal").ap() for w in
             ("wk", "wv", "wr", "wg", "wo", "wfk", "wfr", "wfv")}

    jj, ii = np.meshgrid(np.arange(128), np.arange(128), indexing="ij")
    c_grid = nc.inline_tensor((jj + 1 - ii).astype(np.float32), name="c_gridt")
    c_tri = nc.inline_tensor((ii > jj).astype(np.float32), name="c_trit")
    c_diag = nc.inline_tensor((ii == jj).astype(np.float32), name="c_diagt")
    c_rampi = nc.inline_tensor(
        np.broadcast_to(-np.arange(128, dtype=np.float32), (128, 128)).copy(),
        name="c_rampi")
    c_rampj = nc.inline_tensor(
        np.broadcast_to((np.arange(128, dtype=np.float32) - 127.0)[:, None],
                        (128, 64)).copy(), name="c_rampj")
    c_ident = nc.inline_tensor(np.eye(128, dtype=np.float32), name="c_ident")

    A = nc.alloc_sbuf_tensor
    XNT = A("XNT", [128, NCC * TILE], F32)
    XNB = A("XNB", [128, 8], F32)
    BUFA = A("BUFA", [128, 4096], F32)
    D = A("D", [128, NCC * TILE], F32)
    MIX0H = A("MIX0H", [128, NCC * TILE], BF16)
    MIX0L = A("MIX0L", [128, NCC * TILE], BF16)
    MIX1H = A("MIX1H", [128, NCC * TILE], BF16)
    MIX1L = A("MIX1L", [128, NCC * TILE], BF16)
    SETA = A("SETA", [128, 1024], F32)
    MIXT = SETA[:, 0:1024]
    KFM = A("KFM", [128, NCC * TILE], F32)
    VTM = A("VTM", [128, TPT * C], F32)
    RFM = A("RFM", [128, NCC * TILE], F32)
    ACCT = A("ACCT", [128, TPT * C], F32)
    KTIL = A("KTIL", [128, C], F32)
    SSCR = A("SSCR", [128, 512], F32)
    RS = A("RS", [128, C], F32)
    APR = A("APR", [128, H * 128], F32)
    WF32 = APR[:, 0:2048]
    _KTILB = KTIL.bitcast(BF16)
    _RSB = RS.bitcast(BF16)
    def _wsphl(k):
        buf = _KTILB if (k % 4) < 2 else _RSB
        base = (k % 2) * 1024
        return buf[:, base:base + 1024]
    S_ALL = A("S_ALL", [128, 512], F32)
    S_ALL2 = A("S_ALL2", [128, 512], F32)
    XIO = A("XIO", [128, 2 * C], F32)
    LNS = A("LNS", [128, 32], F32)
    GNS = A("GNS", [128, 320], F32)
    IDENT = A("IDENT", [128, 128], F32)
    GRIDT = SETA[:, 0:128]
    TRIT = SETA[:, 128:256]
    DIAGT = SETA[:, 256:384]
    RAMPI = SETA[:, 384:512]
    RAMPJ = SETA[:, 512:576]
    MWT = A("MWT", [128, H * 128], F32)
    MLAM = A("MLAM", [128, NCC * 128], F32)
    MK = A("MK", [128, C], F32)
    WSBC = A("WSBC", [128, 512], F32)
    MGAM = A("MGAM", [128, C], F32)
    MBET = A("MBET", [128, C], F32)
    DECB = A("DECB", [128, 16], F32)
    FAAB = A("FAAB", [128, 16], F32)
    LNW = A("LNW", [128, 16], F32)
    WS128 = A("WS128", [128, 16], F32)
    ONES64 = A("ONES64", [128, 64], F32)
    NEGHALF = A("NEGHALF", [128, 1], F32)
    INVC = A("INVC", [128, 1], F32)
    NEGINVC = A("NEGINVC", [128, 1], F32)
    NEGEPSC = A("NEGEPSC", [128, 1], F32)
    INVN = A("INVN", [128, 1], F32)
    NEGEPSN = A("NEGEPSN", [128, 1], F32)
    VEC = {m: A("t_" + m, [128, 8], F32) for m in
           ("maak", "maav", "maar", "maag", "mur", "muk")}
    OMR = A("OMR", [128, 8], F32)
    OMK = A("OMK", [128, 8], F32)
    MSCR = SETA[:, 576:704]
    WRINGB = A("WRINGB", [128, RING * 2 * C], BF16)

    PS = nc.alloc_psum_tensor("PS", [128, 8 * 512], F32)
    banks = Banks(PS)

    st = ExitStack()
    with st:
        s_t = Sem(st, nc, "s_t")
        s_v = Sem(st, nc, "s_v")
        s_s = Sem(st, nc, "s_s")
        s_cst = Sem(st, nc, "s_cst")
        s_x = [Sem(st, nc, f"s_x{i}") for i in range(2)]
        s_xr = [Sem(st, nc, f"s_xr{i}") for i in range(2)]
        s_w = [Sem(st, nc, f"s_w{i}") for i in range(RING)]
        s_gst = Sem(st, nc, "s_gst")
        s_wf = Sem(st, nc, "s_wf")
        s_wsp = Sem(st, nc, "s_wsp")

        S = {
            "xio_free": [None, None],
            "xio_loaded": [None, None],
            "x_cnt": [0, 0],
            "xr_cnt": [0, 0],
            "mix_free": {0: None, 1: None},
            "mix_ready": {0: None, 1: None},
            "mixt_free": None,
            "d_ready": None,
            "xnt_ready": None,
            "xnt_free": None,
            "bufa_free": None,
            "kfm_free": None,
            "vtm_free": None,
            "rfm_free": None,
            "acct_free": None,
            "supd": None,
            "yread": None,
            "setup_done": None,
        }

        def vbar(v, ins):
            ev = _inc(ins, s_v)
            _wt(v, ev)
            return ev

        _scnt = [0]

        def state_buf():
            return (S_ALL, S_ALL2)[_scnt[0] % 2]

        def state_dst():
            return (S_ALL, S_ALL2)[(_scnt[0] + 1) % 2]

        wring = {"n": 0, "scount": [0] * RING, "freed": [], "loadevt": []}

        def wring_slot_ap(i):
            s = i % RING
            return WRINGB[:, s * 2 * C:(s + 1) * 2 * C]

        def wring_load(sync, dram_ap, gate=None):
            i = wring["n"]
            s = i % RING
            if i >= RING:
                _wt(sync, wring["freed"][i - RING])
            _wt(sync, gate)
            ins = sync.dma_start(out=wring_slot_ap(i), in_=dram_ap)
            wring["scount"][s] += 1
            ins.then_inc(s_w[s].h, 16)
            s_w[s].v += 16
            wring["loadevt"].append((s_w[s], 16 * wring["scount"][s]))
            wring["freed"].append(None)
            wring["n"] += 1
            return i

        with nc.Block() as block:
            @block.sync
            def _(sync):
                for ap_, dst in ((c_ident.ap(), IDENT), (c_grid.ap(), GRIDT),
                                 (c_tri.ap(), TRIT), (c_diag.ap(), DIAGT),
                                 (c_rampi.ap(), RAMPI), (c_rampj.ap(), RAMPJ)):
                    _inc(sync.dma_start(out=dst[:, :], in_=ap_[:, :]), s_cst, 16)
                for m in VEC:
                    _inc(sync.dma_start(out=VEC[m][:, :], in_=vec_d[m][:, :]),
                         s_cst, 16)
                _inc(sync.dma_start(out=DECB[:, :],
                                    in_=dec_d.broadcast_to([128, 16])), s_cst, 16)
                _inc(sync.dma_start(out=FAAB[:, :],
                                    in_=faa_d.broadcast_to([128, 16])), s_cst, 16)
                _inc(sync.dma_start(out=MGAM[:, :],
                                    in_=lnw_d.broadcast_to([128, C])), s_cst, 16)
                _inc(sync.dma_start(out=MBET[:, :],
                                    in_=lnb_d.broadcast_to([128, C])), s_cst, 16)

            n_cst = s_cst.v

            @block.vector
            def _(v):
                v.wait_ge(s_cst.h, n_cst)
                v.memset(XNB[:, :], 0.0)
                v.memset(S_ALL[:, :], 0.0)
                v.memset(ONES64[:, :], 1.0)
                v.memset(NEGHALF[:, :], -0.5)
                v.memset(INVC[:, :], 1.0 / C)
                v.memset(NEGINVC[:, :], -1.0 / C)
                v.memset(NEGEPSC[:, :], -EPS_LN)
                v.memset(INVN[:, :], 1.0 / NDIM)
                v.memset(NEGEPSN[:, :], -EPS_GN)
                v.tensor_scalar(OMR[:, :], VEC["mur"][:, :], -1.0, 1.0,
                                OP.mult, OP.add)
                ev = _inc(v.tensor_scalar(OMK[:, :], VEC["muk"][:, :], -1.0, 1.0,
                                          OP.mult, OP.add), s_v)
                S["vsetup"] = ev

            @block.scalar
            def _(s):
                s.wait_ge(s_cst.h, n_cst)
                _wt(s, S["vsetup"])
                ev = _inc(s.activation(LNW[:, :], DECB[:, :], ACT.Exp), s_s)
                S["expd"] = ev

            @block.vector
            def _(v):
                _wt(v, S["expd"])
                ev = _inc(v.tensor_scalar_mul(WS128[:, :], LNW[:, :], -128.0),
                          s_v)
                S["ex128"] = ev

            @block.scalar
            def _(s):
                _wt(s, S["ex128"])
                s.activation(WS128[:, :], WS128[:, :], ACT.Exp)
                ev = None
                for h in range(H):
                    hcol = (h % 2) * 1024 + (h // 2) * 128
                    s.activation(MWT[:, hcol:hcol + 128], GRIDT[:, :],
                                 ACT.Exp, scale=LNW[:, h:h + 1])
                    s.activation(MK[:, h * 64:(h + 1) * 64], RAMPJ[:, :],
                                 ACT.Exp, scale=LNW[:, h:h + 1])
                    bp = (h % 2) * 64
                    s.activation(WSBC[bp:bp + 64,
                                      (h // 2) * 64:(h // 2) * 64 + 64],
                                 ONES64[0:64, :], ACT.Identity,
                                 scale=WS128[bp:bp + 64, h:h + 1])
                for cc in range(NCC):
                    s.activation(MLAM[0:64, cc * 128:(cc + 1) * 128],
                                 RAMPI[0:64, :], ACT.Exp,
                                 scale=LNW[0:64, 2 * cc:2 * cc + 1])
                    ev = s.activation(MLAM[64:128, cc * 128:(cc + 1) * 128],
                                      RAMPI[64:128, :], ACT.Exp,
                                      scale=LNW[64:128, 2 * cc + 1:2 * cc + 2])
                S["ssetup"] = _inc(ev, s_s)

            @block.vector
            def _(v):
                _wt(v, S["ssetup"])
                ev = None
                for h in range(H):
                    hcol = (h % 2) * 1024 + (h // 2) * 128
                    blk = MWT[:, hcol:hcol + 128]
                    v.tensor_scalar_mul(MSCR[:, :], DIAGT[:, :],
                                        FAAB[:, h:h + 1])
                    vbar(v, v.tensor_tensor(blk, blk, TRIT[:, :], OP.mult))
                    ev = v.tensor_tensor(blk, blk, MSCR[:, :], OP.add)
                S["setup_done"] = _inc(ev, s_v)

            SPLIT_ORDER = ("wk", "wv", "wr", "wg", "wo", "wfk", "wfr", "wfv")
            split_done = {w: [None] * NCC for w in SPLIT_ORDER}
            sp = {"next": 0, "ld": {}, "tt": {}, "st": {}}
            sp_list = [(w, cc, h) for w in SPLIT_ORDER for cc in range(NCC)
                       for h in range(2)]

            def sp_slot(k, buf, width):
                p = (k % 4) * width
                return buf[:, p:p + width]

            def emit_split_pair():
                k0 = sp["next"]
                if k0 >= len(sp_list):
                    return
                ks = [k for k in (k0, k0 + 1) if k < len(sp_list)]
                sp["next"] = k0 + len(ks)

                @block.sync
                def _(sync):
                    for k in ks:
                        w, cc, h = sp_list[k]
                        _wt(sync, sp["tt"].get(k - 4))
                        ins = sync.dma_start(
                            out=sp_slot(k, WF32, 512),
                            in_=wT[w][cc * 128:(cc + 1) * 128,
                                      h * 512:(h + 1) * 512])
                        sp["ld"][k] = _inc(ins, s_wf, 16)

                @block.vector
                def _(v):
                    cpev = {}
                    for k in ks:
                        _wt(v, sp["ld"][k])
                        _wt(v, sp["st"].get(k - 4))
                        hl = _wsphl(k)
                        cpev[k] = _inc(v.tensor_copy(hl[:, 0:512],
                                                     sp_slot(k, WF32, 512)),
                                       s_v)
                    for k in ks:
                        _wt(v, cpev[k])
                        hl = _wsphl(k)
                        ins = v.tensor_tensor(
                            hl[:, 512:1024],
                            sp_slot(k, WF32, 512),
                            hl[:, 0:512], OP.subtract)
                        sp["tt"][k] = _inc(ins, s_v)

                @block.scalar
                def _(s):
                    for k in ks:
                        w, cc, h = sp_list[k]
                        _wt(s, sp["tt"][k])
                        rows = slice(cc * 128, (cc + 1) * 128)
                        ins = s.dma_start(
                            out=wsp_d[w][rows, h * 1024:(h + 1) * 1024],
                            in_=_wsphl(k))
                        ev = _inc(ins, s_wsp, 16)
                        sp["st"][k] = ev
                        if h == 1:
                            split_done[w][cc] = ev

            def emit_one_split():
                emit_split_pair()

            def pump_splits(n=2):
                for _ in range(n):
                    emit_one_split()

            def ensure_splits(w):
                while split_done[w][NCC - 1] is None:
                    emit_one_split()

            def emit_ln_stats(kind, ti):
                row0 = (0 if kind == "warm" else te) + ti * TILE
                sqev = {}
                for tch in range(TPT):
                    pump_splits(1)
                    sl = tch % 2

                    @block.sync
                    def _(sync, tch=tch, sl=sl, row0=row0):
                        _wt(sync, S["xio_free"][sl])
                        ins = sync.dma_start(
                            out=XIO[:, sl * C:(sl + 1) * C],
                            in_=x_slab[row0 + tch * 128: row0 + (tch + 1) * 128, :])
                        ins.then_inc(s_x[sl].h, 16)
                        s_x[sl].v += 16
                        S["x_cnt"][sl] += 1
                        S["xio_loaded"][sl] = (s_x[sl], 16 * S["x_cnt"][sl])

                    ld_evt = S["xio_loaded"][sl]

                    @block.vector
                    def _(v, tch=tch, sl=sl, ld_evt=ld_evt, sqev=sqev):
                        _wt(v, ld_evt)
                        if tch == 0:
                            _wt(v, S["setup_done"])
                            _wt(v, S["bufa_free"])
                        xio = XIO[:, sl * C:(sl + 1) * C]
                        v.tensor_copy(BUFA[:, tch * C:(tch + 1) * C], xio)
                        v.tensor_reduce(LNS[:, tch:tch + 1], xio, AX.X, OP.add)
                        ev = _inc(v.tensor_tensor(
                            D[:, tch * C:(tch + 1) * C], xio, xio, OP.mult),
                            s_v)
                        sqev[tch] = ev
                        S["xio_free"][sl] = ev
                        if tch > 0:
                            _wt(v, sqev[tch - 1])
                            _inc(v.tensor_reduce(
                                LNS[:, 4 + tch - 1:4 + tch],
                                D[:, (tch - 1) * C:tch * C],
                                AX.X, OP.add), s_v)

                @block.vector
                def _(v, sqev=sqev):
                    _wt(v, sqev[TPT - 1])
                    _inc(v.tensor_reduce(
                        LNS[:, 4 + TPT - 1:4 + TPT],
                        D[:, (TPT - 1) * C:TPT * C],
                        AX.X, OP.add), s_v)

                red_evt = (s_v, s_v.v)

                @block.scalar
                def _(s, red_evt=red_evt):
                    _wt(s, red_evt)
                    s.activation(LNS[:, 8:12], LNS[:, 0:4], ACT.Identity,
                                 scale=INVC[:, :])
                    s.activation(LNS[:, 12:16], LNS[:, 0:4], ACT.Identity,
                                 scale=NEGINVC[:, :])
                    s.activation(LNS[:, 16:20], LNS[:, 8:12], ACT.Square)
                    _inc(s.activation(LNS[:, 16:20], LNS[:, 16:20],
                                      ACT.Identity, bias=NEGEPSC[:, :]), s_s)
                msq_evt = (s_s, s_s.v)

                @block.vector
                def _(v, msq_evt=msq_evt):
                    _wt(v, msq_evt)
                    _inc(v.scalar_tensor_tensor(
                        LNS[:, 20:24], LNS[:, 4:8], 1.0 / C,
                        LNS[:, 16:20], OP.mult, OP.subtract), s_v)
                var_evt = (s_v, s_v.v)

                @block.scalar
                def _(s, var_evt=var_evt):
                    _wt(s, var_evt)
                    s.activation(LNS[:, 24:28], LNS[:, 20:24], ACT.Ln)
                    _inc(s.activation(LNS[:, 28:32], LNS[:, 24:28], ACT.Exp,
                                      scale=NEGHALF[:, :]), s_s)
                    ev = None
                    for tch in range(TPT):
                        dst = BUFA[:, tch * C:(tch + 1) * C]
                        s.activation(dst, dst, ACT.Identity,
                                     bias=LNS[:, 12 + tch:13 + tch])
                        ev = s.activation(dst, dst, ACT.Identity,
                                          scale=LNS[:, 28 + tch:29 + tch])
                    _inc(ev, s_s)

                S["norm_evt"] = (s_s, s_s.v)

            def emit_ln_finish():
                norm_evt = S["norm_evt"]
                last_tr = None
                for cc in range(NCC):
                    bk = 6 + (cc % 2)

                    @block.tensor
                    def _(t, cc=cc, bk=bk, norm_evt=norm_evt):
                        _wt(t, norm_evt)
                        banks.acquire(t, bk)
                        for tch in range(TPT):
                            _inc(t.transpose(
                                banks.ap(bk, tch * 128, (tch + 1) * 128),
                                BUFA[:, tch * C + cc * 128:
                                     tch * C + (cc + 1) * 128],
                                IDENT[:, :]), s_t)
                    tr_evt = (s_t, s_t.v)
                    last_tr = tr_evt

                    @block.scalar
                    def _(s, cc=cc, bk=bk, tr_evt=tr_evt):
                        _wt(s, tr_evt)
                        _wt(s, S["xnt_free"])
                        ev = _inc(s.activation(XNT[:, cc * TILE:(cc + 1) * TILE],
                                               banks.ap(bk), ACT.Identity), s_s)
                        banks.release(bk, ev)

                S["xnt_ready"] = (s_s, s_s.v)
                S["xnt_free"] = None
                S["bufa_free"] = last_tr

                @block.vector
                def _(v):
                    _wt(v, S["xnt_ready"])
                    xv3 = XNT[:, :].rearrange("p (c t) -> p c t", t=TILE)
                    dv3 = D[:, :].rearrange("p (c t) -> p c t", t=TILE)
                    v.tensor_tensor(dv3[:, :, 0:1],
                                    XNB[:, :].rearrange("p (c o) -> p c o", o=1),
                                    xv3[:, :, 0:1], OP.subtract)
                    ev = _inc(v.tensor_tensor(dv3[:, :, 1:TILE],
                                              xv3[:, :, 0:TILE - 1],
                                              xv3[:, :, 1:TILE], OP.subtract),
                              s_v)
                    S["d_ready"] = ev

            def emit_mix(mslot, vtile):
                pump_splits(2)
                bh = MIX0H if mslot == 0 else MIX1H
                bl = MIX0L if mslot == 0 else MIX1L
                evs = {}

                def mslot_ap(cc):
                    p = (cc % 2) * 512
                    return MIXT[:, p:p + 512]

                @block.vector
                def _(v, vtile=vtile, mslot=mslot, bh=bh, bl=bl):
                    def stt(cc):
                        sl = slice(cc * TILE, (cc + 1) * TILE)
                        if cc == 0:
                            _wt(v, S["d_ready"])
                            _wt(v, S["mix_free"][mslot])
                            _wt(v, S["setup_done"])
                            _wt(v, S["mixt_free"])
                        _wt(v, evs.get(("lo", cc - 2)))
                        evs[("t", cc)] = _inc(v.scalar_tensor_tensor(
                            mslot_ap(cc), D[:, sl], vtile[:, cc:cc + 1],
                            XNT[:, sl], OP.mult, OP.add), s_v)

                    def cp(cc):
                        sl = slice(cc * TILE, (cc + 1) * TILE)
                        _wt(v, evs[("t", cc)])
                        evs[("hi", cc)] = _inc(
                            v.tensor_copy(bh[:, sl], mslot_ap(cc)), s_v)

                    def lo(cc):
                        sl = slice(cc * TILE, (cc + 1) * TILE)
                        _wt(v, evs[("hi", cc)])
                        evs[("lo", cc)] = _inc(
                            v.tensor_tensor(bl[:, sl], mslot_ap(cc),
                                            bh[:, sl], OP.subtract), s_v)

                    for k in range(0, NCC, 2):
                        stt(k)
                        stt(k + 1)
                        cp(k)
                        cp(k + 1)
                        lo(k)
                        lo(k + 1)
                    S["mixt_free"] = evs[("lo", NCC - 1)]
                    S["mix_ready"][mslot] = evs[("lo", NCC - 1)]

            def emit_proj_wstat(wname, mslot, dst, out_evts, dst_free_key,
                                evac_act=None):
                pump_splits(2)
                ensure_splits(wname)
                mh = MIX0H if mslot == 0 else MIX1H
                ml = MIX0L if mslot == 0 else MIX1L
                loads = [None] * NCC
                bank_evts = [None] * 8

                @block.sync
                def _(sync):
                    for cc in range(RING):
                        loads[cc] = wring_load(sync,
                                               wsp_d[wname][cc * 128:(cc + 1) * 128, :],
                                               gate=split_done[wname][cc])

                for cc in range(NCC):
                    @block.tensor
                    def _(t, cc=cc):
                        _wt(t, wring["loadevt"][loads[cc]])
                        if cc == 0:
                            _wt(t, S["mix_ready"][mslot])
                            for d in range(8):
                                banks.acquire(t, d)
                        slot = wring_slot_ap(loads[cc])
                        msl = slice(cc * TILE, (cc + 1) * TILE)
                        ins = None
                        for d in range(8):
                            base = (d // 4) * 1024 + (d % 4) * 128
                            w_h = slot[:, base:base + 128]
                            w_l = slot[:, base + 512:base + 640]
                            ins = t.matmul(banks.ap(d), w_h, mh[:, msl],
                                           start=(cc == 0), stop=False)
                            ins = t.matmul(banks.ap(d), w_h, ml[:, msl],
                                           start=False, stop=False)
                            ins = t.matmul(banks.ap(d), w_l, mh[:, msl],
                                           start=False, stop=(cc == NCC - 1))
                            if cc == NCC - 1:
                                bank_evts[d] = _inc(ins, s_t)
                        if cc < NCC - 1:
                            wring["freed"][loads[cc]] = _inc(ins, s_t)
                        else:
                            wring["freed"][loads[cc]] = bank_evts[7]
                            S["mix_free"][mslot] = bank_evts[7]

                    if cc + RING < NCC:
                        @block.sync
                        def _(sync, cc=cc):
                            loads[cc + RING] = wring_load(
                                sync,
                                wsp_d[wname][(cc + RING) * 128:(cc + RING + 1) * 128, :],
                                gate=split_done[wname][cc + RING])

                dst_free = S[dst_free_key]
                S[dst_free_key] = None
                for d in range(8):
                    if evac_act is not None or d % 2 == 0:
                        @block.scalar
                        def _(s, d=d):
                            _wt(s, bank_evts[d])
                            _wt(s, dst_free)
                            ev = _inc(s.activation(
                                dst[:, d * TILE:(d + 1) * TILE], banks.ap(d),
                                evac_act if evac_act is not None
                                else ACT.Identity), s_s)
                            banks.release(d, ev)
                            out_evts[d] = ev
                    else:
                        @block.vector
                        def _(v, d=d):
                            _wt(v, bank_evts[d])
                            _wt(v, dst_free)
                            ev = _inc(v.tensor_copy(
                                dst[:, d * TILE:(d + 1) * TILE], banks.ap(d)),
                                s_v)
                            banks.release(d, ev)
                            out_evts[d] = ev

            def emit_proj_xstat(wname, stats, stat_ready_fn, evac_fn,
                                mix_free_key=None):
                pump_splits(2)
                ensure_splits(wname)
                sh_buf, sl_buf = stats
                loads = [None] * NCC
                bank_evts = [None] * 8

                @block.sync
                def _(sync):
                    for cc in range(RING):
                        loads[cc] = wring_load(sync,
                                               wsp_d[wname][cc * 128:(cc + 1) * 128, :],
                                               gate=split_done[wname][cc])

                for cc in range(NCC):
                    @block.tensor
                    def _(t, cc=cc):
                        _wt(t, wring["loadevt"][loads[cc]])
                        if cc == 0:
                            _wt(t, stat_ready_fn())
                            for b in range(8):
                                banks.acquire(t, b)
                        slot = wring_slot_ap(loads[cc])
                        ins = None
                        for tch in range(TPT):
                            ssl = slice(cc * TILE + tch * 128,
                                        cc * TILE + (tch + 1) * 128)
                            for dh in range(2):
                                b = 2 * tch + dh
                                movh = slot[:, dh * 1024:dh * 1024 + 512]
                                movl = slot[:, dh * 1024 + 512:
                                            (dh + 1) * 1024]
                                ins = t.matmul(banks.ap(b), sh_buf[:, ssl],
                                               movh, start=(cc == 0),
                                               stop=False)
                                ins = t.matmul(banks.ap(b), sh_buf[:, ssl],
                                               movl, start=False, stop=False)
                                ins = t.matmul(banks.ap(b), sl_buf[:, ssl],
                                               movh, start=False,
                                               stop=(cc == NCC - 1))
                                if cc == NCC - 1:
                                    bank_evts[b] = _inc(ins, s_t)
                        if cc < NCC - 1:
                            wring["freed"][loads[cc]] = _inc(ins, s_t)
                        else:
                            wring["freed"][loads[cc]] = bank_evts[7]
                            if mix_free_key is not None:
                                S["mix_free"][mix_free_key] = bank_evts[7]

                    if cc + RING < NCC:
                        @block.sync
                        def _(sync, cc=cc):
                            loads[cc + RING] = wring_load(
                                sync,
                                wsp_d[wname][(cc + RING) * 128:(cc + RING + 1) * 128, :],
                                gate=split_done[wname][cc + RING])

                for tch in range(TPT):
                    for dh in range(2):
                        evac_fn(2 * tch + dh, tch, dh, bank_evts[2 * tch + dh])

            def evac_simple(dst, dst_free_key=None, mask=None, act=None,
                            evts_out=None):
                holder = {"df": S[dst_free_key] if dst_free_key else None}
                if dst_free_key:
                    S[dst_free_key] = None

                def f(b, tch, dh, evt):
                    sl = slice(tch * C + dh * 512, tch * C + (dh + 1) * 512)
                    if mask is not None:
                        @block.vector
                        def _(v, b=b, sl=sl, dh=dh, evt=evt):
                            _wt(v, evt)
                            _wt(v, holder["df"])
                            ev = _inc(v.tensor_tensor(
                                dst[:, sl], banks.ap(b),
                                mask[:, dh * 512:(dh + 1) * 512], OP.mult), s_v)
                            banks.release(b, ev)
                            if evts_out is not None:
                                evts_out[b] = ev
                    elif act is not None:
                        @block.scalar
                        def _(s, b=b, sl=sl, evt=evt):
                            _wt(s, evt)
                            _wt(s, holder["df"])
                            ev = _inc(s.activation(dst[:, sl], banks.ap(b), act),
                                      s_s)
                            banks.release(b, ev)
                            if evts_out is not None:
                                evts_out[b] = ev
                    else:
                        if b % 2 == 0:
                            @block.scalar
                            def _(s, b=b, sl=sl, evt=evt):
                                _wt(s, evt)
                                _wt(s, holder["df"])
                                ev = _inc(s.activation(dst[:, sl],
                                                       banks.ap(b),
                                                       ACT.Identity), s_s)
                                banks.release(b, ev)
                                if evts_out is not None:
                                    evts_out[b] = ev
                        else:
                            @block.vector
                            def _(v, b=b, sl=sl, evt=evt):
                                _wt(v, evt)
                                _wt(v, holder["df"])
                                ev = _inc(v.tensor_copy(dst[:, sl],
                                                        banks.ap(b)), s_v)
                                banks.release(b, ev)
                                if evts_out is not None:
                                    evts_out[b] = ev
                return f

            def rf_sl(b):
                if b < 2:
                    return KTIL[:, b * 512:(b + 1) * 512]
                if b < 4:
                    return RS[:, (b - 2) * 512:(b - 1) * 512]
                return APR[:, (b - 4) * 512:(b - 3) * 512]

            def emit_save_xnb():
                @block.vector
                def _(v):
                    xl = XNT[:, :].rearrange("p (c t) -> p c t", t=TILE)
                    ev = _inc(v.tensor_copy(
                        XNB[:, :].rearrange("p (c o) -> p c o", o=1),
                        xl[:, :, TILE - 1:TILE]), s_v)
                    S["xnt_free"] = ev

            emit_ln_stats("warm", 0)
            emit_ln_finish()
            for wt_i in range(nt):
                if not go():
                    break
                if not go():
                    break
                if wt_i == 0:
                    emit_mix(0, VEC["maak"])
                emit_mix(1, VEC["maav"])
                if wt_i + 1 < nt:
                    emit_ln_stats("warm", wt_i + 1)
                else:
                    emit_ln_stats("emit", 0)

                kevts = [None] * 8
                if not go():
                    break
                emit_proj_xstat("wk", (MIX0H, MIX0L), lambda: S["mix_ready"][0],
                                evac_simple(KFM, "kfm_free", mask=MK,
                                            evts_out=kevts), mix_free_key=0)
                vevts = [None] * 8
                if not go():
                    break
                emit_proj_xstat("wv", (MIX1H, MIX1L), lambda: S["mix_ready"][1],
                                evac_simple(VTM, "vtm_free", evts_out=vevts),
                                mix_free_key=1)
                pump_splits(12)
                if not go():
                    break

                for ch in range(TPT):
                    ube = (ch % 2) * 2
                    ubo = ube + 1

                    @block.tensor
                    def _(t, ch=ch, ube=ube, ubo=ubo):
                        _wt(t, kevts[2 * ch])
                        _wt(t, kevts[2 * ch + 1])
                        _wt(t, vevts[2 * ch])
                        _wt(t, vevts[2 * ch + 1])
                        banks.acquire(t, ube)
                        banks.acquire(t, ubo)
                        for par, ub in ((0, ube), (1, ubo)):
                            bp = par * 64
                            ins = None
                            for hh in range(8):
                                h = 2 * hh + par
                                ins = t.matmul(
                                    PS[bp:bp + 64,
                                       ub * 512 + hh * 64:
                                       ub * 512 + hh * 64 + 64],
                                    KFM[:, ch * C + h * 64: ch * C + (h + 1) * 64],
                                    VTM[:, ch * C + h * 64: ch * C + (h + 1) * 64],
                                    start=True, stop=True,
                                    tile_position=(0, bp))
                            S[f"u_mm{par}"] = _inc(ins, s_t)

                    @block.vector
                    def _(v, ube=ube, ubo=ubo):
                        _wt(v, S["u_mm0"])
                        _wt(v, S["u_mm1"])
                        sa, sd = state_buf(), state_dst()
                        _scnt[0] += 1
                        vbar(v, v.tensor_tensor(SSCR[:, :], sa[:, :],
                                                WSBC[:, :], OP.mult))
                        v.tensor_tensor(sd[0:64, :], SSCR[0:64, :],
                                        banks.ap(ube)[0:64, :], OP.add)
                        ev = _inc(v.tensor_tensor(sd[64:128, :],
                                                  SSCR[64:128, :],
                                                  banks.ap(ubo)[64:128, :],
                                                  OP.add), s_v)
                        S["supd"] = ev
                        banks.release(ube, ev)
                        banks.release(ubo, ev)

                S["kfm_free"] = S["u_mm1"]
                S["vtm_free"] = S["u_mm1"]
                pump_splits(12)
                emit_save_xnb()
                emit_ln_finish()
                emit_mix(0, VEC["maak"])

            ensure_splits("wfv")
            S["split_fin"] = (s_wsp, s_wsp.v)

            for ti in range(nt):
                if not go():
                    break
                if not go():
                    break
                emit_mix(1, VEC["maav"])

                kevts = [None] * 8
                if not go():
                    break
                emit_proj_wstat("wk", 0, KFM, kevts, "kfm_free")
                vevts = [None] * 8
                if not go():
                    break
                emit_proj_xstat("wv", (MIX1H, MIX1L), lambda: S["mix_ready"][1],
                                evac_simple(VTM, "vtm_free", evts_out=vevts),
                                mix_free_key=1)
                emit_mix(0, VEC["maar"])
                revts = [None] * 8
                if not go():
                    break
                emit_proj_wstat("wr", 0, RFM, revts, "rfm_free")
                if not go():
                    break

                yn_evts = [None] * TPT

                def emit_scan_prep(ch):
                    @block.tensor
                    def _(t, ch=ch):
                        for e in kevts:
                            _wt(t, e)
                        banks.acquire(t, 6)
                        banks.acquire(t, 7)
                        ins = None
                        for cc in range(NCC):
                            bk = 6 + cc // 4
                            ins = t.transpose(
                                banks.ap(bk, (cc % 4) * 128, (cc % 4 + 1) * 128),
                                KFM[:, cc * TILE + ch * 128:
                                    cc * TILE + (ch + 1) * 128],
                                IDENT[:, :])
                            _inc(ins, s_t)
                        S["ktr"] = (s_t, s_t.v)

                    @block.vector
                    def _(v, ch=ch, ti=ti):
                        _wt(v, S["ktr"])
                        if ti == 0 and ch == 0:
                            _wt(v, S["split_fin"])
                        v.tensor_tensor(KTIL[:, 0:512], banks.ap(6),
                                        MK[:, 0:512], OP.mult)
                        ev = _inc(v.tensor_tensor(KTIL[:, 512:1024],
                                                  banks.ap(7), MK[:, 512:1024],
                                                  OP.mult), s_v)
                        banks.release(6, ev)
                        banks.release(7, ev)
                        rsv = RS[:, :].rearrange("p (c t) -> p c t", t=128)
                        rfm3 = RFM[:, :].rearrange("p (c t) -> p c t", t=TILE)
                        for e in revts:
                            _wt(v, e)
                        ev = _inc(v.tensor_tensor(
                            rsv, rfm3[:, :, ch * 128:(ch + 1) * 128],
                            MLAM[:, :].rearrange("p (c t) -> p c t", t=128),
                            OP.mult), s_v)
                        S["rs"] = ev

                    @block.tensor
                    def _(t, ch=ch):
                        for e in revts:
                            _wt(t, e)
                        for b in range(4):
                            banks.acquire(t, b)
                        ins = None
                        for h in range(H):
                            cc = h // 2
                            bp = (h % 2) * 64
                            hcol = (h % 2) * 1024 + (h // 2) * 128
                            tok = slice(cc * TILE + ch * 128,
                                        cc * TILE + (ch + 1) * 128)
                            ins = t.matmul(
                                banks.span(0, 4)[:, hcol:hcol + 128],
                                KFM[bp:bp + 64, tok],
                                RFM[bp:bp + 64, tok],
                                start=True, stop=True,
                                tile_position=(bp, 0))
                        S["amm"] = _inc(ins, s_t)

                    @block.vector
                    def _(v, ch=ch, ti=ti):
                        _wt(v, S["amm"])
                        if ti == 0 and ch == 0:
                            _wt(v, S["split_fin"])
                        v.tensor_tensor(APR[:, 0:1024], banks.span(0, 2),
                                        MWT[:, 0:1024], OP.mult)
                        ev = _inc(v.tensor_tensor(APR[:, 1024:2048],
                                                  banks.span(2, 2),
                                                  MWT[:, 1024:2048], OP.mult),
                                  s_v)
                        for b in range(4):
                            banks.release(b, ev)
                        S["apr"] = ev

                emit_scan_prep(0)
                for ch in range(TPT):
                    if not go():
                        break

                    @block.tensor
                    def _(t, ch=ch):
                        _wt(t, S["apr"])
                        _wt(t, S["supd"])
                        _wt(t, S["rs"])
                        banks.acquire(t, 4)
                        banks.acquire(t, 5)
                        sa = state_buf()
                        ins = None
                        for h in range(H):
                            bp = (h % 2) * 64
                            cc = h // 2
                            hcol = (h % 2) * 1024 + (h // 2) * 128
                            ysl = banks.span(4, 2)[:, h * 64:(h + 1) * 64]
                            ins = t.matmul(ysl, APR[:, hcol:hcol + 128],
                                           VTM[:, ch * C + h * 64:
                                               ch * C + (h + 1) * 64],
                                           start=True, stop=False)
                            ins = t.matmul(
                                ysl,
                                RS[bp:bp + 64, cc * 128:(cc + 1) * 128],
                                sa[bp:bp + 64,
                                   (h // 2) * 64:(h // 2) * 64 + 64],
                                start=False, stop=True,
                                tile_position=(bp, 0))
                        S["yread"] = _inc(ins, s_t)

                    @block.tensor
                    def _(t, ch=ch):
                        banks.acquire(t, 2)
                        banks.acquire(t, 3)
                        for par, ub in ((0, 2), (1, 3)):
                            bp = par * 64
                            ins = None
                            for hh in range(8):
                                h = 2 * hh + par
                                ins = t.matmul(
                                    PS[bp:bp + 64,
                                       ub * 512 + hh * 64:
                                       ub * 512 + hh * 64 + 64],
                                    KTIL[:, h * 64:(h + 1) * 64],
                                    VTM[:, ch * C + h * 64: ch * C + (h + 1) * 64],
                                    start=True, stop=True,
                                    tile_position=(0, bp))
                            S[f"u_mm{par}"] = _inc(ins, s_t)

                    @block.vector
                    def _(v, ch=ch):
                        _wt(v, S["u_mm0"])
                        _wt(v, S["u_mm1"])
                        sa, sd = state_buf(), state_dst()
                        _scnt[0] += 1
                        vbar(v, v.tensor_tensor(SSCR[:, :], sa[:, :],
                                                WSBC[:, :], OP.mult))
                        v.tensor_tensor(sd[0:64, :], SSCR[0:64, :],
                                        banks.ap(2)[0:64, :], OP.add)
                        ev = _inc(v.tensor_tensor(sd[64:128, :],
                                                  SSCR[64:128, :],
                                                  banks.ap(3)[64:128, :],
                                                  OP.add), s_v)
                        S["supd"] = ev
                        banks.release(2, ev)
                        banks.release(3, ev)

                    @block.vector
                    def _(v, ch=ch):
                        _wt(v, S["yread"])
                        ev = _inc(v.tensor_copy(BUFA[:, ch * C:(ch + 1) * C],
                                                banks.span(4, 2)), s_v)
                        banks.release(4, ev)
                        banks.release(5, ev)
                        yn_evts[ch] = ev

                    if ch + 1 < TPT:
                        emit_scan_prep(ch + 1)

                if not go():
                    break
                emit_mix(1, VEC["maag"])

                @block.vector
                def _(v):
                    for e in yn_evts:
                        _wt(v, e)
                    y3 = BUFA[:, :].rearrange("p (g n) -> p g n", n=NDIM)
                    _inc(v.tensor_reduce(GNS[:, 0:64], y3, AX.X, OP.add), s_v)
                mean_evt = (s_v, s_v.v)

                @block.scalar
                def _(s, mean_evt=mean_evt):
                    _wt(s, mean_evt)
                    _wt(s, S["acct_free"])
                    S["acct_free"] = None
                    _inc(s.activation(ACCT[:, :], BUFA[:, :], ACT.Square), s_s)
                sq_evt = (s_s, s_s.v)

                @block.vector
                def _(v, sq_evt=sq_evt):
                    _wt(v, sq_evt)
                    sq3 = ACCT[:, :].rearrange("p (g n) -> p g n", n=NDIM)
                    _inc(v.tensor_reduce(GNS[:, 64:128], sq3, AX.X, OP.add),
                         s_v)
                red2_evt = (s_v, s_v.v)

                @block.scalar
                def _(s, red2_evt=red2_evt):
                    _wt(s, red2_evt)
                    s.activation(GNS[:, 128:192], GNS[:, 0:64], ACT.Identity,
                                 scale=INVN[:, :])
                    s.activation(GNS[:, 0:64], GNS[:, 128:192], ACT.Square)
                    s.activation(GNS[:, 0:64], GNS[:, 0:64], ACT.Identity,
                                 bias=NEGEPSN[:, :])
                    _inc(s.activation(GNS[:, 64:128], GNS[:, 64:128],
                                      ACT.Identity), s_s)
                gst_evt = (s_s, s_s.v)

                @block.vector
                def _(v, gst_evt=gst_evt):
                    _wt(v, gst_evt)
                    _inc(v.scalar_tensor_tensor(
                        GNS[:, 192:256], GNS[:, 64:128], 1.0 / NDIM,
                        GNS[:, 0:64], OP.mult, OP.subtract), s_v)
                var_evt = (s_v, s_v.v)

                @block.scalar
                def _(s, var_evt=var_evt):
                    _wt(s, var_evt)
                    s.activation(GNS[:, 64:128], GNS[:, 192:256], ACT.Ln)
                    _inc(s.activation(GNS[:, 256:320], GNS[:, 64:128],
                                      ACT.Exp, scale=NEGHALF[:, :]), s_s)
                sd_evt = (s_s, s_s.v)

                @block.vector
                def _(v, sd_evt=sd_evt):
                    _wt(v, sd_evt)
                    mb = GNS[:, 128:192].rearrange(
                        "p (g o) -> p g o", o=1).broadcast_to([128, 64, NDIM])
                    rb = GNS[:, 256:320].rearrange(
                        "p (g o) -> p g o", o=1).broadcast_to([128, 64, NDIM])
                    y3 = BUFA[:, :].rearrange("p (g n) -> p g n", n=NDIM)
                    vbar(v, v.tensor_tensor(y3, y3, mb, OP.subtract))
                    vbar(v, v.tensor_tensor(y3, y3, rb, OP.mult))
                    yc = BUFA[:, :].rearrange("p (a c) -> p a c", c=C)
                    gg = MGAM[:, :].rearrange(
                        "p (a c) -> p a c", a=1).broadcast_to([128, TPT, C])
                    bb = MBET[:, :].rearrange(
                        "p (a c) -> p a c", a=1).broadcast_to([128, TPT, C])
                    vbar(v, v.tensor_tensor(yc, yc, gg, OP.mult))
                    ev = _inc(v.tensor_tensor(yc, yc, bb, OP.add), s_v)
                    S["yn_all"] = ev

                gevts = [None] * 8
                emit_proj_xstat("wg", (MIX1H, MIX1L),
                                lambda: S["mix_ready"][1],
                                evac_simple(KFM, "kfm_free", act=ACT.Gelu,
                                            evts_out=gevts), mix_free_key=1)

                if not go():
                    break
                @block.vector
                def _(v):
                    for e in gevts:
                        _wt(v, e)
                    _wt(v, S["yn_all"])
                    _wt(v, S["vtm_free"])
                    ev = _inc(v.tensor_tensor(VTM[:, :], BUFA[:, :], KFM[:, :],
                                              OP.mult), s_v)
                    S["z_tm"] = ev
                    S["bufa_free"] = ev
                    S["vtm_free"] = None

                zf_free = S["mix_free"][0]
                S["mix_free"][0] = None
                for p in range(0, NCC, 2):
                    tr_evt = {}
                    for cc in (p, p + 1):
                        bk = 6 + (cc % 2)

                        @block.tensor
                        def _(t, cc=cc, bk=bk):
                            _wt(t, S["z_tm"])
                            banks.acquire(t, bk)
                            for tch in range(TPT):
                                _inc(t.transpose(
                                    banks.ap(bk, tch * 128, (tch + 1) * 128),
                                    VTM[:, tch * C + cc * 128:
                                        tch * C + (cc + 1) * 128],
                                    IDENT[:, :]), s_t)
                        tr_evt[cc] = (s_t, s_t.v)

                    @block.vector
                    def _(v, p=p, tr_evt=tr_evt, zf_free=zf_free):
                        cpev = {}
                        for cc in (p, p + 1):
                            _wt(v, tr_evt[cc])
                            if cc == 0:
                                _wt(v, zf_free)
                            cpev[cc] = _inc(v.tensor_copy(
                                MIX0H[:, cc * TILE:(cc + 1) * TILE],
                                banks.ap(6 + (cc % 2))), s_v)
                        for cc in (p, p + 1):
                            _wt(v, cpev[cc])
                            ev = _inc(v.tensor_tensor(
                                MIX0L[:, cc * TILE:(cc + 1) * TILE],
                                banks.ap(6 + (cc % 2)),
                                MIX0H[:, cc * TILE:(cc + 1) * TILE],
                                OP.subtract), s_v)
                            banks.release(6 + (cc % 2), ev)
                            if cc == NCC - 1:
                                S["zf"] = ev

                if not go():
                    break
                emit_mix(1, OMR)
                xres_ld = [None] * TPT

                def emit_xres_load(tch, ti=ti):
                    sl = tch % 2

                    @block.gpsimd
                    def _(g, tch=tch, sl=sl, ti=ti):
                        _wt(g, S["xio_free"][sl])
                        row0 = te + ti * TILE + tch * 128
                        ins = g.dma_start(out=XIO[:, sl * C:(sl + 1) * C],
                                          in_=x_slab[row0:row0 + 128, :])
                        ins.then_inc(s_xr[sl].h, 16)
                        s_xr[sl].v += 16
                        S["xr_cnt"][sl] += 1
                        xres_ld[tch] = (s_xr[sl], 16 * S["xr_cnt"][sl])

                emit_xres_load(0)
                emit_xres_load(1)

                oevts = [None] * 8

                def o_evac(b, tch, dh, evt):
                    sl = tch % 2

                    @block.vector
                    def _(v, b=b, tch=tch, dh=dh, evt=evt, sl=sl):
                        _wt(v, evt)
                        _wt(v, xres_ld[tch])
                        _wt(v, S["acct_free"])
                        S["acct_free"] = None
                        ev = _inc(v.tensor_tensor(
                            ACCT[:, tch * C + dh * 512:
                                 tch * C + (dh + 1) * 512],
                            banks.ap(b),
                            XIO[:, sl * C + dh * 512: sl * C + (dh + 1) * 512],
                            OP.add), s_v)
                        banks.release(b, ev)
                        oevts[b] = ev
                        if dh == 1:
                            S["xio_free"][sl] = ev
                    if dh == 1 and tch + 2 < TPT:
                        emit_xres_load(tch + 2)

                emit_proj_xstat("wo", (MIX0H, MIX0L), lambda: S["zf"], o_evac,
                                mix_free_key=0)

                if not go():
                    break
                emit_mix(0, OMK)
                emit_save_xnb()
                if ti + 1 < nt:
                    emit_ln_stats("emit", ti + 1)

                frevts = [None] * 8

                def fr_evac(b, tch, dh, evt):
                    @block.scalar
                    def _(s, b=b, evt=evt):
                        _wt(s, evt)
                        ev = _inc(s.activation(rf_sl(b), banks.ap(b),
                                               ACT.Sigmoid), s_s)
                        banks.release(b, ev)
                        frevts[b] = ev

                emit_proj_xstat("wfr", (MIX1H, MIX1L),
                                lambda: S["mix_ready"][1], fr_evac,
                                mix_free_key=1)

                if not go():
                    break
                fkevts = [None] * 8
                emit_proj_wstat("wfk", 0, KFM, fkevts, "kfm_free",
                                evac_act=ACT.Relu)
                if ti + 1 < nt:
                    emit_ln_finish()

                if not go():
                    break

                @block.vector
                def _(v):
                    for e in fkevts:
                        _wt(v, e)
                    ev = _inc(v.tensor_tensor(KFM[:, :], KFM[:, :], KFM[:, :],
                                              OP.mult), s_v)
                    _wt(v, ev)
                    _wt(v, S["mix_free"][1])
                    vbar(v, v.tensor_copy(MIX1H[:, :], KFM[:, :]))
                    ev = vbar(v, v.tensor_tensor(MIX1L[:, :], KFM[:, :],
                                                 MIX1H[:, :], OP.subtract))
                    S["kact"] = ev
                    S["kfm_free"] = ev

                if ti + 1 < nt:
                    emit_mix(0, VEC["maak"])

                out_evts = [None] * 8
                fv_pend = []

                def fv_evac(b, tch, dh, evt):
                    fv_pend.append((b, tch, dh, evt))
                    if len(fv_pend) < 2:
                        return

                    @block.vector
                    def _(v, pend=tuple(fv_pend)):
                        mev = {}
                        for b, tch, dh, evt in pend:
                            _wt(v, evt)
                            _wt(v, frevts[b])
                            _wt(v, oevts[b])
                            sl = slice(tch * C + dh * 512,
                                       tch * C + (dh + 1) * 512)
                            mev[b] = _inc(v.tensor_tensor(
                                VTM[:, sl], banks.ap(b), rf_sl(b), OP.mult),
                                s_v)
                        for b, tch, dh, evt in pend:
                            _wt(v, mev[b])
                            sl = slice(tch * C + dh * 512,
                                       tch * C + (dh + 1) * 512)
                            ev = _inc(v.tensor_tensor(VTM[:, sl], VTM[:, sl],
                                                      ACCT[:, sl], OP.add),
                                      s_v)
                            banks.release(b, ev)
                            out_evts[b] = ev
                    fv_pend.clear()

                emit_proj_xstat("wfv", (MIX1H, MIX1L), lambda: S["kact"],
                                fv_evac, mix_free_key=1)

                for tch in range(TPT):
                    @block.gpsimd
                    def _(g, tch=tch, ti=ti):
                        _wt(g, out_evts[2 * tch])
                        _wt(g, out_evts[2 * tch + 1])
                        row0 = ti * TILE + tch * 128
                        ins = g.dma_start(out=out_d[row0:row0 + 128, :],
                                          in_=VTM[:, tch * C:(tch + 1) * C])
                        ins.then_inc(s_gst.h, 16)
                        s_gst.v += 16

                S["acct_free"] = out_evts[7]
                S["rfm_free"] = S["rs"]
                S["vtm_free"] = (s_gst, s_gst.v)

            _BUFS = {"XNT": XNT, "BUFA": BUFA, "D": D,
                     "KFM": KFM, "VTM": VTM, "RFM": RFM,
                     "ACCT": ACCT, "KTIL": KTIL, "RS": RS, "APR": APR,
                     "S_ALL": S_ALL, "MWT": MWT, "MLAM": MLAM, "MK": MK,
                     "WSBC": WSBC, "MGAM": MGAM, "XNB": XNB, "LNW": LNW, "LNS": LNS, "XIO": XIO,
                     "WS128": WS128}
            _BF16S = {"MIX0S": (MIX0H, MIX0L), "MIX1S": (MIX1H, MIX1L),
                      "MIX0HH": (MIX0H, None), "MIX0LL": (MIX0L, None),
                      "MIX1HH": (MIX1H, None), "MIX1LL": (MIX1L, None)}
            if DBG_TAP in _BF16S:
                bh_, bl_ = _BF16S[DBG_TAP]

                @block.scalar
                def _(s):
                    _wt(s, (s_v, s_v.v))
                    _wt(s, (s_t, s_t.v))
                    _inc(s.activation(ACCT[:, :], bh_[:, :], ACT.Identity), s_s)

                if bl_ is not None:
                    @block.vector
                    def _(v):
                        _wt(v, (s_s, s_s.v))
                        _inc(v.tensor_tensor(ACCT[:, :], ACCT[:, :], bl_[:, :],
                                             OP.add), s_v)
                _BUFS[DBG_TAP] = ACCT
            if DBG_TAP is not None or s_gst.v == 0:
                @block.gpsimd
                def _(g):
                    _wt(g, (s_v, s_v.v))
                    _wt(g, (s_s, s_s.v))
                    _wt(g, (s_t, s_t.v))
                    if DBG_TAP is None:
                        ins = g.dma_start(out=out_d[0:128, 0:8], in_=XNB[:, :])
                        ins.then_inc(s_gst.h, 16)
                        s_gst.v += 16
                    else:
                        buf = _BUFS[DBG_TAP]
                        ncol = buf.shape[1]
                        nrow = min((ncol + C - 1) // C, te // 128)
                        for r in range(nrow):
                            w = min(C, ncol - r * C)
                            ins = g.dma_start(
                                out=out_d[r * 128:(r + 1) * 128, 0:w],
                                in_=buf[:, r * C:r * C + w])
                            ins.then_inc(s_gst.h, 16)
                            s_gst.v += 16

            total_st = s_gst.v

            @block.gpsimd
            def _(g):
                g.wait_ge(s_gst.h, total_st)

    return nc


def kernel(**inputs):
    ins = {k: np.asarray(v, dtype=np.float32) for k, v in inputs.items()}
    x = ins["L0_attn_x"]
    B, T, _ = x.shape

    def vec_tile(v):
        return np.ascontiguousarray(v.reshape(8, 128).T)

    common = {
        "wk": np.ascontiguousarray(ins["L0_attn_Wk"].T),
        "wv": np.ascontiguousarray(ins["L0_attn_Wv"].T),
        "wr": np.ascontiguousarray(ins["L0_attn_Wr"].T),
        "wg": np.ascontiguousarray(ins["L0_attn_Wg"].T),
        "wo": np.ascontiguousarray(ins["L0_attn_Wo"].T),
        "wfr": np.ascontiguousarray(ins["L0_ffn_W_r"].T),
        "wfk": np.ascontiguousarray(ins["L0_ffn_W_k"].T),
        "wfv": np.ascontiguousarray(ins["L0_ffn_W_v"].T),
        "maak": vec_tile(ins["L0_attn_time_maa_k"].reshape(-1)),
        "maav": vec_tile(ins["L0_attn_time_maa_v"].reshape(-1)),
        "maar": vec_tile(ins["L0_attn_time_maa_r"].reshape(-1)),
        "maag": vec_tile(ins["L0_attn_time_maa_g"].reshape(-1)),
        "mur": vec_tile(ins["L0_ffn_mu_r"].reshape(-1)),
        "muk": vec_tile(ins["L0_ffn_mu_k"].reshape(-1)),
        "dec_row": np.ascontiguousarray(ins["L0_attn_time_decay"].reshape(1, 16)),
        "faa_row": np.ascontiguousarray(ins["L0_attn_time_faaaa"].reshape(1, 16)),
        "lnw_row": np.ascontiguousarray(ins["L0_attn_ln_w"].reshape(1, C)),
        "lnb_row": np.ascontiguousarray(ins["L0_attn_ln_b"].reshape(1, C)),
    }

    in_maps = []
    for core in range(8):
        b, half = core // 2, core % 2
        if half == 0:
            slab = np.concatenate(
                [np.zeros((TE, C), np.float32), x[b, 0:TE]], axis=0)
        else:
            slab = np.ascontiguousarray(x[b, 0:2 * TE])
        m = dict(common)
        m["x_slab"] = slab
        in_maps.append(m)

    nc = build_nc(NT)
    res = run_bass_kernel_spmd(nc, in_maps, core_ids=list(range(8)))

    out = np.empty((B, T, C), np.float32)
    for core in range(8):
        b, half = core // 2, core % 2
        out[b, half * TE:(half + 1) * TE] = res.results[core]["out"]
    return out


# version 20
# speedup vs baseline: 1.0235x; 1.0163x over previous
import numpy as np
from contextlib import ExitStack

import concourse.bass as bass
import concourse.mybir as mybir
from concourse.bass_utils import run_bass_kernel_spmd

F32 = mybir.dt.float32
BF16 = mybir.dt.bfloat16
OP = mybir.AluOpType
ACT = mybir.ActivationFunctionType
AX = mybir.AxisListType

C = 1024
H = 16
NDIM = 64
TILE = 512
NCC = 8
TPT = 4
NT = 4
TE = 2048
EPS_LN = 1e-5
EPS_GN = 1e-5 * 64
RING = 2
STOP = None
DBG_Y = "both"
DBG_TAP = None


class Sem:
    def __init__(self, st, nc, name):
        self.h = st.enter_context(nc.semaphore(name))
        self.v = 0

    def tick(self, n=1):
        self.v += n
        return (self, self.v)


def _inc(instr, sem, n=1):
    instr.then_inc(sem.h, n)
    return sem.tick(n)


def _wt(eng, evt):
    if evt is not None:
        eng.wait_ge(evt[0].h, evt[1])


class Banks:
    def __init__(self, ps):
        self.ps = ps
        self.rel = [None] * 8

    def ap(self, b, lo=0, hi=512):
        return self.ps[:, b * 512 + lo: b * 512 + hi]

    def span(self, b0, nbank):
        return self.ps[:, b0 * 512:(b0 + nbank) * 512]

    def acquire(self, eng, b):
        _wt(eng, self.rel[b])
        self.rel[b] = None

    def release(self, b, evt):
        self.rel[b] = evt


def build_nc(nt=NT):
    _stage = [0]

    def go():
        _stage[0] += 1
        return STOP is None or _stage[0] <= STOP

    nc = bass.Bass("TRN2", target_bir_lowering=False,
                   detect_race_conditions=False,
                   dynamic_dma_scratch_size=8192)

    te = nt * TILE
    x_slab = nc.dram_tensor("x_slab", [2 * te, C], F32, kind="ExternalInput").ap()
    wT = {}
    for w in ("wk", "wv", "wr", "wg", "wo", "wfr", "wfk", "wfv"):
        wT[w] = nc.dram_tensor(w, [C, C], F32, kind="ExternalInput").ap()
    vec_d = {}
    for m in ("maak", "maav", "maar", "maag", "mur", "muk"):
        vec_d[m] = nc.dram_tensor(m, [128, 8], F32, kind="ExternalInput").ap()
    dec_d = nc.dram_tensor("dec_row", [1, 16], F32, kind="ExternalInput").ap()
    faa_d = nc.dram_tensor("faa_row", [1, 16], F32, kind="ExternalInput").ap()
    lnw_d = nc.dram_tensor("lnw_row", [1, C], F32, kind="ExternalInput").ap()
    lnb_d = nc.dram_tensor("lnb_row", [1, C], F32, kind="ExternalInput").ap()
    out_d = nc.dram_tensor("out", [te, C], F32, kind="ExternalOutput").ap()
    wsp_d = {w: nc.dram_tensor("wsp_" + w, [NCC * 128, 2 * C], BF16,
                               kind="Internal").ap() for w in
             ("wk", "wv", "wr", "wg", "wo", "wfk", "wfr", "wfv")}

    jj, ii = np.meshgrid(np.arange(128), np.arange(128), indexing="ij")
    c_grid = nc.inline_tensor((jj + 1 - ii).astype(np.float32), name="c_gridt")
    c_tri = nc.inline_tensor((ii > jj).astype(np.float32), name="c_trit")
    c_diag = nc.inline_tensor((ii == jj).astype(np.float32), name="c_diagt")
    c_rampi = nc.inline_tensor(
        np.broadcast_to(-np.arange(128, dtype=np.float32), (128, 128)).copy(),
        name="c_rampi")
    c_rampj = nc.inline_tensor(
        np.broadcast_to((np.arange(128, dtype=np.float32) - 127.0)[:, None],
                        (128, 64)).copy(), name="c_rampj")
    c_ident = nc.inline_tensor(np.eye(128, dtype=np.float32), name="c_ident")

    A = nc.alloc_sbuf_tensor
    XNT = A("XNT", [128, NCC * TILE], F32)
    XNB = A("XNB", [128, 8], F32)
    BUFA = A("BUFA", [128, 4096], F32)
    D = A("D", [128, NCC * TILE], F32)
    MIX0H = A("MIX0H", [128, NCC * TILE], BF16)
    MIX0L = A("MIX0L", [128, NCC * TILE], BF16)
    MIX1H = A("MIX1H", [128, NCC * TILE], BF16)
    MIX1L = A("MIX1L", [128, NCC * TILE], BF16)
    SETA = A("SETA", [128, 1024], F32)
    MIXT = SETA[:, 0:1024]
    KFM = A("KFM", [128, NCC * TILE], F32)
    VTM = A("VTM", [128, TPT * C], F32)
    RFM = A("RFM", [128, NCC * TILE], F32)
    ACCT = A("ACCT", [128, TPT * C], F32)
    KTIL = A("KTIL", [128, C], F32)
    SSCR = A("SSCR", [128, 512], F32)
    RS = A("RS", [128, C], F32)
    APR = A("APR", [128, H * 128], F32)
    WF32 = APR[:, 0:2048]
    _KTILB = KTIL.bitcast(BF16)
    _RSB = RS.bitcast(BF16)
    def _wsphl(k):
        buf = _KTILB if (k % 4) < 2 else _RSB
        base = (k % 2) * 1024
        return buf[:, base:base + 1024]
    S_ALL = A("S_ALL", [128, 512], F32)
    S_ALL2 = A("S_ALL2", [128, 512], F32)
    XIO = A("XIO", [128, 2 * C], F32)
    LNS = A("LNS", [128, 32], F32)
    GNS = A("GNS", [128, 320], F32)
    IDENT = A("IDENT", [128, 128], F32)
    GRIDT = SETA[:, 0:128]
    TRIT = SETA[:, 128:256]
    DIAGT = SETA[:, 256:384]
    RAMPI = SETA[:, 384:512]
    RAMPJ = SETA[:, 512:576]
    MWT = A("MWT", [128, H * 128], F32)
    MLAM = A("MLAM", [128, NCC * 128], F32)
    MK = A("MK", [128, C], F32)
    WSBC = A("WSBC", [128, 512], F32)
    MGAM = A("MGAM", [128, C], F32)
    MBET = A("MBET", [128, C], F32)
    DECB = A("DECB", [128, 16], F32)
    FAAB = A("FAAB", [128, 16], F32)
    LNW = A("LNW", [128, 16], F32)
    WS128 = A("WS128", [128, 16], F32)
    ONES64 = A("ONES64", [128, 64], F32)
    NEGHALF = A("NEGHALF", [128, 1], F32)
    INVC = A("INVC", [128, 1], F32)
    NEGINVC = A("NEGINVC", [128, 1], F32)
    NEGEPSC = A("NEGEPSC", [128, 1], F32)
    INVN = A("INVN", [128, 1], F32)
    NEGEPSN = A("NEGEPSN", [128, 1], F32)
    VEC = {m: A("t_" + m, [128, 8], F32) for m in
           ("maak", "maav", "maar", "maag", "mur", "muk")}
    OMR = A("OMR", [128, 8], F32)
    OMK = A("OMK", [128, 8], F32)
    MSCR = SETA[:, 576:704]
    WRINGB = A("WRINGB", [128, RING * 2 * C], BF16)

    PS = nc.alloc_psum_tensor("PS", [128, 8 * 512], F32)
    banks = Banks(PS)

    st = ExitStack()
    with st:
        s_t = Sem(st, nc, "s_t")
        s_v = Sem(st, nc, "s_v")
        s_s = Sem(st, nc, "s_s")
        s_cst = Sem(st, nc, "s_cst")
        s_x = [Sem(st, nc, f"s_x{i}") for i in range(2)]
        s_xr = [Sem(st, nc, f"s_xr{i}") for i in range(2)]
        s_w = [Sem(st, nc, f"s_w{i}") for i in range(RING)]
        s_gst = Sem(st, nc, "s_gst")
        s_wf = Sem(st, nc, "s_wf")
        s_wsp = Sem(st, nc, "s_wsp")

        S = {
            "xio_free": [None, None],
            "xio_loaded": [None, None],
            "x_cnt": [0, 0],
            "xr_cnt": [0, 0],
            "mix_free": {0: None, 1: None},
            "mix_ready": {0: None, 1: None},
            "mixt_free": None,
            "d_ready": None,
            "xnt_ready": None,
            "xnt_free": None,
            "bufa_free": None,
            "kfm_free": None,
            "vtm_free": None,
            "rfm_free": None,
            "acct_free": None,
            "supd": None,
            "yread": None,
            "setup_done": None,
        }

        def vbar(v, ins):
            ev = _inc(ins, s_v)
            _wt(v, ev)
            return ev

        _scnt = [0]

        def state_buf():
            return (S_ALL, S_ALL2)[_scnt[0] % 2]

        def state_dst():
            return (S_ALL, S_ALL2)[(_scnt[0] + 1) % 2]

        wring = {"n": 0, "scount": [0] * RING, "freed": [], "loadevt": []}

        def wring_slot_ap(i):
            s = i % RING
            return WRINGB[:, s * 2 * C:(s + 1) * 2 * C]

        def wring_load(sync, dram_ap, gate=None):
            i = wring["n"]
            s = i % RING
            if i >= RING:
                _wt(sync, wring["freed"][i - RING])
            _wt(sync, gate)
            ins = sync.dma_start(out=wring_slot_ap(i), in_=dram_ap)
            wring["scount"][s] += 1
            ins.then_inc(s_w[s].h, 16)
            s_w[s].v += 16
            wring["loadevt"].append((s_w[s], 16 * wring["scount"][s]))
            wring["freed"].append(None)
            wring["n"] += 1
            return i

        with nc.Block() as block:
            @block.sync
            def _(sync):
                for ap_, dst in ((c_ident.ap(), IDENT), (c_grid.ap(), GRIDT),
                                 (c_tri.ap(), TRIT), (c_diag.ap(), DIAGT),
                                 (c_rampi.ap(), RAMPI), (c_rampj.ap(), RAMPJ)):
                    _inc(sync.dma_start(out=dst[:, :], in_=ap_[:, :]), s_cst, 16)
                for m in VEC:
                    _inc(sync.dma_start(out=VEC[m][:, :], in_=vec_d[m][:, :]),
                         s_cst, 16)
                _inc(sync.dma_start(out=DECB[:, :],
                                    in_=dec_d.broadcast_to([128, 16])), s_cst, 16)
                _inc(sync.dma_start(out=FAAB[:, :],
                                    in_=faa_d.broadcast_to([128, 16])), s_cst, 16)
                _inc(sync.dma_start(out=MGAM[:, :],
                                    in_=lnw_d.broadcast_to([128, C])), s_cst, 16)
                _inc(sync.dma_start(out=MBET[:, :],
                                    in_=lnb_d.broadcast_to([128, C])), s_cst, 16)

            n_cst = s_cst.v

            @block.vector
            def _(v):
                v.wait_ge(s_cst.h, n_cst)
                v.memset(XNB[:, :], 0.0)
                v.memset(S_ALL[:, :], 0.0)
                v.memset(ONES64[:, :], 1.0)
                v.memset(NEGHALF[:, :], -0.5)
                v.memset(INVC[:, :], 1.0 / C)
                v.memset(NEGINVC[:, :], -1.0 / C)
                v.memset(NEGEPSC[:, :], -EPS_LN)
                v.memset(INVN[:, :], 1.0 / NDIM)
                v.memset(NEGEPSN[:, :], -EPS_GN)
                v.tensor_scalar(OMR[:, :], VEC["mur"][:, :], -1.0, 1.0,
                                OP.mult, OP.add)
                ev = _inc(v.tensor_scalar(OMK[:, :], VEC["muk"][:, :], -1.0, 1.0,
                                          OP.mult, OP.add), s_v)
                S["vsetup"] = ev

            @block.scalar
            def _(s):
                s.wait_ge(s_cst.h, n_cst)
                _wt(s, S["vsetup"])
                ev = _inc(s.activation(LNW[:, :], DECB[:, :], ACT.Exp), s_s)
                S["expd"] = ev

            @block.vector
            def _(v):
                _wt(v, S["expd"])
                ev = _inc(v.tensor_scalar_mul(WS128[:, :], LNW[:, :], -128.0),
                          s_v)
                S["ex128"] = ev

            @block.scalar
            def _(s):
                _wt(s, S["ex128"])
                s.activation(WS128[:, :], WS128[:, :], ACT.Exp)
                ev = None
                for h in range(H):
                    hcol = (h % 2) * 1024 + (h // 2) * 128
                    s.activation(MWT[:, hcol:hcol + 128], GRIDT[:, :],
                                 ACT.Exp, scale=LNW[:, h:h + 1])
                    s.activation(MK[:, h * 64:(h + 1) * 64], RAMPJ[:, :],
                                 ACT.Exp, scale=LNW[:, h:h + 1])
                    bp = (h % 2) * 64
                    s.activation(WSBC[bp:bp + 64,
                                      (h // 2) * 64:(h // 2) * 64 + 64],
                                 ONES64[0:64, :], ACT.Identity,
                                 scale=WS128[bp:bp + 64, h:h + 1])
                for cc in range(NCC):
                    s.activation(MLAM[0:64, cc * 128:(cc + 1) * 128],
                                 RAMPI[0:64, :], ACT.Exp,
                                 scale=LNW[0:64, 2 * cc:2 * cc + 1])
                    ev = s.activation(MLAM[64:128, cc * 128:(cc + 1) * 128],
                                      RAMPI[64:128, :], ACT.Exp,
                                      scale=LNW[64:128, 2 * cc + 1:2 * cc + 2])
                S["ssetup"] = _inc(ev, s_s)

            @block.vector
            def _(v):
                _wt(v, S["ssetup"])
                ev = None
                for h in range(H):
                    hcol = (h % 2) * 1024 + (h // 2) * 128
                    blk = MWT[:, hcol:hcol + 128]
                    v.tensor_scalar_mul(MSCR[:, :], DIAGT[:, :],
                                        FAAB[:, h:h + 1])
                    vbar(v, v.tensor_tensor(blk, blk, TRIT[:, :], OP.mult))
                    ev = v.tensor_tensor(blk, blk, MSCR[:, :], OP.add)
                S["setup_done"] = _inc(ev, s_v)

            SPLIT_ORDER = ("wk", "wv", "wr", "wg", "wo", "wfk", "wfr", "wfv")
            split_done = {w: [None] * NCC for w in SPLIT_ORDER}
            sp = {"next": 0, "ld": {}, "tt": {}, "st": {}}
            sp_list = [(w, cc, h) for w in SPLIT_ORDER for cc in range(NCC)
                       for h in range(2)]

            def sp_slot(k, buf, width):
                p = (k % 4) * width
                return buf[:, p:p + width]

            def emit_split_pair():
                k0 = sp["next"]
                if k0 >= len(sp_list):
                    return
                ks = [k for k in (k0, k0 + 1) if k < len(sp_list)]
                sp["next"] = k0 + len(ks)

                @block.sync
                def _(sync):
                    for k in ks:
                        w, cc, h = sp_list[k]
                        _wt(sync, sp["tt"].get(k - 4))
                        ins = sync.dma_start(
                            out=sp_slot(k, WF32, 512),
                            in_=wT[w][cc * 128:(cc + 1) * 128,
                                      h * 512:(h + 1) * 512])
                        sp["ld"][k] = _inc(ins, s_wf, 16)

                @block.vector
                def _(v):
                    cpev = {}
                    for k in ks:
                        _wt(v, sp["ld"][k])
                        _wt(v, sp["st"].get(k - 4))
                        hl = _wsphl(k)
                        cpev[k] = _inc(v.tensor_copy(hl[:, 0:512],
                                                     sp_slot(k, WF32, 512)),
                                       s_v)
                    for k in ks:
                        _wt(v, cpev[k])
                        hl = _wsphl(k)
                        ins = v.tensor_tensor(
                            hl[:, 512:1024],
                            sp_slot(k, WF32, 512),
                            hl[:, 0:512], OP.subtract)
                        sp["tt"][k] = _inc(ins, s_v)

                @block.scalar
                def _(s):
                    for k in ks:
                        w, cc, h = sp_list[k]
                        _wt(s, sp["tt"][k])
                        rows = slice(cc * 128, (cc + 1) * 128)
                        ins = s.dma_start(
                            out=wsp_d[w][rows, h * 1024:(h + 1) * 1024],
                            in_=_wsphl(k))
                        ev = _inc(ins, s_wsp, 16)
                        sp["st"][k] = ev
                        if h == 1:
                            split_done[w][cc] = ev

            def emit_one_split():
                emit_split_pair()

            def pump_splits(n=2):
                for _ in range(n):
                    emit_one_split()

            def ensure_splits(w):
                while split_done[w][NCC - 1] is None:
                    emit_one_split()

            def emit_ln_stats(kind, ti):
                row0 = (0 if kind == "warm" else te) + ti * TILE
                sqev = {}
                for tch in range(TPT):
                    pump_splits(1)
                    sl = tch % 2

                    @block.sync
                    def _(sync, tch=tch, sl=sl, row0=row0):
                        _wt(sync, S["xio_free"][sl])
                        ins = sync.dma_start(
                            out=XIO[:, sl * C:(sl + 1) * C],
                            in_=x_slab[row0 + tch * 128: row0 + (tch + 1) * 128, :])
                        ins.then_inc(s_x[sl].h, 16)
                        s_x[sl].v += 16
                        S["x_cnt"][sl] += 1
                        S["xio_loaded"][sl] = (s_x[sl], 16 * S["x_cnt"][sl])

                    ld_evt = S["xio_loaded"][sl]

                    @block.vector
                    def _(v, tch=tch, sl=sl, ld_evt=ld_evt, sqev=sqev):
                        _wt(v, ld_evt)
                        if tch == 0:
                            _wt(v, S["setup_done"])
                            _wt(v, S["bufa_free"])
                        xio = XIO[:, sl * C:(sl + 1) * C]
                        v.tensor_copy(BUFA[:, tch * C:(tch + 1) * C], xio)
                        v.tensor_reduce(LNS[:, tch:tch + 1], xio, AX.X, OP.add)
                        ev = _inc(v.tensor_tensor(
                            D[:, tch * C:(tch + 1) * C], xio, xio, OP.mult),
                            s_v)
                        sqev[tch] = ev
                        S["xio_free"][sl] = ev
                        if tch > 0:
                            _wt(v, sqev[tch - 1])
                            _inc(v.tensor_reduce(
                                LNS[:, 4 + tch - 1:4 + tch],
                                D[:, (tch - 1) * C:tch * C],
                                AX.X, OP.add), s_v)

                @block.vector
                def _(v, sqev=sqev):
                    _wt(v, sqev[TPT - 1])
                    _inc(v.tensor_reduce(
                        LNS[:, 4 + TPT - 1:4 + TPT],
                        D[:, (TPT - 1) * C:TPT * C],
                        AX.X, OP.add), s_v)

                red_evt = (s_v, s_v.v)

                @block.scalar
                def _(s, red_evt=red_evt):
                    _wt(s, red_evt)
                    s.activation(LNS[:, 8:12], LNS[:, 0:4], ACT.Identity,
                                 scale=INVC[:, :])
                    s.activation(LNS[:, 12:16], LNS[:, 0:4], ACT.Identity,
                                 scale=NEGINVC[:, :])
                    s.activation(LNS[:, 16:20], LNS[:, 8:12], ACT.Square)
                    _inc(s.activation(LNS[:, 16:20], LNS[:, 16:20],
                                      ACT.Identity, bias=NEGEPSC[:, :]), s_s)
                msq_evt = (s_s, s_s.v)

                @block.vector
                def _(v, msq_evt=msq_evt):
                    _wt(v, msq_evt)
                    _inc(v.scalar_tensor_tensor(
                        LNS[:, 20:24], LNS[:, 4:8], 1.0 / C,
                        LNS[:, 16:20], OP.mult, OP.subtract), s_v)
                var_evt = (s_v, s_v.v)

                @block.scalar
                def _(s, var_evt=var_evt):
                    _wt(s, var_evt)
                    s.activation(LNS[:, 24:28], LNS[:, 20:24], ACT.Ln)
                    _inc(s.activation(LNS[:, 28:32], LNS[:, 24:28], ACT.Exp,
                                      scale=NEGHALF[:, :]), s_s)
                    ev = None
                    for tch in range(TPT):
                        dst = BUFA[:, tch * C:(tch + 1) * C]
                        s.activation(dst, dst, ACT.Identity,
                                     bias=LNS[:, 12 + tch:13 + tch])
                        ev = s.activation(dst, dst, ACT.Identity,
                                          scale=LNS[:, 28 + tch:29 + tch])
                    _inc(ev, s_s)

                S["norm_evt"] = (s_s, s_s.v)

            def emit_ln_finish():
                norm_evt = S["norm_evt"]
                last_tr = None
                for cc in range(NCC):
                    bk = 6 + (cc % 2)

                    @block.tensor
                    def _(t, cc=cc, bk=bk, norm_evt=norm_evt):
                        _wt(t, norm_evt)
                        banks.acquire(t, bk)
                        for tch in range(TPT):
                            _inc(t.transpose(
                                banks.ap(bk, tch * 128, (tch + 1) * 128),
                                BUFA[:, tch * C + cc * 128:
                                     tch * C + (cc + 1) * 128],
                                IDENT[:, :]), s_t)
                    tr_evt = (s_t, s_t.v)
                    last_tr = tr_evt

                    @block.vector
                    def _(v, cc=cc, bk=bk, tr_evt=tr_evt):
                        _wt(v, tr_evt)
                        _wt(v, S["xnt_free"])
                        ev = _inc(v.tensor_copy(XNT[:, cc * TILE:(cc + 1) * TILE],
                                                banks.ap(bk)), s_v)
                        banks.release(bk, ev)

                S["xnt_ready"] = (s_v, s_v.v)
                S["xnt_free"] = None
                S["bufa_free"] = last_tr

                @block.vector
                def _(v):
                    _wt(v, S["xnt_ready"])
                    xv3 = XNT[:, :].rearrange("p (c t) -> p c t", t=TILE)
                    dv3 = D[:, :].rearrange("p (c t) -> p c t", t=TILE)
                    v.tensor_tensor(dv3[:, :, 0:1],
                                    XNB[:, :].rearrange("p (c o) -> p c o", o=1),
                                    xv3[:, :, 0:1], OP.subtract)
                    ev = _inc(v.tensor_tensor(dv3[:, :, 1:TILE],
                                              xv3[:, :, 0:TILE - 1],
                                              xv3[:, :, 1:TILE], OP.subtract),
                              s_v)
                    S["d_ready"] = ev

            def emit_mix(mslot, vtile):
                pump_splits(2)
                bh = MIX0H if mslot == 0 else MIX1H
                bl = MIX0L if mslot == 0 else MIX1L
                evs = {}

                def mslot_ap(cc):
                    p = (cc % 2) * 512
                    return MIXT[:, p:p + 512]

                @block.vector
                def _(v, vtile=vtile, mslot=mslot, bh=bh, bl=bl):
                    def stt(cc):
                        sl = slice(cc * TILE, (cc + 1) * TILE)
                        if cc == 0:
                            _wt(v, S["d_ready"])
                            _wt(v, S["mix_free"][mslot])
                            _wt(v, S["setup_done"])
                            _wt(v, S["mixt_free"])
                        _wt(v, evs.get(("lo", cc - 2)))
                        evs[("t", cc)] = _inc(v.scalar_tensor_tensor(
                            mslot_ap(cc), D[:, sl], vtile[:, cc:cc + 1],
                            XNT[:, sl], OP.mult, OP.add), s_v)

                    def cp(cc):
                        sl = slice(cc * TILE, (cc + 1) * TILE)
                        _wt(v, evs[("t", cc)])
                        evs[("hi", cc)] = _inc(
                            v.tensor_copy(bh[:, sl], mslot_ap(cc)), s_v)

                    def lo(cc):
                        sl = slice(cc * TILE, (cc + 1) * TILE)
                        _wt(v, evs[("hi", cc)])
                        evs[("lo", cc)] = _inc(
                            v.tensor_tensor(bl[:, sl], mslot_ap(cc),
                                            bh[:, sl], OP.subtract), s_v)

                    for k in range(0, NCC, 2):
                        stt(k)
                        stt(k + 1)
                        cp(k)
                        cp(k + 1)
                        lo(k)
                        lo(k + 1)
                    S["mixt_free"] = evs[("lo", NCC - 1)]
                    S["mix_ready"][mslot] = evs[("lo", NCC - 1)]

            def emit_proj_wstat(wname, mslot, dst, out_evts, dst_free_key,
                                evac_act=None):
                pump_splits(2)
                ensure_splits(wname)
                mh = MIX0H if mslot == 0 else MIX1H
                ml = MIX0L if mslot == 0 else MIX1L
                loads = [None] * NCC
                bank_evts = [None] * 8

                @block.sync
                def _(sync):
                    for cc in range(RING):
                        loads[cc] = wring_load(sync,
                                               wsp_d[wname][cc * 128:(cc + 1) * 128, :],
                                               gate=split_done[wname][cc])

                for cc in range(NCC):
                    @block.tensor
                    def _(t, cc=cc):
                        _wt(t, wring["loadevt"][loads[cc]])
                        if cc == 0:
                            _wt(t, S["mix_ready"][mslot])
                            for d in range(8):
                                banks.acquire(t, d)
                        slot = wring_slot_ap(loads[cc])
                        msl = slice(cc * TILE, (cc + 1) * TILE)
                        ins = None
                        for d in range(8):
                            base = (d // 4) * 1024 + (d % 4) * 128
                            w_h = slot[:, base:base + 128]
                            w_l = slot[:, base + 512:base + 640]
                            ins = t.matmul(banks.ap(d), w_h, mh[:, msl],
                                           start=(cc == 0), stop=False)
                            ins = t.matmul(banks.ap(d), w_h, ml[:, msl],
                                           start=False, stop=False)
                            ins = t.matmul(banks.ap(d), w_l, mh[:, msl],
                                           start=False, stop=(cc == NCC - 1))
                            if cc == NCC - 1:
                                bank_evts[d] = _inc(ins, s_t)
                        if cc < NCC - 1:
                            wring["freed"][loads[cc]] = _inc(ins, s_t)
                        else:
                            wring["freed"][loads[cc]] = bank_evts[7]
                            S["mix_free"][mslot] = bank_evts[7]

                    if cc + RING < NCC:
                        @block.sync
                        def _(sync, cc=cc):
                            loads[cc + RING] = wring_load(
                                sync,
                                wsp_d[wname][(cc + RING) * 128:(cc + RING + 1) * 128, :],
                                gate=split_done[wname][cc + RING])

                dst_free = S[dst_free_key]
                S[dst_free_key] = None
                for d in range(8):
                    if evac_act is not None or d % 2 == 0:
                        @block.scalar
                        def _(s, d=d):
                            _wt(s, bank_evts[d])
                            _wt(s, dst_free)
                            ev = _inc(s.activation(
                                dst[:, d * TILE:(d + 1) * TILE], banks.ap(d),
                                evac_act if evac_act is not None
                                else ACT.Identity), s_s)
                            banks.release(d, ev)
                            out_evts[d] = ev
                    else:
                        @block.vector
                        def _(v, d=d):
                            _wt(v, bank_evts[d])
                            _wt(v, dst_free)
                            ev = _inc(v.tensor_copy(
                                dst[:, d * TILE:(d + 1) * TILE], banks.ap(d)),
                                s_v)
                            banks.release(d, ev)
                            out_evts[d] = ev

            def emit_proj_xstat(wname, stats, stat_ready_fn, evac_fn,
                                mix_free_key=None):
                pump_splits(2)
                ensure_splits(wname)
                sh_buf, sl_buf = stats
                loads = [None] * NCC
                bank_evts = [None] * 8

                @block.sync
                def _(sync):
                    for cc in range(RING):
                        loads[cc] = wring_load(sync,
                                               wsp_d[wname][cc * 128:(cc + 1) * 128, :],
                                               gate=split_done[wname][cc])

                for cc in range(NCC):
                    @block.tensor
                    def _(t, cc=cc):
                        _wt(t, wring["loadevt"][loads[cc]])
                        if cc == 0:
                            _wt(t, stat_ready_fn())
                            for b in range(8):
                                banks.acquire(t, b)
                        slot = wring_slot_ap(loads[cc])
                        ins = None
                        for tch in range(TPT):
                            ssl = slice(cc * TILE + tch * 128,
                                        cc * TILE + (tch + 1) * 128)
                            for dh in range(2):
                                b = 2 * tch + dh
                                movh = slot[:, dh * 1024:dh * 1024 + 512]
                                movl = slot[:, dh * 1024 + 512:
                                            (dh + 1) * 1024]
                                ins = t.matmul(banks.ap(b), sh_buf[:, ssl],
                                               movh, start=(cc == 0),
                                               stop=False)
                                ins = t.matmul(banks.ap(b), sh_buf[:, ssl],
                                               movl, start=False, stop=False)
                                ins = t.matmul(banks.ap(b), sl_buf[:, ssl],
                                               movh, start=False,
                                               stop=(cc == NCC - 1))
                                if cc == NCC - 1:
                                    bank_evts[b] = _inc(ins, s_t)
                        if cc < NCC - 1:
                            wring["freed"][loads[cc]] = _inc(ins, s_t)
                        else:
                            wring["freed"][loads[cc]] = bank_evts[7]
                            if mix_free_key is not None:
                                S["mix_free"][mix_free_key] = bank_evts[7]

                    if cc + RING < NCC:
                        @block.sync
                        def _(sync, cc=cc):
                            loads[cc + RING] = wring_load(
                                sync,
                                wsp_d[wname][(cc + RING) * 128:(cc + RING + 1) * 128, :],
                                gate=split_done[wname][cc + RING])

                for tch in range(TPT):
                    for dh in range(2):
                        evac_fn(2 * tch + dh, tch, dh, bank_evts[2 * tch + dh])

            def evac_simple(dst, dst_free_key=None, mask=None, act=None,
                            evts_out=None):
                holder = {"df": S[dst_free_key] if dst_free_key else None}
                if dst_free_key:
                    S[dst_free_key] = None

                def f(b, tch, dh, evt):
                    sl = slice(tch * C + dh * 512, tch * C + (dh + 1) * 512)
                    if mask is not None:
                        @block.vector
                        def _(v, b=b, sl=sl, dh=dh, evt=evt):
                            _wt(v, evt)
                            _wt(v, holder["df"])
                            ev = _inc(v.tensor_tensor(
                                dst[:, sl], banks.ap(b),
                                mask[:, dh * 512:(dh + 1) * 512], OP.mult), s_v)
                            banks.release(b, ev)
                            if evts_out is not None:
                                evts_out[b] = ev
                    elif act is not None:
                        @block.scalar
                        def _(s, b=b, sl=sl, evt=evt):
                            _wt(s, evt)
                            _wt(s, holder["df"])
                            ev = _inc(s.activation(dst[:, sl], banks.ap(b), act),
                                      s_s)
                            banks.release(b, ev)
                            if evts_out is not None:
                                evts_out[b] = ev
                    else:
                        if b % 2 == 0:
                            @block.scalar
                            def _(s, b=b, sl=sl, evt=evt):
                                _wt(s, evt)
                                _wt(s, holder["df"])
                                ev = _inc(s.activation(dst[:, sl],
                                                       banks.ap(b),
                                                       ACT.Identity), s_s)
                                banks.release(b, ev)
                                if evts_out is not None:
                                    evts_out[b] = ev
                        else:
                            @block.vector
                            def _(v, b=b, sl=sl, evt=evt):
                                _wt(v, evt)
                                _wt(v, holder["df"])
                                ev = _inc(v.tensor_copy(dst[:, sl],
                                                        banks.ap(b)), s_v)
                                banks.release(b, ev)
                                if evts_out is not None:
                                    evts_out[b] = ev
                return f

            def rf_sl(b):
                if b < 2:
                    return KTIL[:, b * 512:(b + 1) * 512]
                if b < 4:
                    return RS[:, (b - 2) * 512:(b - 1) * 512]
                return APR[:, (b - 4) * 512:(b - 3) * 512]

            def emit_save_xnb():
                @block.vector
                def _(v):
                    xl = XNT[:, :].rearrange("p (c t) -> p c t", t=TILE)
                    ev = _inc(v.tensor_copy(
                        XNB[:, :].rearrange("p (c o) -> p c o", o=1),
                        xl[:, :, TILE - 1:TILE]), s_v)
                    S["xnt_free"] = ev

            emit_ln_stats("warm", 0)
            emit_ln_finish()
            for wt_i in range(nt):
                if not go():
                    break
                if not go():
                    break
                if wt_i == 0:
                    emit_mix(0, VEC["maak"])
                emit_mix(1, VEC["maav"])
                if wt_i + 1 < nt:
                    emit_ln_stats("warm", wt_i + 1)
                else:
                    emit_ln_stats("emit", 0)

                kevts = [None] * 8
                if not go():
                    break
                emit_proj_xstat("wk", (MIX0H, MIX0L), lambda: S["mix_ready"][0],
                                evac_simple(KFM, "kfm_free", mask=MK,
                                            evts_out=kevts), mix_free_key=0)
                vevts = [None] * 8
                if not go():
                    break
                emit_proj_xstat("wv", (MIX1H, MIX1L), lambda: S["mix_ready"][1],
                                evac_simple(VTM, "vtm_free", evts_out=vevts),
                                mix_free_key=1)
                pump_splits(12)
                if not go():
                    break

                for ch in range(TPT):
                    ube = (ch % 2) * 2
                    ubo = ube + 1

                    @block.tensor
                    def _(t, ch=ch, ube=ube, ubo=ubo):
                        _wt(t, kevts[2 * ch])
                        _wt(t, kevts[2 * ch + 1])
                        _wt(t, vevts[2 * ch])
                        _wt(t, vevts[2 * ch + 1])
                        banks.acquire(t, ube)
                        banks.acquire(t, ubo)
                        for par, ub in ((0, ube), (1, ubo)):
                            bp = par * 64
                            ins = None
                            for hh in range(8):
                                h = 2 * hh + par
                                ins = t.matmul(
                                    PS[bp:bp + 64,
                                       ub * 512 + hh * 64:
                                       ub * 512 + hh * 64 + 64],
                                    KFM[:, ch * C + h * 64: ch * C + (h + 1) * 64],
                                    VTM[:, ch * C + h * 64: ch * C + (h + 1) * 64],
                                    start=True, stop=True,
                                    tile_position=(0, bp))
                            S[f"u_mm{par}"] = _inc(ins, s_t)

                    @block.vector
                    def _(v, ube=ube, ubo=ubo):
                        _wt(v, S["u_mm0"])
                        _wt(v, S["u_mm1"])
                        sa, sd = state_buf(), state_dst()
                        _scnt[0] += 1
                        vbar(v, v.tensor_tensor(SSCR[:, :], sa[:, :],
                                                WSBC[:, :], OP.mult))
                        v.tensor_tensor(sd[0:64, :], SSCR[0:64, :],
                                        banks.ap(ube)[0:64, :], OP.add)
                        ev = _inc(v.tensor_tensor(sd[64:128, :],
                                                  SSCR[64:128, :],
                                                  banks.ap(ubo)[64:128, :],
                                                  OP.add), s_v)
                        S["supd"] = ev
                        banks.release(ube, ev)
                        banks.release(ubo, ev)

                S["kfm_free"] = S["u_mm1"]
                S["vtm_free"] = S["u_mm1"]
                pump_splits(12)
                emit_save_xnb()
                emit_ln_finish()
                emit_mix(0, VEC["maak"])

            ensure_splits("wfv")
            S["split_fin"] = (s_wsp, s_wsp.v)

            for ti in range(nt):
                if not go():
                    break
                if not go():
                    break
                emit_mix(1, VEC["maav"])

                kevts = [None] * 8
                if not go():
                    break
                emit_proj_wstat("wk", 0, KFM, kevts, "kfm_free")
                vevts = [None] * 8
                if not go():
                    break
                emit_proj_xstat("wv", (MIX1H, MIX1L), lambda: S["mix_ready"][1],
                                evac_simple(VTM, "vtm_free", evts_out=vevts),
                                mix_free_key=1)
                emit_mix(0, VEC["maar"])
                revts = [None] * 8
                if not go():
                    break
                emit_proj_wstat("wr", 0, RFM, revts, "rfm_free")
                if not go():
                    break

                yn_evts = [None] * TPT

                def emit_scan_prep(ch):
                    @block.tensor
                    def _(t, ch=ch):
                        for e in kevts:
                            _wt(t, e)
                        banks.acquire(t, 6)
                        banks.acquire(t, 7)
                        ins = None
                        for cc in range(NCC):
                            bk = 6 + cc // 4
                            ins = t.transpose(
                                banks.ap(bk, (cc % 4) * 128, (cc % 4 + 1) * 128),
                                KFM[:, cc * TILE + ch * 128:
                                    cc * TILE + (ch + 1) * 128],
                                IDENT[:, :])
                            _inc(ins, s_t)
                        S["ktr"] = (s_t, s_t.v)

                    @block.vector
                    def _(v, ch=ch, ti=ti):
                        _wt(v, S["ktr"])
                        if ti == 0 and ch == 0:
                            _wt(v, S["split_fin"])
                        v.tensor_tensor(KTIL[:, 0:512], banks.ap(6),
                                        MK[:, 0:512], OP.mult)
                        ev = _inc(v.tensor_tensor(KTIL[:, 512:1024],
                                                  banks.ap(7), MK[:, 512:1024],
                                                  OP.mult), s_v)
                        banks.release(6, ev)
                        banks.release(7, ev)
                        rsv = RS[:, :].rearrange("p (c t) -> p c t", t=128)
                        rfm3 = RFM[:, :].rearrange("p (c t) -> p c t", t=TILE)
                        for e in revts:
                            _wt(v, e)
                        ev = _inc(v.tensor_tensor(
                            rsv, rfm3[:, :, ch * 128:(ch + 1) * 128],
                            MLAM[:, :].rearrange("p (c t) -> p c t", t=128),
                            OP.mult), s_v)
                        S["rs"] = ev

                    @block.tensor
                    def _(t, ch=ch):
                        for e in revts:
                            _wt(t, e)
                        for b in range(4):
                            banks.acquire(t, b)
                        ins = None
                        for h in range(H):
                            cc = h // 2
                            bp = (h % 2) * 64
                            hcol = (h % 2) * 1024 + (h // 2) * 128
                            tok = slice(cc * TILE + ch * 128,
                                        cc * TILE + (ch + 1) * 128)
                            ins = t.matmul(
                                banks.span(0, 4)[:, hcol:hcol + 128],
                                KFM[bp:bp + 64, tok],
                                RFM[bp:bp + 64, tok],
                                start=True, stop=True,
                                tile_position=(bp, 0))
                        S["amm"] = _inc(ins, s_t)

                    @block.vector
                    def _(v, ch=ch, ti=ti):
                        _wt(v, S["amm"])
                        if ti == 0 and ch == 0:
                            _wt(v, S["split_fin"])
                        v.tensor_tensor(APR[:, 0:1024], banks.span(0, 2),
                                        MWT[:, 0:1024], OP.mult)
                        ev = _inc(v.tensor_tensor(APR[:, 1024:2048],
                                                  banks.span(2, 2),
                                                  MWT[:, 1024:2048], OP.mult),
                                  s_v)
                        for b in range(4):
                            banks.release(b, ev)
                        S["apr"] = ev

                emit_scan_prep(0)
                for ch in range(TPT):
                    if not go():
                        break

                    @block.tensor
                    def _(t, ch=ch):
                        _wt(t, S["apr"])
                        _wt(t, S["supd"])
                        _wt(t, S["rs"])
                        banks.acquire(t, 4)
                        banks.acquire(t, 5)
                        sa = state_buf()
                        ins = None
                        for h in range(H):
                            bp = (h % 2) * 64
                            cc = h // 2
                            hcol = (h % 2) * 1024 + (h // 2) * 128
                            ysl = banks.span(4, 2)[:, h * 64:(h + 1) * 64]
                            ins = t.matmul(ysl, APR[:, hcol:hcol + 128],
                                           VTM[:, ch * C + h * 64:
                                               ch * C + (h + 1) * 64],
                                           start=True, stop=False)
                            ins = t.matmul(
                                ysl,
                                RS[bp:bp + 64, cc * 128:(cc + 1) * 128],
                                sa[bp:bp + 64,
                                   (h // 2) * 64:(h // 2) * 64 + 64],
                                start=False, stop=True,
                                tile_position=(bp, 0))
                        S["yread"] = _inc(ins, s_t)

                    @block.tensor
                    def _(t, ch=ch):
                        banks.acquire(t, 2)
                        banks.acquire(t, 3)
                        for par, ub in ((0, 2), (1, 3)):
                            bp = par * 64
                            ins = None
                            for hh in range(8):
                                h = 2 * hh + par
                                ins = t.matmul(
                                    PS[bp:bp + 64,
                                       ub * 512 + hh * 64:
                                       ub * 512 + hh * 64 + 64],
                                    KTIL[:, h * 64:(h + 1) * 64],
                                    VTM[:, ch * C + h * 64: ch * C + (h + 1) * 64],
                                    start=True, stop=True,
                                    tile_position=(0, bp))
                            S[f"u_mm{par}"] = _inc(ins, s_t)

                    @block.vector
                    def _(v, ch=ch):
                        _wt(v, S["u_mm0"])
                        _wt(v, S["u_mm1"])
                        sa, sd = state_buf(), state_dst()
                        _scnt[0] += 1
                        vbar(v, v.tensor_tensor(SSCR[:, :], sa[:, :],
                                                WSBC[:, :], OP.mult))
                        v.tensor_tensor(sd[0:64, :], SSCR[0:64, :],
                                        banks.ap(2)[0:64, :], OP.add)
                        ev = _inc(v.tensor_tensor(sd[64:128, :],
                                                  SSCR[64:128, :],
                                                  banks.ap(3)[64:128, :],
                                                  OP.add), s_v)
                        S["supd"] = ev
                        banks.release(2, ev)
                        banks.release(3, ev)

                    @block.vector
                    def _(v, ch=ch):
                        _wt(v, S["yread"])
                        ev = _inc(v.tensor_copy(BUFA[:, ch * C:(ch + 1) * C],
                                                banks.span(4, 2)), s_v)
                        banks.release(4, ev)
                        banks.release(5, ev)
                        yn_evts[ch] = ev

                    if ch + 1 < TPT:
                        emit_scan_prep(ch + 1)

                if not go():
                    break
                emit_mix(1, VEC["maag"])

                @block.vector
                def _(v):
                    for e in yn_evts:
                        _wt(v, e)
                    y3 = BUFA[:, :].rearrange("p (g n) -> p g n", n=NDIM)
                    _inc(v.tensor_reduce(GNS[:, 0:64], y3, AX.X, OP.add), s_v)
                mean_evt = (s_v, s_v.v)

                @block.scalar
                def _(s, mean_evt=mean_evt):
                    _wt(s, mean_evt)
                    _wt(s, S["acct_free"])
                    S["acct_free"] = None
                    _inc(s.activation(ACCT[:, :], BUFA[:, :], ACT.Square), s_s)
                sq_evt = (s_s, s_s.v)

                @block.vector
                def _(v, sq_evt=sq_evt):
                    _wt(v, sq_evt)
                    sq3 = ACCT[:, :].rearrange("p (g n) -> p g n", n=NDIM)
                    _inc(v.tensor_reduce(GNS[:, 64:128], sq3, AX.X, OP.add),
                         s_v)
                red2_evt = (s_v, s_v.v)

                @block.scalar
                def _(s, red2_evt=red2_evt):
                    _wt(s, red2_evt)
                    s.activation(GNS[:, 128:192], GNS[:, 0:64], ACT.Identity,
                                 scale=INVN[:, :])
                    s.activation(GNS[:, 0:64], GNS[:, 128:192], ACT.Square)
                    s.activation(GNS[:, 0:64], GNS[:, 0:64], ACT.Identity,
                                 bias=NEGEPSN[:, :])
                    _inc(s.activation(GNS[:, 64:128], GNS[:, 64:128],
                                      ACT.Identity), s_s)
                gst_evt = (s_s, s_s.v)

                @block.vector
                def _(v, gst_evt=gst_evt):
                    _wt(v, gst_evt)
                    _inc(v.scalar_tensor_tensor(
                        GNS[:, 192:256], GNS[:, 64:128], 1.0 / NDIM,
                        GNS[:, 0:64], OP.mult, OP.subtract), s_v)
                var_evt = (s_v, s_v.v)

                @block.scalar
                def _(s, var_evt=var_evt):
                    _wt(s, var_evt)
                    s.activation(GNS[:, 64:128], GNS[:, 192:256], ACT.Ln)
                    _inc(s.activation(GNS[:, 256:320], GNS[:, 64:128],
                                      ACT.Exp, scale=NEGHALF[:, :]), s_s)
                sd_evt = (s_s, s_s.v)

                @block.vector
                def _(v, sd_evt=sd_evt):
                    _wt(v, sd_evt)
                    mb = GNS[:, 128:192].rearrange(
                        "p (g o) -> p g o", o=1).broadcast_to([128, 64, NDIM])
                    rb = GNS[:, 256:320].rearrange(
                        "p (g o) -> p g o", o=1).broadcast_to([128, 64, NDIM])
                    y3 = BUFA[:, :].rearrange("p (g n) -> p g n", n=NDIM)
                    vbar(v, v.tensor_tensor(y3, y3, mb, OP.subtract))
                    vbar(v, v.tensor_tensor(y3, y3, rb, OP.mult))
                    yc = BUFA[:, :].rearrange("p (a c) -> p a c", c=C)
                    gg = MGAM[:, :].rearrange(
                        "p (a c) -> p a c", a=1).broadcast_to([128, TPT, C])
                    bb = MBET[:, :].rearrange(
                        "p (a c) -> p a c", a=1).broadcast_to([128, TPT, C])
                    vbar(v, v.tensor_tensor(yc, yc, gg, OP.mult))
                    ev = _inc(v.tensor_tensor(yc, yc, bb, OP.add), s_v)
                    S["yn_all"] = ev

                gevts = [None] * 8
                emit_proj_xstat("wg", (MIX1H, MIX1L),
                                lambda: S["mix_ready"][1],
                                evac_simple(KFM, "kfm_free", act=ACT.Gelu,
                                            evts_out=gevts), mix_free_key=1)

                if not go():
                    break
                @block.vector
                def _(v):
                    for e in gevts:
                        _wt(v, e)
                    _wt(v, S["yn_all"])
                    _wt(v, S["vtm_free"])
                    ev = _inc(v.tensor_tensor(VTM[:, :], BUFA[:, :], KFM[:, :],
                                              OP.mult), s_v)
                    S["z_tm"] = ev
                    S["bufa_free"] = ev
                    S["vtm_free"] = None

                zf_free = S["mix_free"][0]
                S["mix_free"][0] = None
                for p in range(0, NCC, 2):
                    tr_evt = {}
                    for cc in (p, p + 1):
                        bk = 6 + (cc % 2)

                        @block.tensor
                        def _(t, cc=cc, bk=bk):
                            _wt(t, S["z_tm"])
                            banks.acquire(t, bk)
                            for tch in range(TPT):
                                _inc(t.transpose(
                                    banks.ap(bk, tch * 128, (tch + 1) * 128),
                                    VTM[:, tch * C + cc * 128:
                                        tch * C + (cc + 1) * 128],
                                    IDENT[:, :]), s_t)
                        tr_evt[cc] = (s_t, s_t.v)

                    @block.vector
                    def _(v, p=p, tr_evt=tr_evt, zf_free=zf_free):
                        cpev = {}
                        for cc in (p, p + 1):
                            _wt(v, tr_evt[cc])
                            if cc == 0:
                                _wt(v, zf_free)
                            cpev[cc] = _inc(v.tensor_copy(
                                MIX0H[:, cc * TILE:(cc + 1) * TILE],
                                banks.ap(6 + (cc % 2))), s_v)
                        for cc in (p, p + 1):
                            _wt(v, cpev[cc])
                            ev = _inc(v.tensor_tensor(
                                MIX0L[:, cc * TILE:(cc + 1) * TILE],
                                banks.ap(6 + (cc % 2)),
                                MIX0H[:, cc * TILE:(cc + 1) * TILE],
                                OP.subtract), s_v)
                            banks.release(6 + (cc % 2), ev)
                            if cc == NCC - 1:
                                S["zf"] = ev

                if not go():
                    break
                emit_mix(1, OMR)
                xres_ld = [None] * TPT

                def emit_xres_load(tch, ti=ti):
                    sl = tch % 2

                    @block.gpsimd
                    def _(g, tch=tch, sl=sl, ti=ti):
                        _wt(g, S["xio_free"][sl])
                        row0 = te + ti * TILE + tch * 128
                        ins = g.dma_start(out=XIO[:, sl * C:(sl + 1) * C],
                                          in_=x_slab[row0:row0 + 128, :])
                        ins.then_inc(s_xr[sl].h, 16)
                        s_xr[sl].v += 16
                        S["xr_cnt"][sl] += 1
                        xres_ld[tch] = (s_xr[sl], 16 * S["xr_cnt"][sl])

                emit_xres_load(0)
                emit_xres_load(1)

                oevts = [None] * 8

                def o_evac(b, tch, dh, evt):
                    sl = tch % 2

                    @block.vector
                    def _(v, b=b, tch=tch, dh=dh, evt=evt, sl=sl):
                        _wt(v, evt)
                        _wt(v, xres_ld[tch])
                        _wt(v, S["acct_free"])
                        S["acct_free"] = None
                        ev = _inc(v.tensor_tensor(
                            ACCT[:, tch * C + dh * 512:
                                 tch * C + (dh + 1) * 512],
                            banks.ap(b),
                            XIO[:, sl * C + dh * 512: sl * C + (dh + 1) * 512],
                            OP.add), s_v)
                        banks.release(b, ev)
                        oevts[b] = ev
                        if dh == 1:
                            S["xio_free"][sl] = ev
                    if dh == 1 and tch + 2 < TPT:
                        emit_xres_load(tch + 2)

                emit_proj_xstat("wo", (MIX0H, MIX0L), lambda: S["zf"], o_evac,
                                mix_free_key=0)

                if not go():
                    break
                emit_mix(0, OMK)
                emit_save_xnb()
                if ti + 1 < nt:
                    emit_ln_stats("emit", ti + 1)

                frevts = [None] * 8

                def fr_evac(b, tch, dh, evt):
                    @block.scalar
                    def _(s, b=b, evt=evt):
                        _wt(s, evt)
                        ev = _inc(s.activation(rf_sl(b), banks.ap(b),
                                               ACT.Sigmoid), s_s)
                        banks.release(b, ev)
                        frevts[b] = ev

                emit_proj_xstat("wfr", (MIX1H, MIX1L),
                                lambda: S["mix_ready"][1], fr_evac,
                                mix_free_key=1)

                if not go():
                    break
                fkevts = [None] * 8
                emit_proj_wstat("wfk", 0, KFM, fkevts, "kfm_free",
                                evac_act=ACT.Relu)
                if ti + 1 < nt:
                    emit_ln_finish()

                if not go():
                    break

                @block.vector
                def _(v):
                    for e in fkevts:
                        _wt(v, e)
                    ev = _inc(v.tensor_tensor(KFM[:, :], KFM[:, :], KFM[:, :],
                                              OP.mult), s_v)
                    _wt(v, ev)
                    _wt(v, S["mix_free"][1])
                    vbar(v, v.tensor_copy(MIX1H[:, :], KFM[:, :]))
                    ev = vbar(v, v.tensor_tensor(MIX1L[:, :], KFM[:, :],
                                                 MIX1H[:, :], OP.subtract))
                    S["kact"] = ev
                    S["kfm_free"] = ev

                if ti + 1 < nt:
                    emit_mix(0, VEC["maak"])

                out_evts = [None] * 8

                def fv_evac(b, tch, dh, evt):
                    @block.vector
                    def _(v, b=b, tch=tch, dh=dh, evt=evt):
                        _wt(v, evt)
                        _wt(v, frevts[b])
                        _wt(v, oevts[b])
                        sl = slice(tch * C + dh * 512, tch * C + (dh + 1) * 512)
                        vbar(v, v.tensor_tensor(VTM[:, sl], banks.ap(b),
                                                rf_sl(b), OP.mult))
                        ev = _inc(v.tensor_tensor(VTM[:, sl], VTM[:, sl],
                                                  ACCT[:, sl], OP.add), s_v)
                        banks.release(b, ev)
                        out_evts[b] = ev

                emit_proj_xstat("wfv", (MIX1H, MIX1L), lambda: S["kact"],
                                fv_evac, mix_free_key=1)

                for tch in range(TPT):
                    @block.gpsimd
                    def _(g, tch=tch, ti=ti):
                        _wt(g, out_evts[2 * tch])
                        _wt(g, out_evts[2 * tch + 1])
                        row0 = ti * TILE + tch * 128
                        ins = g.dma_start(out=out_d[row0:row0 + 128, :],
                                          in_=VTM[:, tch * C:(tch + 1) * C])
                        ins.then_inc(s_gst.h, 16)
                        s_gst.v += 16

                S["acct_free"] = out_evts[7]
                S["rfm_free"] = S["rs"]
                S["vtm_free"] = (s_gst, s_gst.v)

            _BUFS = {"XNT": XNT, "BUFA": BUFA, "D": D,
                     "KFM": KFM, "VTM": VTM, "RFM": RFM,
                     "ACCT": ACCT, "KTIL": KTIL, "RS": RS, "APR": APR,
                     "S_ALL": S_ALL, "MWT": MWT, "MLAM": MLAM, "MK": MK,
                     "WSBC": WSBC, "MGAM": MGAM, "XNB": XNB, "LNW": LNW, "LNS": LNS, "XIO": XIO,
                     "WS128": WS128}
            _BF16S = {"MIX0S": (MIX0H, MIX0L), "MIX1S": (MIX1H, MIX1L),
                      "MIX0HH": (MIX0H, None), "MIX0LL": (MIX0L, None),
                      "MIX1HH": (MIX1H, None), "MIX1LL": (MIX1L, None)}
            if DBG_TAP in _BF16S:
                bh_, bl_ = _BF16S[DBG_TAP]

                @block.scalar
                def _(s):
                    _wt(s, (s_v, s_v.v))
                    _wt(s, (s_t, s_t.v))
                    _inc(s.activation(ACCT[:, :], bh_[:, :], ACT.Identity), s_s)

                if bl_ is not None:
                    @block.vector
                    def _(v):
                        _wt(v, (s_s, s_s.v))
                        _inc(v.tensor_tensor(ACCT[:, :], ACCT[:, :], bl_[:, :],
                                             OP.add), s_v)
                _BUFS[DBG_TAP] = ACCT
            if DBG_TAP is not None or s_gst.v == 0:
                @block.gpsimd
                def _(g):
                    _wt(g, (s_v, s_v.v))
                    _wt(g, (s_s, s_s.v))
                    _wt(g, (s_t, s_t.v))
                    if DBG_TAP is None:
                        ins = g.dma_start(out=out_d[0:128, 0:8], in_=XNB[:, :])
                        ins.then_inc(s_gst.h, 16)
                        s_gst.v += 16
                    else:
                        buf = _BUFS[DBG_TAP]
                        ncol = buf.shape[1]
                        nrow = min((ncol + C - 1) // C, te // 128)
                        for r in range(nrow):
                            w = min(C, ncol - r * C)
                            ins = g.dma_start(
                                out=out_d[r * 128:(r + 1) * 128, 0:w],
                                in_=buf[:, r * C:r * C + w])
                            ins.then_inc(s_gst.h, 16)
                            s_gst.v += 16

            total_st = s_gst.v

            @block.gpsimd
            def _(g):
                g.wait_ge(s_gst.h, total_st)

    return nc


def kernel(**inputs):
    ins = {k: np.asarray(v, dtype=np.float32) for k, v in inputs.items()}
    x = ins["L0_attn_x"]
    B, T, _ = x.shape

    def vec_tile(v):
        return np.ascontiguousarray(v.reshape(8, 128).T)

    common = {
        "wk": np.ascontiguousarray(ins["L0_attn_Wk"].T),
        "wv": np.ascontiguousarray(ins["L0_attn_Wv"].T),
        "wr": np.ascontiguousarray(ins["L0_attn_Wr"].T),
        "wg": np.ascontiguousarray(ins["L0_attn_Wg"].T),
        "wo": np.ascontiguousarray(ins["L0_attn_Wo"].T),
        "wfr": np.ascontiguousarray(ins["L0_ffn_W_r"].T),
        "wfk": np.ascontiguousarray(ins["L0_ffn_W_k"].T),
        "wfv": np.ascontiguousarray(ins["L0_ffn_W_v"].T),
        "maak": vec_tile(ins["L0_attn_time_maa_k"].reshape(-1)),
        "maav": vec_tile(ins["L0_attn_time_maa_v"].reshape(-1)),
        "maar": vec_tile(ins["L0_attn_time_maa_r"].reshape(-1)),
        "maag": vec_tile(ins["L0_attn_time_maa_g"].reshape(-1)),
        "mur": vec_tile(ins["L0_ffn_mu_r"].reshape(-1)),
        "muk": vec_tile(ins["L0_ffn_mu_k"].reshape(-1)),
        "dec_row": np.ascontiguousarray(ins["L0_attn_time_decay"].reshape(1, 16)),
        "faa_row": np.ascontiguousarray(ins["L0_attn_time_faaaa"].reshape(1, 16)),
        "lnw_row": np.ascontiguousarray(ins["L0_attn_ln_w"].reshape(1, C)),
        "lnb_row": np.ascontiguousarray(ins["L0_attn_ln_b"].reshape(1, C)),
    }

    in_maps = []
    for core in range(8):
        b, half = core // 2, core % 2
        if half == 0:
            slab = np.concatenate(
                [np.zeros((TE, C), np.float32), x[b, 0:TE]], axis=0)
        else:
            slab = np.ascontiguousarray(x[b, 0:2 * TE])
        m = dict(common)
        m["x_slab"] = slab
        in_maps.append(m)

    nc = build_nc(NT)
    res = run_bass_kernel_spmd(nc, in_maps, core_ids=list(range(8)))

    out = np.empty((B, T, C), np.float32)
    for core in range(8):
        b, half = core // 2, core % 2
        out[b, half * TE:(half + 1) * TE] = res.results[core]["out"]
    return out


# version 21
# speedup vs baseline: 1.0289x; 1.0053x over previous
import numpy as np
from contextlib import ExitStack

import concourse.bass as bass
import concourse.mybir as mybir
from concourse.bass_utils import run_bass_kernel_spmd

F32 = mybir.dt.float32
BF16 = mybir.dt.bfloat16
OP = mybir.AluOpType
ACT = mybir.ActivationFunctionType
AX = mybir.AxisListType

C = 1024
H = 16
NDIM = 64
TILE = 512
NCC = 8
TPT = 4
NT = 4
TE = 2048
EPS_LN = 1e-5
EPS_GN = 1e-5 * 64
RING = 2
STOP = None
DBG_Y = "both"
DBG_TAP = None


class Sem:
    def __init__(self, st, nc, name):
        self.h = st.enter_context(nc.semaphore(name))
        self.v = 0

    def tick(self, n=1):
        self.v += n
        return (self, self.v)


def _inc(instr, sem, n=1):
    instr.then_inc(sem.h, n)
    return sem.tick(n)


def _wt(eng, evt):
    if evt is not None:
        eng.wait_ge(evt[0].h, evt[1])


class Banks:
    def __init__(self, ps):
        self.ps = ps
        self.rel = [None] * 8

    def ap(self, b, lo=0, hi=512):
        return self.ps[:, b * 512 + lo: b * 512 + hi]

    def span(self, b0, nbank):
        return self.ps[:, b0 * 512:(b0 + nbank) * 512]

    def acquire(self, eng, b):
        _wt(eng, self.rel[b])
        self.rel[b] = None

    def release(self, b, evt):
        self.rel[b] = evt


def build_nc(nt=NT):
    _stage = [0]

    def go():
        _stage[0] += 1
        return STOP is None or _stage[0] <= STOP

    nc = bass.Bass("TRN2", target_bir_lowering=False,
                   detect_race_conditions=False,
                   dynamic_dma_scratch_size=8192)

    te = nt * TILE
    x_slab = nc.dram_tensor("x_slab", [2 * te, C], F32, kind="ExternalInput").ap()
    wT = {}
    for w in ("wk", "wv", "wr", "wg", "wo", "wfr", "wfk", "wfv"):
        wT[w] = nc.dram_tensor(w, [C, C], F32, kind="ExternalInput").ap()
    vec_d = {}
    for m in ("maak", "maav", "maar", "maag", "mur", "muk"):
        vec_d[m] = nc.dram_tensor(m, [128, 8], F32, kind="ExternalInput").ap()
    dec_d = nc.dram_tensor("dec_row", [1, 16], F32, kind="ExternalInput").ap()
    faa_d = nc.dram_tensor("faa_row", [1, 16], F32, kind="ExternalInput").ap()
    lnw_d = nc.dram_tensor("lnw_row", [1, C], F32, kind="ExternalInput").ap()
    lnb_d = nc.dram_tensor("lnb_row", [1, C], F32, kind="ExternalInput").ap()
    out_d = nc.dram_tensor("out", [te, C], F32, kind="ExternalOutput").ap()
    wsp_d = {w: nc.dram_tensor("wsp_" + w, [NCC * 128, 2 * C], BF16,
                               kind="Internal").ap() for w in
             ("wk", "wv", "wr", "wg", "wo", "wfk", "wfr", "wfv")}

    jj, ii = np.meshgrid(np.arange(128), np.arange(128), indexing="ij")
    c_grid = nc.inline_tensor((jj + 1 - ii).astype(np.float32), name="c_gridt")
    c_tri = nc.inline_tensor((ii > jj).astype(np.float32), name="c_trit")
    c_diag = nc.inline_tensor((ii == jj).astype(np.float32), name="c_diagt")
    c_rampi = nc.inline_tensor(
        np.broadcast_to(-np.arange(128, dtype=np.float32), (128, 128)).copy(),
        name="c_rampi")
    c_rampj = nc.inline_tensor(
        np.broadcast_to((np.arange(128, dtype=np.float32) - 127.0)[:, None],
                        (128, 64)).copy(), name="c_rampj")
    c_ident = nc.inline_tensor(np.eye(128, dtype=np.float32), name="c_ident")

    A = nc.alloc_sbuf_tensor
    XNT = A("XNT", [128, NCC * TILE], F32)
    XNB = A("XNB", [128, 8], F32)
    BUFA = A("BUFA", [128, 4096], F32)
    D = A("D", [128, NCC * TILE], F32)
    MIX0H = A("MIX0H", [128, NCC * TILE], BF16)
    MIX0L = A("MIX0L", [128, NCC * TILE], BF16)
    MIX1H = A("MIX1H", [128, NCC * TILE], BF16)
    MIX1L = A("MIX1L", [128, NCC * TILE], BF16)
    SETA = A("SETA", [128, 1024], F32)
    MIXT = SETA[:, 0:1024]
    KFM = A("KFM", [128, NCC * TILE], F32)
    VTM = A("VTM", [128, TPT * C], F32)
    RFM = A("RFM", [128, NCC * TILE], F32)
    ACCT = A("ACCT", [128, TPT * C], F32)
    KTIL = A("KTIL", [128, C], F32)
    SSCR = A("SSCR", [128, 512], F32)
    RS = A("RS", [128, C], F32)
    APR = A("APR", [128, H * 128], F32)
    WF32 = APR[:, 0:2048]
    _KTILB = KTIL.bitcast(BF16)
    _RSB = RS.bitcast(BF16)
    def _wsphl(k):
        buf = _KTILB if (k % 4) < 2 else _RSB
        base = (k % 2) * 1024
        return buf[:, base:base + 1024]
    S_ALL = A("S_ALL", [128, 512], F32)
    S_ALL2 = A("S_ALL2", [128, 512], F32)
    XIO = A("XIO", [128, 2 * C], F32)
    LNS = A("LNS", [128, 32], F32)
    GNS = A("GNS", [128, 320], F32)
    IDENT = A("IDENT", [128, 128], F32)
    GRIDT = SETA[:, 0:128]
    TRIT = SETA[:, 128:256]
    DIAGT = SETA[:, 256:384]
    RAMPI = SETA[:, 384:512]
    RAMPJ = SETA[:, 512:576]
    MWT = A("MWT", [128, H * 128], F32)
    MLAM = A("MLAM", [128, NCC * 128], F32)
    MK = A("MK", [128, C], F32)
    WSBC = A("WSBC", [128, 512], F32)
    MGAM = A("MGAM", [128, C], F32)
    MBET = A("MBET", [128, C], F32)
    DECB = A("DECB", [128, 16], F32)
    FAAB = A("FAAB", [128, 16], F32)
    LNW = A("LNW", [128, 16], F32)
    WS128 = A("WS128", [128, 16], F32)
    ONES64 = A("ONES64", [128, 64], F32)
    NEGHALF = A("NEGHALF", [128, 1], F32)
    INVC = A("INVC", [128, 1], F32)
    NEGINVC = A("NEGINVC", [128, 1], F32)
    NEGEPSC = A("NEGEPSC", [128, 1], F32)
    INVN = A("INVN", [128, 1], F32)
    NEGEPSN = A("NEGEPSN", [128, 1], F32)
    VEC = {m: A("t_" + m, [128, 8], F32) for m in
           ("maak", "maav", "maar", "maag", "mur", "muk")}
    OMR = A("OMR", [128, 8], F32)
    OMK = A("OMK", [128, 8], F32)
    MSCR = SETA[:, 576:704]
    WRINGB = A("WRINGB", [128, RING * 2 * C], BF16)

    PS = nc.alloc_psum_tensor("PS", [128, 8 * 512], F32)
    banks = Banks(PS)

    st = ExitStack()
    with st:
        s_t = Sem(st, nc, "s_t")
        s_v = Sem(st, nc, "s_v")
        s_s = Sem(st, nc, "s_s")
        s_cst = Sem(st, nc, "s_cst")
        s_x = [Sem(st, nc, f"s_x{i}") for i in range(2)]
        s_xr = [Sem(st, nc, f"s_xr{i}") for i in range(2)]
        s_w = [Sem(st, nc, f"s_w{i}") for i in range(RING)]
        s_gst = Sem(st, nc, "s_gst")
        s_wf = Sem(st, nc, "s_wf")
        s_wsp = Sem(st, nc, "s_wsp")

        S = {
            "xio_free": [None, None],
            "xio_loaded": [None, None],
            "x_cnt": [0, 0],
            "xr_cnt": [0, 0],
            "mix_free": {0: None, 1: None},
            "mix_ready": {0: None, 1: None},
            "mixt_free": None,
            "d_ready": None,
            "xnt_ready": None,
            "xnt_free": None,
            "bufa_free": None,
            "kfm_free": None,
            "vtm_free": None,
            "rfm_free": None,
            "acct_free": None,
            "supd": None,
            "yread": None,
            "setup_done": None,
        }

        def vbar(v, ins):
            ev = _inc(ins, s_v)
            _wt(v, ev)
            return ev

        _scnt = [0]

        def state_buf():
            return (S_ALL, S_ALL2)[_scnt[0] % 2]

        def state_dst():
            return (S_ALL, S_ALL2)[(_scnt[0] + 1) % 2]

        wring = {"n": 0, "scount": [0] * RING, "freed": [], "loadevt": []}

        def wring_slot_ap(i):
            s = i % RING
            return WRINGB[:, s * 2 * C:(s + 1) * 2 * C]

        def wring_load(sync, dram_ap, gate=None):
            i = wring["n"]
            s = i % RING
            if i >= RING:
                _wt(sync, wring["freed"][i - RING])
            _wt(sync, gate)
            ins = sync.dma_start(out=wring_slot_ap(i), in_=dram_ap)
            wring["scount"][s] += 1
            ins.then_inc(s_w[s].h, 16)
            s_w[s].v += 16
            wring["loadevt"].append((s_w[s], 16 * wring["scount"][s]))
            wring["freed"].append(None)
            wring["n"] += 1
            return i

        with nc.Block() as block:
            @block.sync
            def _(sync):
                for ap_, dst in ((c_ident.ap(), IDENT), (c_grid.ap(), GRIDT),
                                 (c_tri.ap(), TRIT), (c_diag.ap(), DIAGT),
                                 (c_rampi.ap(), RAMPI), (c_rampj.ap(), RAMPJ)):
                    _inc(sync.dma_start(out=dst[:, :], in_=ap_[:, :]), s_cst, 16)
                for m in VEC:
                    _inc(sync.dma_start(out=VEC[m][:, :], in_=vec_d[m][:, :]),
                         s_cst, 16)
                _inc(sync.dma_start(out=DECB[:, :],
                                    in_=dec_d.broadcast_to([128, 16])), s_cst, 16)
                _inc(sync.dma_start(out=FAAB[:, :],
                                    in_=faa_d.broadcast_to([128, 16])), s_cst, 16)
                _inc(sync.dma_start(out=MGAM[:, :],
                                    in_=lnw_d.broadcast_to([128, C])), s_cst, 16)
                _inc(sync.dma_start(out=MBET[:, :],
                                    in_=lnb_d.broadcast_to([128, C])), s_cst, 16)

            n_cst = s_cst.v

            @block.vector
            def _(v):
                v.wait_ge(s_cst.h, n_cst)
                v.memset(XNB[:, :], 0.0)
                v.memset(S_ALL[:, :], 0.0)
                v.memset(ONES64[:, :], 1.0)
                v.memset(NEGHALF[:, :], -0.5)
                v.memset(INVC[:, :], 1.0 / C)
                v.memset(NEGINVC[:, :], -1.0 / C)
                v.memset(NEGEPSC[:, :], -EPS_LN)
                v.memset(INVN[:, :], 1.0 / NDIM)
                v.memset(NEGEPSN[:, :], -EPS_GN)
                v.tensor_scalar(OMR[:, :], VEC["mur"][:, :], -1.0, 1.0,
                                OP.mult, OP.add)
                ev = _inc(v.tensor_scalar(OMK[:, :], VEC["muk"][:, :], -1.0, 1.0,
                                          OP.mult, OP.add), s_v)
                S["vsetup"] = ev

            @block.scalar
            def _(s):
                s.wait_ge(s_cst.h, n_cst)
                _wt(s, S["vsetup"])
                ev = _inc(s.activation(LNW[:, :], DECB[:, :], ACT.Exp), s_s)
                S["expd"] = ev

            @block.vector
            def _(v):
                _wt(v, S["expd"])
                ev = _inc(v.tensor_scalar_mul(WS128[:, :], LNW[:, :], -128.0),
                          s_v)
                S["ex128"] = ev

            @block.scalar
            def _(s):
                _wt(s, S["ex128"])
                s.activation(WS128[:, :], WS128[:, :], ACT.Exp)
                ev = None
                for h in range(H):
                    hcol = (h % 2) * 1024 + (h // 2) * 128
                    s.activation(MWT[:, hcol:hcol + 128], GRIDT[:, :],
                                 ACT.Exp, scale=LNW[:, h:h + 1])
                    s.activation(MK[:, h * 64:(h + 1) * 64], RAMPJ[:, :],
                                 ACT.Exp, scale=LNW[:, h:h + 1])
                    bp = (h % 2) * 64
                    s.activation(WSBC[bp:bp + 64,
                                      (h // 2) * 64:(h // 2) * 64 + 64],
                                 ONES64[0:64, :], ACT.Identity,
                                 scale=WS128[bp:bp + 64, h:h + 1])
                for cc in range(NCC):
                    s.activation(MLAM[0:64, cc * 128:(cc + 1) * 128],
                                 RAMPI[0:64, :], ACT.Exp,
                                 scale=LNW[0:64, 2 * cc:2 * cc + 1])
                    ev = s.activation(MLAM[64:128, cc * 128:(cc + 1) * 128],
                                      RAMPI[64:128, :], ACT.Exp,
                                      scale=LNW[64:128, 2 * cc + 1:2 * cc + 2])
                S["ssetup"] = _inc(ev, s_s)

            @block.vector
            def _(v):
                _wt(v, S["ssetup"])
                ev = None
                for h in range(H):
                    hcol = (h % 2) * 1024 + (h // 2) * 128
                    blk = MWT[:, hcol:hcol + 128]
                    v.tensor_scalar_mul(MSCR[:, :], DIAGT[:, :],
                                        FAAB[:, h:h + 1])
                    vbar(v, v.tensor_tensor(blk, blk, TRIT[:, :], OP.mult))
                    ev = v.tensor_tensor(blk, blk, MSCR[:, :], OP.add)
                S["setup_done"] = _inc(ev, s_v)

            SPLIT_ORDER = ("wk", "wv", "wr", "wg", "wo", "wfk", "wfr", "wfv")
            split_done = {w: [None] * NCC for w in SPLIT_ORDER}
            sp = {"next": 0, "ld": {}, "tt": {}, "st": {}}
            sp_list = [(w, cc, h) for w in SPLIT_ORDER for cc in range(NCC)
                       for h in range(2)]

            def sp_slot(k, buf, width):
                p = (k % 4) * width
                return buf[:, p:p + width]

            def emit_split_pair():
                k0 = sp["next"]
                if k0 >= len(sp_list):
                    return
                ks = [k for k in (k0, k0 + 1) if k < len(sp_list)]
                sp["next"] = k0 + len(ks)

                @block.sync
                def _(sync):
                    for k in ks:
                        w, cc, h = sp_list[k]
                        _wt(sync, sp["tt"].get(k - 4))
                        ins = sync.dma_start(
                            out=sp_slot(k, WF32, 512),
                            in_=wT[w][cc * 128:(cc + 1) * 128,
                                      h * 512:(h + 1) * 512])
                        sp["ld"][k] = _inc(ins, s_wf, 16)

                @block.vector
                def _(v):
                    cpev = {}
                    for k in ks:
                        _wt(v, sp["ld"][k])
                        _wt(v, sp["st"].get(k - 4))
                        hl = _wsphl(k)
                        cpev[k] = _inc(v.tensor_copy(hl[:, 0:512],
                                                     sp_slot(k, WF32, 512)),
                                       s_v)
                    for k in ks:
                        _wt(v, cpev[k])
                        hl = _wsphl(k)
                        ins = v.tensor_tensor(
                            hl[:, 512:1024],
                            sp_slot(k, WF32, 512),
                            hl[:, 0:512], OP.subtract)
                        sp["tt"][k] = _inc(ins, s_v)

                @block.scalar
                def _(s):
                    for k in ks:
                        w, cc, h = sp_list[k]
                        _wt(s, sp["tt"][k])
                        rows = slice(cc * 128, (cc + 1) * 128)
                        ins = s.dma_start(
                            out=wsp_d[w][rows, h * 1024:(h + 1) * 1024],
                            in_=_wsphl(k))
                        ev = _inc(ins, s_wsp, 16)
                        sp["st"][k] = ev
                        if h == 1:
                            split_done[w][cc] = ev

            def emit_one_split():
                emit_split_pair()

            def pump_splits(n=2):
                for _ in range(n):
                    emit_one_split()

            def ensure_splits(w):
                while split_done[w][NCC - 1] is None:
                    emit_one_split()

            def emit_ln_stats(kind, ti):
                row0 = (0 if kind == "warm" else te) + ti * TILE
                sqev = {}
                for tch in range(TPT):
                    pump_splits(1)
                    sl = tch % 2

                    @block.sync
                    def _(sync, tch=tch, sl=sl, row0=row0):
                        _wt(sync, S["xio_free"][sl])
                        ins = sync.dma_start(
                            out=XIO[:, sl * C:(sl + 1) * C],
                            in_=x_slab[row0 + tch * 128: row0 + (tch + 1) * 128, :])
                        ins.then_inc(s_x[sl].h, 16)
                        s_x[sl].v += 16
                        S["x_cnt"][sl] += 1
                        S["xio_loaded"][sl] = (s_x[sl], 16 * S["x_cnt"][sl])

                    ld_evt = S["xio_loaded"][sl]

                    @block.vector
                    def _(v, tch=tch, sl=sl, ld_evt=ld_evt, sqev=sqev):
                        _wt(v, ld_evt)
                        if tch == 0:
                            _wt(v, S["setup_done"])
                            _wt(v, S["bufa_free"])
                        xio = XIO[:, sl * C:(sl + 1) * C]
                        v.tensor_copy(BUFA[:, tch * C:(tch + 1) * C], xio)
                        v.tensor_reduce(LNS[:, tch:tch + 1], xio, AX.X, OP.add)
                        ev = _inc(v.tensor_tensor(
                            D[:, tch * C:(tch + 1) * C], xio, xio, OP.mult),
                            s_v)
                        sqev[tch] = ev
                        S["xio_free"][sl] = ev
                        if tch > 0:
                            _wt(v, sqev[tch - 1])
                            _inc(v.tensor_reduce(
                                LNS[:, 4 + tch - 1:4 + tch],
                                D[:, (tch - 1) * C:tch * C],
                                AX.X, OP.add), s_v)

                @block.vector
                def _(v, sqev=sqev):
                    _wt(v, sqev[TPT - 1])
                    _inc(v.tensor_reduce(
                        LNS[:, 4 + TPT - 1:4 + TPT],
                        D[:, (TPT - 1) * C:TPT * C],
                        AX.X, OP.add), s_v)

                red_evt = (s_v, s_v.v)

                @block.scalar
                def _(s, red_evt=red_evt):
                    _wt(s, red_evt)
                    s.activation(LNS[:, 8:12], LNS[:, 0:4], ACT.Identity,
                                 scale=INVC[:, :])
                    s.activation(LNS[:, 12:16], LNS[:, 0:4], ACT.Identity,
                                 scale=NEGINVC[:, :])
                    s.activation(LNS[:, 16:20], LNS[:, 8:12], ACT.Square)
                    _inc(s.activation(LNS[:, 16:20], LNS[:, 16:20],
                                      ACT.Identity, bias=NEGEPSC[:, :]), s_s)
                msq_evt = (s_s, s_s.v)

                @block.vector
                def _(v, msq_evt=msq_evt):
                    _wt(v, msq_evt)
                    _inc(v.scalar_tensor_tensor(
                        LNS[:, 20:24], LNS[:, 4:8], 1.0 / C,
                        LNS[:, 16:20], OP.mult, OP.subtract), s_v)
                var_evt = (s_v, s_v.v)

                @block.scalar
                def _(s, var_evt=var_evt):
                    _wt(s, var_evt)
                    s.activation(LNS[:, 24:28], LNS[:, 20:24], ACT.Ln)
                    _inc(s.activation(LNS[:, 28:32], LNS[:, 24:28], ACT.Exp,
                                      scale=NEGHALF[:, :]), s_s)
                sd_evt = (s_s, s_s.v)

                @block.vector
                def _(v, sd_evt=sd_evt):
                    _wt(v, sd_evt)
                    ev = None
                    for tch in range(TPT):
                        dst = BUFA[:, tch * C:(tch + 1) * C]
                        ev = v.tensor_scalar(dst, dst,
                                             LNS[:, 12 + tch:13 + tch],
                                             LNS[:, 28 + tch:29 + tch],
                                             OP.add, OP.mult)
                    _inc(ev, s_v)

                S["norm_evt"] = (s_v, s_v.v)

            def emit_ln_finish():
                norm_evt = S["norm_evt"]
                last_tr = None
                for cc in range(NCC):
                    bk = 6 + (cc % 2)

                    @block.tensor
                    def _(t, cc=cc, bk=bk, norm_evt=norm_evt):
                        _wt(t, norm_evt)
                        banks.acquire(t, bk)
                        for tch in range(TPT):
                            _inc(t.transpose(
                                banks.ap(bk, tch * 128, (tch + 1) * 128),
                                BUFA[:, tch * C + cc * 128:
                                     tch * C + (cc + 1) * 128],
                                IDENT[:, :]), s_t)
                    tr_evt = (s_t, s_t.v)
                    last_tr = tr_evt

                    @block.vector
                    def _(v, cc=cc, bk=bk, tr_evt=tr_evt):
                        _wt(v, tr_evt)
                        _wt(v, S["xnt_free"])
                        ev = _inc(v.tensor_copy(XNT[:, cc * TILE:(cc + 1) * TILE],
                                                banks.ap(bk)), s_v)
                        banks.release(bk, ev)

                S["xnt_ready"] = (s_v, s_v.v)
                S["xnt_free"] = None
                S["bufa_free"] = last_tr

                @block.vector
                def _(v):
                    _wt(v, S["xnt_ready"])
                    xv3 = XNT[:, :].rearrange("p (c t) -> p c t", t=TILE)
                    dv3 = D[:, :].rearrange("p (c t) -> p c t", t=TILE)
                    v.tensor_tensor(dv3[:, :, 0:1],
                                    XNB[:, :].rearrange("p (c o) -> p c o", o=1),
                                    xv3[:, :, 0:1], OP.subtract)
                    ev = _inc(v.tensor_tensor(dv3[:, :, 1:TILE],
                                              xv3[:, :, 0:TILE - 1],
                                              xv3[:, :, 1:TILE], OP.subtract),
                              s_v)
                    S["d_ready"] = ev

            def emit_mix(mslot, vtile):
                pump_splits(2)
                bh = MIX0H if mslot == 0 else MIX1H
                bl = MIX0L if mslot == 0 else MIX1L
                evs = {}

                def mslot_ap(cc):
                    p = (cc % 2) * 512
                    return MIXT[:, p:p + 512]

                @block.vector
                def _(v, vtile=vtile, mslot=mslot, bh=bh, bl=bl):
                    def stt(cc):
                        sl = slice(cc * TILE, (cc + 1) * TILE)
                        if cc == 0:
                            _wt(v, S["d_ready"])
                            _wt(v, S["mix_free"][mslot])
                            _wt(v, S["setup_done"])
                            _wt(v, S["mixt_free"])
                        _wt(v, evs.get(("lo", cc - 2)))
                        evs[("t", cc)] = _inc(v.scalar_tensor_tensor(
                            mslot_ap(cc), D[:, sl], vtile[:, cc:cc + 1],
                            XNT[:, sl], OP.mult, OP.add), s_v)

                    def cp(cc):
                        sl = slice(cc * TILE, (cc + 1) * TILE)
                        _wt(v, evs[("t", cc)])
                        evs[("hi", cc)] = _inc(
                            v.tensor_copy(bh[:, sl], mslot_ap(cc)), s_v)

                    def lo(cc):
                        sl = slice(cc * TILE, (cc + 1) * TILE)
                        _wt(v, evs[("hi", cc)])
                        evs[("lo", cc)] = _inc(
                            v.tensor_tensor(bl[:, sl], mslot_ap(cc),
                                            bh[:, sl], OP.subtract), s_v)

                    for k in range(0, NCC, 2):
                        stt(k)
                        stt(k + 1)
                        cp(k)
                        cp(k + 1)
                        lo(k)
                        lo(k + 1)
                    S["mixt_free"] = evs[("lo", NCC - 1)]
                    S["mix_ready"][mslot] = evs[("lo", NCC - 1)]

            def emit_proj_wstat(wname, mslot, dst, out_evts, dst_free_key,
                                evac_act=None):
                pump_splits(2)
                ensure_splits(wname)
                mh = MIX0H if mslot == 0 else MIX1H
                ml = MIX0L if mslot == 0 else MIX1L
                loads = [None] * NCC
                bank_evts = [None] * 8

                @block.sync
                def _(sync):
                    for cc in range(RING):
                        loads[cc] = wring_load(sync,
                                               wsp_d[wname][cc * 128:(cc + 1) * 128, :],
                                               gate=split_done[wname][cc])

                for cc in range(NCC):
                    @block.tensor
                    def _(t, cc=cc):
                        _wt(t, wring["loadevt"][loads[cc]])
                        if cc == 0:
                            _wt(t, S["mix_ready"][mslot])
                            for d in range(8):
                                banks.acquire(t, d)
                        slot = wring_slot_ap(loads[cc])
                        msl = slice(cc * TILE, (cc + 1) * TILE)
                        ins = None
                        for d in range(8):
                            base = (d // 4) * 1024 + (d % 4) * 128
                            w_h = slot[:, base:base + 128]
                            w_l = slot[:, base + 512:base + 640]
                            ins = t.matmul(banks.ap(d), w_h, mh[:, msl],
                                           start=(cc == 0), stop=False)
                            ins = t.matmul(banks.ap(d), w_h, ml[:, msl],
                                           start=False, stop=False)
                            ins = t.matmul(banks.ap(d), w_l, mh[:, msl],
                                           start=False, stop=(cc == NCC - 1))
                            if cc == NCC - 1:
                                bank_evts[d] = _inc(ins, s_t)
                        if cc < NCC - 1:
                            wring["freed"][loads[cc]] = _inc(ins, s_t)
                        else:
                            wring["freed"][loads[cc]] = bank_evts[7]
                            S["mix_free"][mslot] = bank_evts[7]

                    if cc + RING < NCC:
                        @block.sync
                        def _(sync, cc=cc):
                            loads[cc + RING] = wring_load(
                                sync,
                                wsp_d[wname][(cc + RING) * 128:(cc + RING + 1) * 128, :],
                                gate=split_done[wname][cc + RING])

                dst_free = S[dst_free_key]
                S[dst_free_key] = None
                for d in range(8):
                    if evac_act is not None or d % 2 == 0:
                        @block.scalar
                        def _(s, d=d):
                            _wt(s, bank_evts[d])
                            _wt(s, dst_free)
                            ev = _inc(s.activation(
                                dst[:, d * TILE:(d + 1) * TILE], banks.ap(d),
                                evac_act if evac_act is not None
                                else ACT.Identity), s_s)
                            banks.release(d, ev)
                            out_evts[d] = ev
                    else:
                        @block.vector
                        def _(v, d=d):
                            _wt(v, bank_evts[d])
                            _wt(v, dst_free)
                            ev = _inc(v.tensor_copy(
                                dst[:, d * TILE:(d + 1) * TILE], banks.ap(d)),
                                s_v)
                            banks.release(d, ev)
                            out_evts[d] = ev

            def emit_proj_xstat(wname, stats, stat_ready_fn, evac_fn,
                                mix_free_key=None):
                pump_splits(2)
                ensure_splits(wname)
                sh_buf, sl_buf = stats
                loads = [None] * NCC
                bank_evts = [None] * 8

                @block.sync
                def _(sync):
                    for cc in range(RING):
                        loads[cc] = wring_load(sync,
                                               wsp_d[wname][cc * 128:(cc + 1) * 128, :],
                                               gate=split_done[wname][cc])

                for cc in range(NCC):
                    @block.tensor
                    def _(t, cc=cc):
                        _wt(t, wring["loadevt"][loads[cc]])
                        if cc == 0:
                            _wt(t, stat_ready_fn())
                            for b in range(8):
                                banks.acquire(t, b)
                        slot = wring_slot_ap(loads[cc])
                        ins = None
                        for tch in range(TPT):
                            ssl = slice(cc * TILE + tch * 128,
                                        cc * TILE + (tch + 1) * 128)
                            for dh in range(2):
                                b = 2 * tch + dh
                                movh = slot[:, dh * 1024:dh * 1024 + 512]
                                movl = slot[:, dh * 1024 + 512:
                                            (dh + 1) * 1024]
                                ins = t.matmul(banks.ap(b), sh_buf[:, ssl],
                                               movh, start=(cc == 0),
                                               stop=False)
                                ins = t.matmul(banks.ap(b), sh_buf[:, ssl],
                                               movl, start=False, stop=False)
                                ins = t.matmul(banks.ap(b), sl_buf[:, ssl],
                                               movh, start=False,
                                               stop=(cc == NCC - 1))
                                if cc == NCC - 1:
                                    bank_evts[b] = _inc(ins, s_t)
                        if cc < NCC - 1:
                            wring["freed"][loads[cc]] = _inc(ins, s_t)
                        else:
                            wring["freed"][loads[cc]] = bank_evts[7]
                            if mix_free_key is not None:
                                S["mix_free"][mix_free_key] = bank_evts[7]

                    if cc + RING < NCC:
                        @block.sync
                        def _(sync, cc=cc):
                            loads[cc + RING] = wring_load(
                                sync,
                                wsp_d[wname][(cc + RING) * 128:(cc + RING + 1) * 128, :],
                                gate=split_done[wname][cc + RING])

                for tch in range(TPT):
                    for dh in range(2):
                        evac_fn(2 * tch + dh, tch, dh, bank_evts[2 * tch + dh])

            def evac_simple(dst, dst_free_key=None, mask=None, act=None,
                            evts_out=None):
                holder = {"df": S[dst_free_key] if dst_free_key else None}
                if dst_free_key:
                    S[dst_free_key] = None

                def f(b, tch, dh, evt):
                    sl = slice(tch * C + dh * 512, tch * C + (dh + 1) * 512)
                    if mask is not None:
                        @block.vector
                        def _(v, b=b, sl=sl, dh=dh, evt=evt):
                            _wt(v, evt)
                            _wt(v, holder["df"])
                            ev = _inc(v.tensor_tensor(
                                dst[:, sl], banks.ap(b),
                                mask[:, dh * 512:(dh + 1) * 512], OP.mult), s_v)
                            banks.release(b, ev)
                            if evts_out is not None:
                                evts_out[b] = ev
                    elif act is not None:
                        @block.scalar
                        def _(s, b=b, sl=sl, evt=evt):
                            _wt(s, evt)
                            _wt(s, holder["df"])
                            ev = _inc(s.activation(dst[:, sl], banks.ap(b), act),
                                      s_s)
                            banks.release(b, ev)
                            if evts_out is not None:
                                evts_out[b] = ev
                    else:
                        if b % 2 == 0:
                            @block.scalar
                            def _(s, b=b, sl=sl, evt=evt):
                                _wt(s, evt)
                                _wt(s, holder["df"])
                                ev = _inc(s.activation(dst[:, sl],
                                                       banks.ap(b),
                                                       ACT.Identity), s_s)
                                banks.release(b, ev)
                                if evts_out is not None:
                                    evts_out[b] = ev
                        else:
                            @block.vector
                            def _(v, b=b, sl=sl, evt=evt):
                                _wt(v, evt)
                                _wt(v, holder["df"])
                                ev = _inc(v.tensor_copy(dst[:, sl],
                                                        banks.ap(b)), s_v)
                                banks.release(b, ev)
                                if evts_out is not None:
                                    evts_out[b] = ev
                return f

            def rf_sl(b):
                if b < 2:
                    return KTIL[:, b * 512:(b + 1) * 512]
                if b < 4:
                    return RS[:, (b - 2) * 512:(b - 1) * 512]
                return APR[:, (b - 4) * 512:(b - 3) * 512]

            def emit_save_xnb():
                @block.vector
                def _(v):
                    xl = XNT[:, :].rearrange("p (c t) -> p c t", t=TILE)
                    ev = _inc(v.tensor_copy(
                        XNB[:, :].rearrange("p (c o) -> p c o", o=1),
                        xl[:, :, TILE - 1:TILE]), s_v)
                    S["xnt_free"] = ev

            emit_ln_stats("warm", 0)
            emit_ln_finish()
            for wt_i in range(nt):
                if not go():
                    break
                if not go():
                    break
                if wt_i == 0:
                    emit_mix(0, VEC["maak"])
                emit_mix(1, VEC["maav"])
                if wt_i + 1 < nt:
                    emit_ln_stats("warm", wt_i + 1)
                else:
                    emit_ln_stats("emit", 0)

                kevts = [None] * 8
                if not go():
                    break
                emit_proj_xstat("wk", (MIX0H, MIX0L), lambda: S["mix_ready"][0],
                                evac_simple(KFM, "kfm_free", mask=MK,
                                            evts_out=kevts), mix_free_key=0)
                vevts = [None] * 8
                if not go():
                    break
                emit_proj_xstat("wv", (MIX1H, MIX1L), lambda: S["mix_ready"][1],
                                evac_simple(VTM, "vtm_free", evts_out=vevts),
                                mix_free_key=1)
                pump_splits(12)
                if not go():
                    break

                for ch in range(TPT):
                    ube = (ch % 2) * 2
                    ubo = ube + 1

                    @block.tensor
                    def _(t, ch=ch, ube=ube, ubo=ubo):
                        _wt(t, kevts[2 * ch])
                        _wt(t, kevts[2 * ch + 1])
                        _wt(t, vevts[2 * ch])
                        _wt(t, vevts[2 * ch + 1])
                        banks.acquire(t, ube)
                        banks.acquire(t, ubo)
                        for par, ub in ((0, ube), (1, ubo)):
                            bp = par * 64
                            ins = None
                            for hh in range(8):
                                h = 2 * hh + par
                                ins = t.matmul(
                                    PS[bp:bp + 64,
                                       ub * 512 + hh * 64:
                                       ub * 512 + hh * 64 + 64],
                                    KFM[:, ch * C + h * 64: ch * C + (h + 1) * 64],
                                    VTM[:, ch * C + h * 64: ch * C + (h + 1) * 64],
                                    start=True, stop=True,
                                    tile_position=(0, bp))
                            S[f"u_mm{par}"] = _inc(ins, s_t)

                    @block.vector
                    def _(v, ube=ube, ubo=ubo):
                        _wt(v, S["u_mm0"])
                        _wt(v, S["u_mm1"])
                        sa, sd = state_buf(), state_dst()
                        _scnt[0] += 1
                        vbar(v, v.tensor_tensor(SSCR[:, :], sa[:, :],
                                                WSBC[:, :], OP.mult))
                        v.tensor_tensor(sd[0:64, :], SSCR[0:64, :],
                                        banks.ap(ube)[0:64, :], OP.add)
                        ev = _inc(v.tensor_tensor(sd[64:128, :],
                                                  SSCR[64:128, :],
                                                  banks.ap(ubo)[64:128, :],
                                                  OP.add), s_v)
                        S["supd"] = ev
                        banks.release(ube, ev)
                        banks.release(ubo, ev)

                S["kfm_free"] = S["u_mm1"]
                S["vtm_free"] = S["u_mm1"]
                pump_splits(12)
                emit_save_xnb()
                emit_ln_finish()
                emit_mix(0, VEC["maak"])

            ensure_splits("wfv")
            S["split_fin"] = (s_wsp, s_wsp.v)

            for ti in range(nt):
                if not go():
                    break
                if not go():
                    break
                emit_mix(1, VEC["maav"])

                kevts = [None] * 8
                if not go():
                    break
                emit_proj_wstat("wk", 0, KFM, kevts, "kfm_free")
                vevts = [None] * 8
                if not go():
                    break
                emit_proj_xstat("wv", (MIX1H, MIX1L), lambda: S["mix_ready"][1],
                                evac_simple(VTM, "vtm_free", evts_out=vevts),
                                mix_free_key=1)
                emit_mix(0, VEC["maar"])
                revts = [None] * 8
                if not go():
                    break
                emit_proj_wstat("wr", 0, RFM, revts, "rfm_free")
                if not go():
                    break

                yn_evts = [None] * TPT

                def emit_scan_prep(ch):
                    @block.tensor
                    def _(t, ch=ch):
                        for e in kevts:
                            _wt(t, e)
                        banks.acquire(t, 6)
                        banks.acquire(t, 7)
                        ins = None
                        for cc in range(NCC):
                            bk = 6 + cc // 4
                            ins = t.transpose(
                                banks.ap(bk, (cc % 4) * 128, (cc % 4 + 1) * 128),
                                KFM[:, cc * TILE + ch * 128:
                                    cc * TILE + (ch + 1) * 128],
                                IDENT[:, :])
                            _inc(ins, s_t)
                        S["ktr"] = (s_t, s_t.v)

                    @block.vector
                    def _(v, ch=ch, ti=ti):
                        _wt(v, S["ktr"])
                        if ti == 0 and ch == 0:
                            _wt(v, S["split_fin"])
                        v.tensor_tensor(KTIL[:, 0:512], banks.ap(6),
                                        MK[:, 0:512], OP.mult)
                        ev = _inc(v.tensor_tensor(KTIL[:, 512:1024],
                                                  banks.ap(7), MK[:, 512:1024],
                                                  OP.mult), s_v)
                        banks.release(6, ev)
                        banks.release(7, ev)
                        rsv = RS[:, :].rearrange("p (c t) -> p c t", t=128)
                        rfm3 = RFM[:, :].rearrange("p (c t) -> p c t", t=TILE)
                        for e in revts:
                            _wt(v, e)
                        ev = _inc(v.tensor_tensor(
                            rsv, rfm3[:, :, ch * 128:(ch + 1) * 128],
                            MLAM[:, :].rearrange("p (c t) -> p c t", t=128),
                            OP.mult), s_v)
                        S["rs"] = ev

                    @block.tensor
                    def _(t, ch=ch):
                        for e in revts:
                            _wt(t, e)
                        for b in range(4):
                            banks.acquire(t, b)
                        ins = None
                        for h in range(H):
                            cc = h // 2
                            bp = (h % 2) * 64
                            hcol = (h % 2) * 1024 + (h // 2) * 128
                            tok = slice(cc * TILE + ch * 128,
                                        cc * TILE + (ch + 1) * 128)
                            ins = t.matmul(
                                banks.span(0, 4)[:, hcol:hcol + 128],
                                KFM[bp:bp + 64, tok],
                                RFM[bp:bp + 64, tok],
                                start=True, stop=True,
                                tile_position=(bp, 0))
                        S["amm"] = _inc(ins, s_t)

                    @block.vector
                    def _(v, ch=ch, ti=ti):
                        _wt(v, S["amm"])
                        if ti == 0 and ch == 0:
                            _wt(v, S["split_fin"])
                        v.tensor_tensor(APR[:, 0:1024], banks.span(0, 2),
                                        MWT[:, 0:1024], OP.mult)
                        ev = _inc(v.tensor_tensor(APR[:, 1024:2048],
                                                  banks.span(2, 2),
                                                  MWT[:, 1024:2048], OP.mult),
                                  s_v)
                        for b in range(4):
                            banks.release(b, ev)
                        S["apr"] = ev

                emit_scan_prep(0)
                for ch in range(TPT):
                    if not go():
                        break

                    @block.tensor
                    def _(t, ch=ch):
                        _wt(t, S["apr"])
                        _wt(t, S["supd"])
                        _wt(t, S["rs"])
                        banks.acquire(t, 4)
                        banks.acquire(t, 5)
                        sa = state_buf()
                        ins = None
                        for h in range(H):
                            bp = (h % 2) * 64
                            cc = h // 2
                            hcol = (h % 2) * 1024 + (h // 2) * 128
                            ysl = banks.span(4, 2)[:, h * 64:(h + 1) * 64]
                            ins = t.matmul(ysl, APR[:, hcol:hcol + 128],
                                           VTM[:, ch * C + h * 64:
                                               ch * C + (h + 1) * 64],
                                           start=True, stop=False)
                            ins = t.matmul(
                                ysl,
                                RS[bp:bp + 64, cc * 128:(cc + 1) * 128],
                                sa[bp:bp + 64,
                                   (h // 2) * 64:(h // 2) * 64 + 64],
                                start=False, stop=True,
                                tile_position=(bp, 0))
                        S["yread"] = _inc(ins, s_t)

                    @block.tensor
                    def _(t, ch=ch):
                        banks.acquire(t, 2)
                        banks.acquire(t, 3)
                        for par, ub in ((0, 2), (1, 3)):
                            bp = par * 64
                            ins = None
                            for hh in range(8):
                                h = 2 * hh + par
                                ins = t.matmul(
                                    PS[bp:bp + 64,
                                       ub * 512 + hh * 64:
                                       ub * 512 + hh * 64 + 64],
                                    KTIL[:, h * 64:(h + 1) * 64],
                                    VTM[:, ch * C + h * 64: ch * C + (h + 1) * 64],
                                    start=True, stop=True,
                                    tile_position=(0, bp))
                            S[f"u_mm{par}"] = _inc(ins, s_t)

                    @block.vector
                    def _(v, ch=ch):
                        _wt(v, S["u_mm0"])
                        _wt(v, S["u_mm1"])
                        sa, sd = state_buf(), state_dst()
                        _scnt[0] += 1
                        vbar(v, v.tensor_tensor(SSCR[:, :], sa[:, :],
                                                WSBC[:, :], OP.mult))
                        v.tensor_tensor(sd[0:64, :], SSCR[0:64, :],
                                        banks.ap(2)[0:64, :], OP.add)
                        ev = _inc(v.tensor_tensor(sd[64:128, :],
                                                  SSCR[64:128, :],
                                                  banks.ap(3)[64:128, :],
                                                  OP.add), s_v)
                        S["supd"] = ev
                        banks.release(2, ev)
                        banks.release(3, ev)

                    @block.vector
                    def _(v, ch=ch):
                        _wt(v, S["yread"])
                        ev = _inc(v.tensor_copy(BUFA[:, ch * C:(ch + 1) * C],
                                                banks.span(4, 2)), s_v)
                        banks.release(4, ev)
                        banks.release(5, ev)
                        yn_evts[ch] = ev

                    if ch + 1 < TPT:
                        emit_scan_prep(ch + 1)

                if not go():
                    break
                emit_mix(1, VEC["maag"])

                @block.vector
                def _(v):
                    for e in yn_evts:
                        _wt(v, e)
                    y3 = BUFA[:, :].rearrange("p (g n) -> p g n", n=NDIM)
                    _inc(v.tensor_reduce(GNS[:, 0:64], y3, AX.X, OP.add), s_v)
                mean_evt = (s_v, s_v.v)

                @block.scalar
                def _(s, mean_evt=mean_evt):
                    _wt(s, mean_evt)
                    _wt(s, S["acct_free"])
                    S["acct_free"] = None
                    _inc(s.activation(ACCT[:, :], BUFA[:, :], ACT.Square), s_s)
                sq_evt = (s_s, s_s.v)

                @block.vector
                def _(v, sq_evt=sq_evt):
                    _wt(v, sq_evt)
                    sq3 = ACCT[:, :].rearrange("p (g n) -> p g n", n=NDIM)
                    _inc(v.tensor_reduce(GNS[:, 64:128], sq3, AX.X, OP.add),
                         s_v)
                red2_evt = (s_v, s_v.v)

                @block.scalar
                def _(s, red2_evt=red2_evt):
                    _wt(s, red2_evt)
                    s.activation(GNS[:, 128:192], GNS[:, 0:64], ACT.Identity,
                                 scale=INVN[:, :])
                    s.activation(GNS[:, 0:64], GNS[:, 128:192], ACT.Square)
                    s.activation(GNS[:, 0:64], GNS[:, 0:64], ACT.Identity,
                                 bias=NEGEPSN[:, :])
                    _inc(s.activation(GNS[:, 64:128], GNS[:, 64:128],
                                      ACT.Identity), s_s)
                gst_evt = (s_s, s_s.v)

                @block.vector
                def _(v, gst_evt=gst_evt):
                    _wt(v, gst_evt)
                    _inc(v.scalar_tensor_tensor(
                        GNS[:, 192:256], GNS[:, 64:128], 1.0 / NDIM,
                        GNS[:, 0:64], OP.mult, OP.subtract), s_v)
                var_evt = (s_v, s_v.v)

                @block.scalar
                def _(s, var_evt=var_evt):
                    _wt(s, var_evt)
                    s.activation(GNS[:, 64:128], GNS[:, 192:256], ACT.Ln)
                    _inc(s.activation(GNS[:, 256:320], GNS[:, 64:128],
                                      ACT.Exp, scale=NEGHALF[:, :]), s_s)
                sd_evt = (s_s, s_s.v)

                @block.vector
                def _(v, sd_evt=sd_evt):
                    _wt(v, sd_evt)
                    mb = GNS[:, 128:192].rearrange(
                        "p (g o) -> p g o", o=1).broadcast_to([128, 64, NDIM])
                    rb = GNS[:, 256:320].rearrange(
                        "p (g o) -> p g o", o=1).broadcast_to([128, 64, NDIM])
                    y3 = BUFA[:, :].rearrange("p (g n) -> p g n", n=NDIM)
                    vbar(v, v.tensor_tensor(y3, y3, mb, OP.subtract))
                    vbar(v, v.tensor_tensor(y3, y3, rb, OP.mult))
                    yc = BUFA[:, :].rearrange("p (a c) -> p a c", c=C)
                    gg = MGAM[:, :].rearrange(
                        "p (a c) -> p a c", a=1).broadcast_to([128, TPT, C])
                    bb = MBET[:, :].rearrange(
                        "p (a c) -> p a c", a=1).broadcast_to([128, TPT, C])
                    vbar(v, v.tensor_tensor(yc, yc, gg, OP.mult))
                    ev = _inc(v.tensor_tensor(yc, yc, bb, OP.add), s_v)
                    S["yn_all"] = ev

                gevts = [None] * 8
                emit_proj_xstat("wg", (MIX1H, MIX1L),
                                lambda: S["mix_ready"][1],
                                evac_simple(KFM, "kfm_free", act=ACT.Gelu,
                                            evts_out=gevts), mix_free_key=1)

                if not go():
                    break
                @block.vector
                def _(v):
                    for e in gevts:
                        _wt(v, e)
                    _wt(v, S["yn_all"])
                    _wt(v, S["vtm_free"])
                    ev = _inc(v.tensor_tensor(VTM[:, :], BUFA[:, :], KFM[:, :],
                                              OP.mult), s_v)
                    S["z_tm"] = ev
                    S["bufa_free"] = ev
                    S["vtm_free"] = None

                zf_free = S["mix_free"][0]
                S["mix_free"][0] = None
                for p in range(0, NCC, 2):
                    tr_evt = {}
                    for cc in (p, p + 1):
                        bk = 6 + (cc % 2)

                        @block.tensor
                        def _(t, cc=cc, bk=bk):
                            _wt(t, S["z_tm"])
                            banks.acquire(t, bk)
                            for tch in range(TPT):
                                _inc(t.transpose(
                                    banks.ap(bk, tch * 128, (tch + 1) * 128),
                                    VTM[:, tch * C + cc * 128:
                                        tch * C + (cc + 1) * 128],
                                    IDENT[:, :]), s_t)
                        tr_evt[cc] = (s_t, s_t.v)

                    @block.vector
                    def _(v, p=p, tr_evt=tr_evt, zf_free=zf_free):
                        cpev = {}
                        for cc in (p, p + 1):
                            _wt(v, tr_evt[cc])
                            if cc == 0:
                                _wt(v, zf_free)
                            cpev[cc] = _inc(v.tensor_copy(
                                MIX0H[:, cc * TILE:(cc + 1) * TILE],
                                banks.ap(6 + (cc % 2))), s_v)
                        for cc in (p, p + 1):
                            _wt(v, cpev[cc])
                            ev = _inc(v.tensor_tensor(
                                MIX0L[:, cc * TILE:(cc + 1) * TILE],
                                banks.ap(6 + (cc % 2)),
                                MIX0H[:, cc * TILE:(cc + 1) * TILE],
                                OP.subtract), s_v)
                            banks.release(6 + (cc % 2), ev)
                            if cc == NCC - 1:
                                S["zf"] = ev

                if not go():
                    break
                emit_mix(1, OMR)
                xres_ld = [None] * TPT

                def emit_xres_load(tch, ti=ti):
                    sl = tch % 2

                    @block.gpsimd
                    def _(g, tch=tch, sl=sl, ti=ti):
                        _wt(g, S["xio_free"][sl])
                        row0 = te + ti * TILE + tch * 128
                        ins = g.dma_start(out=XIO[:, sl * C:(sl + 1) * C],
                                          in_=x_slab[row0:row0 + 128, :])
                        ins.then_inc(s_xr[sl].h, 16)
                        s_xr[sl].v += 16
                        S["xr_cnt"][sl] += 1
                        xres_ld[tch] = (s_xr[sl], 16 * S["xr_cnt"][sl])

                emit_xres_load(0)
                emit_xres_load(1)

                oevts = [None] * 8

                def o_evac(b, tch, dh, evt):
                    sl = tch % 2

                    @block.vector
                    def _(v, b=b, tch=tch, dh=dh, evt=evt, sl=sl):
                        _wt(v, evt)
                        _wt(v, xres_ld[tch])
                        _wt(v, S["acct_free"])
                        S["acct_free"] = None
                        ev = _inc(v.tensor_tensor(
                            ACCT[:, tch * C + dh * 512:
                                 tch * C + (dh + 1) * 512],
                            banks.ap(b),
                            XIO[:, sl * C + dh * 512: sl * C + (dh + 1) * 512],
                            OP.add), s_v)
                        banks.release(b, ev)
                        oevts[b] = ev
                        if dh == 1:
                            S["xio_free"][sl] = ev
                    if dh == 1 and tch + 2 < TPT:
                        emit_xres_load(tch + 2)

                emit_proj_xstat("wo", (MIX0H, MIX0L), lambda: S["zf"], o_evac,
                                mix_free_key=0)

                if not go():
                    break
                emit_mix(0, OMK)
                emit_save_xnb()
                if ti + 1 < nt:
                    emit_ln_stats("emit", ti + 1)

                frevts = [None] * 8

                def fr_evac(b, tch, dh, evt):
                    @block.scalar
                    def _(s, b=b, evt=evt):
                        _wt(s, evt)
                        ev = _inc(s.activation(rf_sl(b), banks.ap(b),
                                               ACT.Sigmoid), s_s)
                        banks.release(b, ev)
                        frevts[b] = ev

                emit_proj_xstat("wfr", (MIX1H, MIX1L),
                                lambda: S["mix_ready"][1], fr_evac,
                                mix_free_key=1)

                if not go():
                    break
                fkevts = [None] * 8
                emit_proj_wstat("wfk", 0, KFM, fkevts, "kfm_free",
                                evac_act=ACT.Relu)
                if ti + 1 < nt:
                    emit_ln_finish()

                if not go():
                    break

                @block.vector
                def _(v):
                    for e in fkevts:
                        _wt(v, e)
                    ev = _inc(v.tensor_tensor(KFM[:, :], KFM[:, :], KFM[:, :],
                                              OP.mult), s_v)
                    _wt(v, ev)
                    _wt(v, S["mix_free"][1])
                    vbar(v, v.tensor_copy(MIX1H[:, :], KFM[:, :]))
                    ev = vbar(v, v.tensor_tensor(MIX1L[:, :], KFM[:, :],
                                                 MIX1H[:, :], OP.subtract))
                    S["kact"] = ev
                    S["kfm_free"] = ev

                if ti + 1 < nt:
                    emit_mix(0, VEC["maak"])

                out_evts = [None] * 8

                def fv_evac(b, tch, dh, evt):
                    @block.vector
                    def _(v, b=b, tch=tch, dh=dh, evt=evt):
                        _wt(v, evt)
                        _wt(v, frevts[b])
                        _wt(v, oevts[b])
                        sl = slice(tch * C + dh * 512, tch * C + (dh + 1) * 512)
                        vbar(v, v.tensor_tensor(VTM[:, sl], banks.ap(b),
                                                rf_sl(b), OP.mult))
                        ev = _inc(v.tensor_tensor(VTM[:, sl], VTM[:, sl],
                                                  ACCT[:, sl], OP.add), s_v)
                        banks.release(b, ev)
                        out_evts[b] = ev

                emit_proj_xstat("wfv", (MIX1H, MIX1L), lambda: S["kact"],
                                fv_evac, mix_free_key=1)

                for tch in range(TPT):
                    @block.gpsimd
                    def _(g, tch=tch, ti=ti):
                        _wt(g, out_evts[2 * tch])
                        _wt(g, out_evts[2 * tch + 1])
                        row0 = ti * TILE + tch * 128
                        ins = g.dma_start(out=out_d[row0:row0 + 128, :],
                                          in_=VTM[:, tch * C:(tch + 1) * C])
                        ins.then_inc(s_gst.h, 16)
                        s_gst.v += 16

                S["acct_free"] = out_evts[7]
                S["rfm_free"] = S["rs"]
                S["vtm_free"] = (s_gst, s_gst.v)

            _BUFS = {"XNT": XNT, "BUFA": BUFA, "D": D,
                     "KFM": KFM, "VTM": VTM, "RFM": RFM,
                     "ACCT": ACCT, "KTIL": KTIL, "RS": RS, "APR": APR,
                     "S_ALL": S_ALL, "MWT": MWT, "MLAM": MLAM, "MK": MK,
                     "WSBC": WSBC, "MGAM": MGAM, "XNB": XNB, "LNW": LNW, "LNS": LNS, "XIO": XIO,
                     "WS128": WS128}
            _BF16S = {"MIX0S": (MIX0H, MIX0L), "MIX1S": (MIX1H, MIX1L),
                      "MIX0HH": (MIX0H, None), "MIX0LL": (MIX0L, None),
                      "MIX1HH": (MIX1H, None), "MIX1LL": (MIX1L, None)}
            if DBG_TAP in _BF16S:
                bh_, bl_ = _BF16S[DBG_TAP]

                @block.scalar
                def _(s):
                    _wt(s, (s_v, s_v.v))
                    _wt(s, (s_t, s_t.v))
                    _inc(s.activation(ACCT[:, :], bh_[:, :], ACT.Identity), s_s)

                if bl_ is not None:
                    @block.vector
                    def _(v):
                        _wt(v, (s_s, s_s.v))
                        _inc(v.tensor_tensor(ACCT[:, :], ACCT[:, :], bl_[:, :],
                                             OP.add), s_v)
                _BUFS[DBG_TAP] = ACCT
            if DBG_TAP is not None or s_gst.v == 0:
                @block.gpsimd
                def _(g):
                    _wt(g, (s_v, s_v.v))
                    _wt(g, (s_s, s_s.v))
                    _wt(g, (s_t, s_t.v))
                    if DBG_TAP is None:
                        ins = g.dma_start(out=out_d[0:128, 0:8], in_=XNB[:, :])
                        ins.then_inc(s_gst.h, 16)
                        s_gst.v += 16
                    else:
                        buf = _BUFS[DBG_TAP]
                        ncol = buf.shape[1]
                        nrow = min((ncol + C - 1) // C, te // 128)
                        for r in range(nrow):
                            w = min(C, ncol - r * C)
                            ins = g.dma_start(
                                out=out_d[r * 128:(r + 1) * 128, 0:w],
                                in_=buf[:, r * C:r * C + w])
                            ins.then_inc(s_gst.h, 16)
                            s_gst.v += 16

            total_st = s_gst.v

            @block.gpsimd
            def _(g):
                g.wait_ge(s_gst.h, total_st)

    return nc


def kernel(**inputs):
    ins = {k: np.asarray(v, dtype=np.float32) for k, v in inputs.items()}
    x = ins["L0_attn_x"]
    B, T, _ = x.shape

    def vec_tile(v):
        return np.ascontiguousarray(v.reshape(8, 128).T)

    common = {
        "wk": np.ascontiguousarray(ins["L0_attn_Wk"].T),
        "wv": np.ascontiguousarray(ins["L0_attn_Wv"].T),
        "wr": np.ascontiguousarray(ins["L0_attn_Wr"].T),
        "wg": np.ascontiguousarray(ins["L0_attn_Wg"].T),
        "wo": np.ascontiguousarray(ins["L0_attn_Wo"].T),
        "wfr": np.ascontiguousarray(ins["L0_ffn_W_r"].T),
        "wfk": np.ascontiguousarray(ins["L0_ffn_W_k"].T),
        "wfv": np.ascontiguousarray(ins["L0_ffn_W_v"].T),
        "maak": vec_tile(ins["L0_attn_time_maa_k"].reshape(-1)),
        "maav": vec_tile(ins["L0_attn_time_maa_v"].reshape(-1)),
        "maar": vec_tile(ins["L0_attn_time_maa_r"].reshape(-1)),
        "maag": vec_tile(ins["L0_attn_time_maa_g"].reshape(-1)),
        "mur": vec_tile(ins["L0_ffn_mu_r"].reshape(-1)),
        "muk": vec_tile(ins["L0_ffn_mu_k"].reshape(-1)),
        "dec_row": np.ascontiguousarray(ins["L0_attn_time_decay"].reshape(1, 16)),
        "faa_row": np.ascontiguousarray(ins["L0_attn_time_faaaa"].reshape(1, 16)),
        "lnw_row": np.ascontiguousarray(ins["L0_attn_ln_w"].reshape(1, C)),
        "lnb_row": np.ascontiguousarray(ins["L0_attn_ln_b"].reshape(1, C)),
    }

    in_maps = []
    for core in range(8):
        b, half = core // 2, core % 2
        if half == 0:
            slab = np.concatenate(
                [np.zeros((TE, C), np.float32), x[b, 0:TE]], axis=0)
        else:
            slab = np.ascontiguousarray(x[b, 0:2 * TE])
        m = dict(common)
        m["x_slab"] = slab
        in_maps.append(m)

    nc = build_nc(NT)
    res = run_bass_kernel_spmd(nc, in_maps, core_ids=list(range(8)))

    out = np.empty((B, T, C), np.float32)
    for core in range(8):
        b, half = core // 2, core % 2
        out[b, half * TE:(half + 1) * TE] = res.results[core]["out"]
    return out


# version 22 (final)
# speedup vs baseline: 1.0428x; 1.0135x over previous
import numpy as np
from contextlib import ExitStack

import concourse.bass as bass
import concourse.mybir as mybir
from concourse.bass_utils import run_bass_kernel_spmd

F32 = mybir.dt.float32
BF16 = mybir.dt.bfloat16
OP = mybir.AluOpType
ACT = mybir.ActivationFunctionType
AX = mybir.AxisListType

C = 1024
H = 16
NDIM = 64
TILE = 512
NCC = 8
TPT = 4
NT = 4
TE = 2048
EPS_LN = 1e-5
EPS_GN = 1e-5 * 64
RING = 2
STOP = None
DBG_Y = "both"
DBG_TAP = None


class Sem:
    def __init__(self, st, nc, name):
        self.h = st.enter_context(nc.semaphore(name))
        self.v = 0

    def tick(self, n=1):
        self.v += n
        return (self, self.v)


def _inc(instr, sem, n=1):
    instr.then_inc(sem.h, n)
    return sem.tick(n)


def _wt(eng, evt):
    if evt is not None:
        eng.wait_ge(evt[0].h, evt[1])


class Banks:
    def __init__(self, ps):
        self.ps = ps
        self.rel = [None] * 8

    def ap(self, b, lo=0, hi=512):
        return self.ps[:, b * 512 + lo: b * 512 + hi]

    def span(self, b0, nbank):
        return self.ps[:, b0 * 512:(b0 + nbank) * 512]

    def acquire(self, eng, b):
        _wt(eng, self.rel[b])
        self.rel[b] = None

    def release(self, b, evt):
        self.rel[b] = evt


def build_nc(nt=NT):
    _stage = [0]

    def go():
        _stage[0] += 1
        return STOP is None or _stage[0] <= STOP

    nc = bass.Bass("TRN2", target_bir_lowering=False,
                   detect_race_conditions=False,
                   dynamic_dma_scratch_size=8192)

    te = nt * TILE
    x_slab = nc.dram_tensor("x_slab", [2 * te, C], F32, kind="ExternalInput").ap()
    wT = {}
    for w in ("wk", "wv", "wr", "wg", "wo", "wfr", "wfk", "wfv"):
        wT[w] = nc.dram_tensor(w, [C, C], F32, kind="ExternalInput").ap()
    vec_d = {}
    for m in ("maak", "maav", "maar", "maag", "mur", "muk"):
        vec_d[m] = nc.dram_tensor(m, [128, 8], F32, kind="ExternalInput").ap()
    dec_d = nc.dram_tensor("dec_row", [1, 16], F32, kind="ExternalInput").ap()
    faa_d = nc.dram_tensor("faa_row", [1, 16], F32, kind="ExternalInput").ap()
    lnw_d = nc.dram_tensor("lnw_row", [1, C], F32, kind="ExternalInput").ap()
    lnb_d = nc.dram_tensor("lnb_row", [1, C], F32, kind="ExternalInput").ap()
    out_d = nc.dram_tensor("out", [te, C], F32, kind="ExternalOutput").ap()
    wsp_d = {w: nc.dram_tensor("wsp_" + w, [NCC * 128, 2 * C], BF16,
                               kind="Internal").ap() for w in
             ("wk", "wv", "wr", "wg", "wo", "wfk", "wfr", "wfv")}

    jj, ii = np.meshgrid(np.arange(128), np.arange(128), indexing="ij")
    c_grid = nc.inline_tensor((jj + 1 - ii).astype(np.float32), name="c_gridt")
    c_tri = nc.inline_tensor((ii > jj).astype(np.float32), name="c_trit")
    c_diag = nc.inline_tensor((ii == jj).astype(np.float32), name="c_diagt")
    c_rampi = nc.inline_tensor(
        np.broadcast_to(-np.arange(128, dtype=np.float32), (128, 128)).copy(),
        name="c_rampi")
    c_rampj = nc.inline_tensor(
        np.broadcast_to((np.arange(128, dtype=np.float32) - 127.0)[:, None],
                        (128, 64)).copy(), name="c_rampj")
    c_ident = nc.inline_tensor(np.eye(128, dtype=np.float32), name="c_ident")

    A = nc.alloc_sbuf_tensor
    XNT = A("XNT", [128, NCC * TILE], F32)
    XNB = A("XNB", [128, 8], F32)
    BUFA = A("BUFA", [128, 4096], F32)
    D = A("D", [128, NCC * TILE], F32)
    MIX0H = A("MIX0H", [128, NCC * TILE], BF16)
    MIX0L = A("MIX0L", [128, NCC * TILE], BF16)
    MIX1H = A("MIX1H", [128, NCC * TILE], BF16)
    MIX1L = A("MIX1L", [128, NCC * TILE], BF16)
    SETA = A("SETA", [128, 1024], F32)
    MIXT = SETA[:, 0:1024]
    KFM = A("KFM", [128, NCC * TILE], F32)
    VTM = A("VTM", [128, TPT * C], F32)
    RFM = A("RFM", [128, NCC * TILE], F32)
    ACCT = A("ACCT", [128, TPT * C], F32)
    KTIL = A("KTIL", [128, C], F32)
    SSCR = A("SSCR", [128, 512], F32)
    RS = A("RS", [128, C], F32)
    APR = A("APR", [128, H * 128], F32)
    WF32 = APR[:, 0:2048]
    _KTILB = KTIL.bitcast(BF16)
    _RSB = RS.bitcast(BF16)
    def _wsphl(k):
        buf = _KTILB if (k % 4) < 2 else _RSB
        base = (k % 2) * 1024
        return buf[:, base:base + 1024]
    S_ALL = A("S_ALL", [128, 512], F32)
    S_ALL2 = A("S_ALL2", [128, 512], F32)
    XIO = A("XIO", [128, 2 * C], F32)
    LNS = A("LNS", [128, 32], F32)
    GNS = A("GNS", [128, 320], F32)
    IDENT = A("IDENT", [128, 128], F32)
    GRIDT = SETA[:, 0:128]
    TRIT = SETA[:, 128:256]
    DIAGT = SETA[:, 256:384]
    RAMPI = SETA[:, 384:512]
    RAMPJ = SETA[:, 512:576]
    MWT = A("MWT", [128, H * 128], F32)
    MLAM = A("MLAM", [128, NCC * 128], F32)
    MK = A("MK", [128, C], F32)
    WSBC = A("WSBC", [128, 512], F32)
    MGAM = A("MGAM", [128, C], F32)
    MBET = A("MBET", [128, C], F32)
    DECB = A("DECB", [128, 16], F32)
    FAAB = A("FAAB", [128, 16], F32)
    LNW = A("LNW", [128, 16], F32)
    WS128 = A("WS128", [128, 16], F32)
    ONES64 = A("ONES64", [128, 64], F32)
    NEGHALF = A("NEGHALF", [128, 1], F32)
    INVC = A("INVC", [128, 1], F32)
    NEGINVC = A("NEGINVC", [128, 1], F32)
    NEGEPSC = A("NEGEPSC", [128, 1], F32)
    INVN = A("INVN", [128, 1], F32)
    NEGEPSN = A("NEGEPSN", [128, 1], F32)
    VEC = {m: A("t_" + m, [128, 8], F32) for m in
           ("maak", "maav", "maar", "maag", "mur", "muk")}
    OMR = A("OMR", [128, 8], F32)
    OMK = A("OMK", [128, 8], F32)
    MSCR = SETA[:, 576:704]
    WRINGB = A("WRINGB", [128, RING * 2 * C], BF16)

    PS = nc.alloc_psum_tensor("PS", [128, 8 * 512], F32)
    banks = Banks(PS)

    st = ExitStack()
    with st:
        s_t = Sem(st, nc, "s_t")
        s_v = Sem(st, nc, "s_v")
        s_s = Sem(st, nc, "s_s")
        s_cst = Sem(st, nc, "s_cst")
        s_x = [Sem(st, nc, f"s_x{i}") for i in range(2)]
        s_xr = [Sem(st, nc, f"s_xr{i}") for i in range(2)]
        s_w = [Sem(st, nc, f"s_w{i}") for i in range(RING)]
        s_gst = Sem(st, nc, "s_gst")
        s_wf = Sem(st, nc, "s_wf")
        s_wsp = Sem(st, nc, "s_wsp")

        S = {
            "xio_free": [None, None],
            "xio_loaded": [None, None],
            "x_cnt": [0, 0],
            "xr_cnt": [0, 0],
            "mix_free": {0: None, 1: None},
            "mix_ready": {0: None, 1: None},
            "mixt_free": None,
            "d_ready": None,
            "xnt_ready": None,
            "xnt_free": None,
            "bufa_free": None,
            "kfm_free": None,
            "vtm_free": None,
            "rfm_free": None,
            "acct_free": None,
            "supd": None,
            "yread": None,
            "setup_done": None,
        }

        def vbar(v, ins):
            ev = _inc(ins, s_v)
            _wt(v, ev)
            return ev

        _scnt = [0]

        def state_buf():
            return (S_ALL, S_ALL2)[_scnt[0] % 2]

        def state_dst():
            return (S_ALL, S_ALL2)[(_scnt[0] + 1) % 2]

        wring = {"n": 0, "scount": [0] * RING, "freed": [], "loadevt": []}

        def wring_slot_ap(i):
            s = i % RING
            return WRINGB[:, s * 2 * C:(s + 1) * 2 * C]

        def wring_load(sync, dram_ap, gate=None):
            i = wring["n"]
            s = i % RING
            if i >= RING:
                _wt(sync, wring["freed"][i - RING])
            _wt(sync, gate)
            ins = sync.dma_start(out=wring_slot_ap(i), in_=dram_ap)
            wring["scount"][s] += 1
            ins.then_inc(s_w[s].h, 16)
            s_w[s].v += 16
            wring["loadevt"].append((s_w[s], 16 * wring["scount"][s]))
            wring["freed"].append(None)
            wring["n"] += 1
            return i

        with nc.Block() as block:
            @block.sync
            def _(sync):
                for ap_, dst in ((c_ident.ap(), IDENT), (c_grid.ap(), GRIDT),
                                 (c_tri.ap(), TRIT), (c_diag.ap(), DIAGT),
                                 (c_rampi.ap(), RAMPI), (c_rampj.ap(), RAMPJ)):
                    _inc(sync.dma_start(out=dst[:, :], in_=ap_[:, :]), s_cst, 16)
                for m in VEC:
                    _inc(sync.dma_start(out=VEC[m][:, :], in_=vec_d[m][:, :]),
                         s_cst, 16)
                _inc(sync.dma_start(out=DECB[:, :],
                                    in_=dec_d.broadcast_to([128, 16])), s_cst, 16)
                _inc(sync.dma_start(out=FAAB[:, :],
                                    in_=faa_d.broadcast_to([128, 16])), s_cst, 16)
                _inc(sync.dma_start(out=MGAM[:, :],
                                    in_=lnw_d.broadcast_to([128, C])), s_cst, 16)
                _inc(sync.dma_start(out=MBET[:, :],
                                    in_=lnb_d.broadcast_to([128, C])), s_cst, 16)

            n_cst = s_cst.v

            @block.vector
            def _(v):
                v.wait_ge(s_cst.h, n_cst)
                v.memset(XNB[:, :], 0.0)
                v.memset(S_ALL[:, :], 0.0)
                v.memset(ONES64[:, :], 1.0)
                v.memset(NEGHALF[:, :], -0.5)
                v.memset(INVC[:, :], 1.0 / C)
                v.memset(NEGINVC[:, :], -1.0 / C)
                v.memset(NEGEPSC[:, :], -EPS_LN)
                v.memset(INVN[:, :], 1.0 / NDIM)
                v.memset(NEGEPSN[:, :], -EPS_GN)
                v.tensor_scalar(OMR[:, :], VEC["mur"][:, :], -1.0, 1.0,
                                OP.mult, OP.add)
                ev = _inc(v.tensor_scalar(OMK[:, :], VEC["muk"][:, :], -1.0, 1.0,
                                          OP.mult, OP.add), s_v)
                S["vsetup"] = ev

            @block.scalar
            def _(s):
                s.wait_ge(s_cst.h, n_cst)
                _wt(s, S["vsetup"])
                ev = _inc(s.activation(LNW[:, :], DECB[:, :], ACT.Exp), s_s)
                S["expd"] = ev

            @block.vector
            def _(v):
                _wt(v, S["expd"])
                ev = _inc(v.tensor_scalar_mul(WS128[:, :], LNW[:, :], -128.0),
                          s_v)
                S["ex128"] = ev

            @block.scalar
            def _(s):
                _wt(s, S["ex128"])
                s.activation(WS128[:, :], WS128[:, :], ACT.Exp)
                ev = None
                for h in range(H):
                    hcol = (h % 2) * 1024 + (h // 2) * 128
                    s.activation(MWT[:, hcol:hcol + 128], GRIDT[:, :],
                                 ACT.Exp, scale=LNW[:, h:h + 1])
                    s.activation(MK[:, h * 64:(h + 1) * 64], RAMPJ[:, :],
                                 ACT.Exp, scale=LNW[:, h:h + 1])
                    bp = (h % 2) * 64
                    s.activation(WSBC[bp:bp + 64,
                                      (h // 2) * 64:(h // 2) * 64 + 64],
                                 ONES64[0:64, :], ACT.Identity,
                                 scale=WS128[bp:bp + 64, h:h + 1])
                for cc in range(NCC):
                    s.activation(MLAM[0:64, cc * 128:(cc + 1) * 128],
                                 RAMPI[0:64, :], ACT.Exp,
                                 scale=LNW[0:64, 2 * cc:2 * cc + 1])
                    ev = s.activation(MLAM[64:128, cc * 128:(cc + 1) * 128],
                                      RAMPI[64:128, :], ACT.Exp,
                                      scale=LNW[64:128, 2 * cc + 1:2 * cc + 2])
                S["ssetup"] = _inc(ev, s_s)

            @block.vector
            def _(v):
                _wt(v, S["ssetup"])
                ev = None
                for h in range(H):
                    hcol = (h % 2) * 1024 + (h // 2) * 128
                    blk = MWT[:, hcol:hcol + 128]
                    v.tensor_scalar_mul(MSCR[:, :], DIAGT[:, :],
                                        FAAB[:, h:h + 1])
                    vbar(v, v.tensor_tensor(blk, blk, TRIT[:, :], OP.mult))
                    ev = v.tensor_tensor(blk, blk, MSCR[:, :], OP.add)
                S["setup_done"] = _inc(ev, s_v)

            SPLIT_ORDER = ("wk", "wv", "wr", "wg", "wo", "wfk", "wfr", "wfv")
            split_done = {w: [None] * NCC for w in SPLIT_ORDER}
            sp = {"next": 0, "ld": {}, "tt": {}, "st": {}}
            sp_list = [(w, cc, h) for w in SPLIT_ORDER for cc in range(NCC)
                       for h in range(2)]

            def sp_slot(k, buf, width):
                p = (k % 4) * width
                return buf[:, p:p + width]

            def emit_split_pair():
                k0 = sp["next"]
                if k0 >= len(sp_list):
                    return
                ks = [k for k in (k0, k0 + 1) if k < len(sp_list)]
                sp["next"] = k0 + len(ks)

                @block.sync
                def _(sync):
                    for k in ks:
                        w, cc, h = sp_list[k]
                        _wt(sync, sp["tt"].get(k - 4))
                        ins = sync.dma_start(
                            out=sp_slot(k, WF32, 512),
                            in_=wT[w][cc * 128:(cc + 1) * 128,
                                      h * 512:(h + 1) * 512])
                        sp["ld"][k] = _inc(ins, s_wf, 16)

                @block.vector
                def _(v):
                    cpev = {}
                    for k in ks:
                        _wt(v, sp["ld"][k])
                        _wt(v, sp["st"].get(k - 4))
                        hl = _wsphl(k)
                        cpev[k] = _inc(v.tensor_copy(hl[:, 0:512],
                                                     sp_slot(k, WF32, 512)),
                                       s_v)
                    for k in ks:
                        _wt(v, cpev[k])
                        hl = _wsphl(k)
                        ins = v.tensor_tensor(
                            hl[:, 512:1024],
                            sp_slot(k, WF32, 512),
                            hl[:, 0:512], OP.subtract)
                        sp["tt"][k] = _inc(ins, s_v)

                @block.scalar
                def _(s):
                    for k in ks:
                        w, cc, h = sp_list[k]
                        _wt(s, sp["tt"][k])
                        rows = slice(cc * 128, (cc + 1) * 128)
                        ins = s.dma_start(
                            out=wsp_d[w][rows, h * 1024:(h + 1) * 1024],
                            in_=_wsphl(k))
                        ev = _inc(ins, s_wsp, 16)
                        sp["st"][k] = ev
                        if h == 1:
                            split_done[w][cc] = ev

            def emit_one_split():
                emit_split_pair()

            def pump_splits(n=2):
                for _ in range(n):
                    emit_one_split()

            def ensure_splits(w):
                while split_done[w][NCC - 1] is None:
                    emit_one_split()

            def emit_ln_stats(kind, ti):
                row0 = (0 if kind == "warm" else te) + ti * TILE
                sqev = {}
                for tch in range(TPT):
                    pump_splits(1)
                    sl = tch % 2

                    @block.sync
                    def _(sync, tch=tch, sl=sl, row0=row0):
                        _wt(sync, S["xio_free"][sl])
                        ins = sync.dma_start(
                            out=XIO[:, sl * C:(sl + 1) * C],
                            in_=x_slab[row0 + tch * 128: row0 + (tch + 1) * 128, :])
                        ins.then_inc(s_x[sl].h, 16)
                        s_x[sl].v += 16
                        S["x_cnt"][sl] += 1
                        S["xio_loaded"][sl] = (s_x[sl], 16 * S["x_cnt"][sl])

                    ld_evt = S["xio_loaded"][sl]

                    @block.vector
                    def _(v, tch=tch, sl=sl, ld_evt=ld_evt, sqev=sqev):
                        _wt(v, ld_evt)
                        if tch == 0:
                            _wt(v, S["setup_done"])
                            _wt(v, S["bufa_free"])
                        xio = XIO[:, sl * C:(sl + 1) * C]
                        v.tensor_copy(BUFA[:, tch * C:(tch + 1) * C], xio)
                        v.tensor_reduce(LNS[:, tch:tch + 1], xio, AX.X, OP.add)
                        ev = _inc(v.tensor_tensor(
                            D[:, tch * C:(tch + 1) * C], xio, xio, OP.mult),
                            s_v)
                        sqev[tch] = ev
                        S["xio_free"][sl] = ev
                        if tch > 0:
                            _wt(v, sqev[tch - 1])
                            _inc(v.tensor_reduce(
                                LNS[:, 4 + tch - 1:4 + tch],
                                D[:, (tch - 1) * C:tch * C],
                                AX.X, OP.add), s_v)

                @block.vector
                def _(v, sqev=sqev):
                    _wt(v, sqev[TPT - 1])
                    _inc(v.tensor_reduce(
                        LNS[:, 4 + TPT - 1:4 + TPT],
                        D[:, (TPT - 1) * C:TPT * C],
                        AX.X, OP.add), s_v)

                red_evt = (s_v, s_v.v)

                @block.scalar
                def _(s, red_evt=red_evt):
                    _wt(s, red_evt)
                    s.activation(LNS[:, 8:12], LNS[:, 0:4], ACT.Identity,
                                 scale=INVC[:, :])
                    s.activation(LNS[:, 12:16], LNS[:, 0:4], ACT.Identity,
                                 scale=NEGINVC[:, :])
                    s.activation(LNS[:, 16:20], LNS[:, 8:12], ACT.Square)
                    _inc(s.activation(LNS[:, 16:20], LNS[:, 16:20],
                                      ACT.Identity, bias=NEGEPSC[:, :]), s_s)
                msq_evt = (s_s, s_s.v)

                @block.vector
                def _(v, msq_evt=msq_evt):
                    _wt(v, msq_evt)
                    _inc(v.scalar_tensor_tensor(
                        LNS[:, 20:24], LNS[:, 4:8], 1.0 / C,
                        LNS[:, 16:20], OP.mult, OP.subtract), s_v)
                var_evt = (s_v, s_v.v)

                @block.scalar
                def _(s, var_evt=var_evt):
                    _wt(s, var_evt)
                    s.activation(LNS[:, 24:28], LNS[:, 20:24], ACT.Ln)
                    _inc(s.activation(LNS[:, 28:32], LNS[:, 24:28], ACT.Exp,
                                      scale=NEGHALF[:, :]), s_s)
                sd_evt = (s_s, s_s.v)

                @block.vector
                def _(v, sd_evt=sd_evt):
                    _wt(v, sd_evt)
                    ev = None
                    for tch in range(TPT):
                        dst = BUFA[:, tch * C:(tch + 1) * C]
                        ev = v.tensor_scalar(dst, dst,
                                             LNS[:, 12 + tch:13 + tch],
                                             LNS[:, 28 + tch:29 + tch],
                                             OP.add, OP.mult)
                    _inc(ev, s_v)

                S["norm_evt"] = (s_v, s_v.v)

            def emit_ln_finish():
                norm_evt = S["norm_evt"]
                last_tr = None
                for cc in range(NCC):
                    bk = 6 + (cc % 2)

                    @block.tensor
                    def _(t, cc=cc, bk=bk, norm_evt=norm_evt):
                        _wt(t, norm_evt)
                        banks.acquire(t, bk)
                        for tch in range(TPT):
                            _inc(t.transpose(
                                banks.ap(bk, tch * 128, (tch + 1) * 128),
                                BUFA[:, tch * C + cc * 128:
                                     tch * C + (cc + 1) * 128],
                                IDENT[:, :]), s_t)
                    tr_evt = (s_t, s_t.v)
                    last_tr = tr_evt

                    @block.vector
                    def _(v, cc=cc, bk=bk, tr_evt=tr_evt):
                        _wt(v, tr_evt)
                        _wt(v, S["xnt_free"])
                        ev = _inc(v.tensor_copy(XNT[:, cc * TILE:(cc + 1) * TILE],
                                                banks.ap(bk)), s_v)
                        banks.release(bk, ev)

                S["xnt_ready"] = (s_v, s_v.v)
                S["xnt_free"] = None
                S["bufa_free"] = last_tr

                @block.vector
                def _(v):
                    _wt(v, S["xnt_ready"])
                    xv3 = XNT[:, :].rearrange("p (c t) -> p c t", t=TILE)
                    dv3 = D[:, :].rearrange("p (c t) -> p c t", t=TILE)
                    v.tensor_tensor(dv3[:, :, 0:1],
                                    XNB[:, :].rearrange("p (c o) -> p c o", o=1),
                                    xv3[:, :, 0:1], OP.subtract)
                    ev = _inc(v.tensor_tensor(dv3[:, :, 1:TILE],
                                              xv3[:, :, 0:TILE - 1],
                                              xv3[:, :, 1:TILE], OP.subtract),
                              s_v)
                    S["d_ready"] = ev

            def emit_mix(mslot, vtile):
                pump_splits(2)
                bh = MIX0H if mslot == 0 else MIX1H
                bl = MIX0L if mslot == 0 else MIX1L
                evs = {}

                def mslot_ap(cc):
                    p = (cc % 2) * 512
                    return MIXT[:, p:p + 512]

                @block.vector
                def _(v, vtile=vtile, mslot=mslot, bh=bh, bl=bl):
                    def stt(cc):
                        sl = slice(cc * TILE, (cc + 1) * TILE)
                        if cc == 0:
                            _wt(v, S["d_ready"])
                            _wt(v, S["mix_free"][mslot])
                            _wt(v, S["setup_done"])
                            _wt(v, S["mixt_free"])
                        _wt(v, evs.get(("lo", cc - 2)))
                        evs[("t", cc)] = _inc(v.scalar_tensor_tensor(
                            mslot_ap(cc), D[:, sl], vtile[:, cc:cc + 1],
                            XNT[:, sl], OP.mult, OP.add), s_v)

                    def cp(cc):
                        sl = slice(cc * TILE, (cc + 1) * TILE)
                        _wt(v, evs[("t", cc)])
                        evs[("hi", cc)] = _inc(
                            v.tensor_copy(bh[:, sl], mslot_ap(cc)), s_v)

                    def lo(cc):
                        sl = slice(cc * TILE, (cc + 1) * TILE)
                        _wt(v, evs[("hi", cc)])
                        evs[("lo", cc)] = _inc(
                            v.tensor_tensor(bl[:, sl], mslot_ap(cc),
                                            bh[:, sl], OP.subtract), s_v)

                    for k in range(0, NCC, 2):
                        stt(k)
                        stt(k + 1)
                        cp(k)
                        cp(k + 1)
                        lo(k)
                        lo(k + 1)
                    S["mixt_free"] = evs[("lo", NCC - 1)]
                    S["mix_ready"][mslot] = evs[("lo", NCC - 1)]

            def emit_proj_wstat(wname, mslot, dst, out_evts, dst_free_key,
                                evac_act=None):
                pump_splits(2)
                ensure_splits(wname)
                mh = MIX0H if mslot == 0 else MIX1H
                ml = MIX0L if mslot == 0 else MIX1L
                loads = [None] * NCC
                bank_evts = [None] * 8

                @block.sync
                def _(sync):
                    for cc in range(RING):
                        loads[cc] = wring_load(sync,
                                               wsp_d[wname][cc * 128:(cc + 1) * 128, :],
                                               gate=split_done[wname][cc])

                for cc in range(NCC):
                    @block.tensor
                    def _(t, cc=cc):
                        _wt(t, wring["loadevt"][loads[cc]])
                        if cc == 0:
                            _wt(t, S["mix_ready"][mslot])
                            for d in range(8):
                                banks.acquire(t, d)
                        slot = wring_slot_ap(loads[cc])
                        msl = slice(cc * TILE, (cc + 1) * TILE)
                        ins = None
                        for d in range(8):
                            base = (d // 4) * 1024 + (d % 4) * 128
                            w_h = slot[:, base:base + 128]
                            w_l = slot[:, base + 512:base + 640]
                            ins = t.matmul(banks.ap(d), w_h, mh[:, msl],
                                           start=(cc == 0), stop=False)
                            ins = t.matmul(banks.ap(d), w_h, ml[:, msl],
                                           start=False, stop=False)
                            ins = t.matmul(banks.ap(d), w_l, mh[:, msl],
                                           start=False, stop=(cc == NCC - 1))
                            if cc == NCC - 1:
                                bank_evts[d] = _inc(ins, s_t)
                        if cc < NCC - 1:
                            wring["freed"][loads[cc]] = _inc(ins, s_t)
                        else:
                            wring["freed"][loads[cc]] = bank_evts[7]
                            S["mix_free"][mslot] = bank_evts[7]

                    if cc + RING < NCC:
                        @block.sync
                        def _(sync, cc=cc):
                            loads[cc + RING] = wring_load(
                                sync,
                                wsp_d[wname][(cc + RING) * 128:(cc + RING + 1) * 128, :],
                                gate=split_done[wname][cc + RING])

                dst_free = S[dst_free_key]
                S[dst_free_key] = None
                for d in range(8):
                    if evac_act is not None:
                        @block.scalar
                        def _(s, d=d):
                            _wt(s, bank_evts[d])
                            _wt(s, dst_free)
                            ev = _inc(s.activation(
                                dst[:, d * TILE:(d + 1) * TILE], banks.ap(d),
                                evac_act), s_s)
                            banks.release(d, ev)
                            out_evts[d] = ev
                    else:
                        @block.vector
                        def _(v, d=d):
                            _wt(v, bank_evts[d])
                            _wt(v, dst_free)
                            ev = _inc(v.tensor_copy(
                                dst[:, d * TILE:(d + 1) * TILE], banks.ap(d)),
                                s_v)
                            banks.release(d, ev)
                            out_evts[d] = ev

            def emit_proj_xstat(wname, stats, stat_ready_fn, evac_fn,
                                mix_free_key=None):
                pump_splits(2)
                ensure_splits(wname)
                sh_buf, sl_buf = stats
                loads = [None] * NCC
                bank_evts = [None] * 8

                @block.sync
                def _(sync):
                    for cc in range(RING):
                        loads[cc] = wring_load(sync,
                                               wsp_d[wname][cc * 128:(cc + 1) * 128, :],
                                               gate=split_done[wname][cc])

                for cc in range(NCC):
                    @block.tensor
                    def _(t, cc=cc):
                        _wt(t, wring["loadevt"][loads[cc]])
                        if cc == 0:
                            _wt(t, stat_ready_fn())
                            for b in range(8):
                                banks.acquire(t, b)
                        slot = wring_slot_ap(loads[cc])
                        ins = None
                        for tch in range(TPT):
                            ssl = slice(cc * TILE + tch * 128,
                                        cc * TILE + (tch + 1) * 128)
                            for dh in range(2):
                                b = 2 * tch + dh
                                movh = slot[:, dh * 1024:dh * 1024 + 512]
                                movl = slot[:, dh * 1024 + 512:
                                            (dh + 1) * 1024]
                                ins = t.matmul(banks.ap(b), sh_buf[:, ssl],
                                               movh, start=(cc == 0),
                                               stop=False)
                                ins = t.matmul(banks.ap(b), sh_buf[:, ssl],
                                               movl, start=False, stop=False)
                                ins = t.matmul(banks.ap(b), sl_buf[:, ssl],
                                               movh, start=False,
                                               stop=(cc == NCC - 1))
                                if cc == NCC - 1:
                                    bank_evts[b] = _inc(ins, s_t)
                        if cc < NCC - 1:
                            wring["freed"][loads[cc]] = _inc(ins, s_t)
                        else:
                            wring["freed"][loads[cc]] = bank_evts[7]
                            if mix_free_key is not None:
                                S["mix_free"][mix_free_key] = bank_evts[7]

                    if cc + RING < NCC:
                        @block.sync
                        def _(sync, cc=cc):
                            loads[cc + RING] = wring_load(
                                sync,
                                wsp_d[wname][(cc + RING) * 128:(cc + RING + 1) * 128, :],
                                gate=split_done[wname][cc + RING])

                for tch in range(TPT):
                    for dh in range(2):
                        evac_fn(2 * tch + dh, tch, dh, bank_evts[2 * tch + dh])

            def evac_simple(dst, dst_free_key=None, mask=None, act=None,
                            evts_out=None):
                holder = {"df": S[dst_free_key] if dst_free_key else None}
                if dst_free_key:
                    S[dst_free_key] = None

                def f(b, tch, dh, evt):
                    sl = slice(tch * C + dh * 512, tch * C + (dh + 1) * 512)
                    if mask is not None:
                        @block.vector
                        def _(v, b=b, sl=sl, dh=dh, evt=evt):
                            _wt(v, evt)
                            _wt(v, holder["df"])
                            ev = _inc(v.tensor_tensor(
                                dst[:, sl], banks.ap(b),
                                mask[:, dh * 512:(dh + 1) * 512], OP.mult), s_v)
                            banks.release(b, ev)
                            if evts_out is not None:
                                evts_out[b] = ev
                    elif act is not None:
                        @block.scalar
                        def _(s, b=b, sl=sl, evt=evt):
                            _wt(s, evt)
                            _wt(s, holder["df"])
                            ev = _inc(s.activation(dst[:, sl], banks.ap(b), act),
                                      s_s)
                            banks.release(b, ev)
                            if evts_out is not None:
                                evts_out[b] = ev
                    else:
                        if b % 2 == 0:
                            @block.scalar
                            def _(s, b=b, sl=sl, evt=evt):
                                _wt(s, evt)
                                _wt(s, holder["df"])
                                ev = _inc(s.activation(dst[:, sl],
                                                       banks.ap(b),
                                                       ACT.Identity), s_s)
                                banks.release(b, ev)
                                if evts_out is not None:
                                    evts_out[b] = ev
                        else:
                            @block.vector
                            def _(v, b=b, sl=sl, evt=evt):
                                _wt(v, evt)
                                _wt(v, holder["df"])
                                ev = _inc(v.tensor_copy(dst[:, sl],
                                                        banks.ap(b)), s_v)
                                banks.release(b, ev)
                                if evts_out is not None:
                                    evts_out[b] = ev
                return f

            def rf_sl(b):
                if b < 2:
                    return KTIL[:, b * 512:(b + 1) * 512]
                if b < 4:
                    return RS[:, (b - 2) * 512:(b - 1) * 512]
                return APR[:, (b - 4) * 512:(b - 3) * 512]

            def emit_save_xnb():
                @block.vector
                def _(v):
                    xl = XNT[:, :].rearrange("p (c t) -> p c t", t=TILE)
                    ev = _inc(v.tensor_copy(
                        XNB[:, :].rearrange("p (c o) -> p c o", o=1),
                        xl[:, :, TILE - 1:TILE]), s_v)
                    S["xnt_free"] = ev

            emit_ln_stats("warm", 0)
            emit_ln_finish()
            for wt_i in range(nt):
                if not go():
                    break
                if not go():
                    break
                if wt_i == 0:
                    emit_mix(0, VEC["maak"])
                emit_mix(1, VEC["maav"])
                if wt_i + 1 < nt:
                    emit_ln_stats("warm", wt_i + 1)
                else:
                    emit_ln_stats("emit", 0)

                kevts = [None] * 8
                if not go():
                    break
                emit_proj_xstat("wk", (MIX0H, MIX0L), lambda: S["mix_ready"][0],
                                evac_simple(KFM, "kfm_free", mask=MK,
                                            evts_out=kevts), mix_free_key=0)
                vevts = [None] * 8
                if not go():
                    break
                emit_proj_xstat("wv", (MIX1H, MIX1L), lambda: S["mix_ready"][1],
                                evac_simple(VTM, "vtm_free", evts_out=vevts),
                                mix_free_key=1)
                pump_splits(12)
                if not go():
                    break

                for ch in range(TPT):
                    ube = (ch % 2) * 2
                    ubo = ube + 1

                    @block.tensor
                    def _(t, ch=ch, ube=ube, ubo=ubo):
                        _wt(t, kevts[2 * ch])
                        _wt(t, kevts[2 * ch + 1])
                        _wt(t, vevts[2 * ch])
                        _wt(t, vevts[2 * ch + 1])
                        banks.acquire(t, ube)
                        banks.acquire(t, ubo)
                        for par, ub in ((0, ube), (1, ubo)):
                            bp = par * 64
                            ins = None
                            for hh in range(8):
                                h = 2 * hh + par
                                ins = t.matmul(
                                    PS[bp:bp + 64,
                                       ub * 512 + hh * 64:
                                       ub * 512 + hh * 64 + 64],
                                    KFM[:, ch * C + h * 64: ch * C + (h + 1) * 64],
                                    VTM[:, ch * C + h * 64: ch * C + (h + 1) * 64],
                                    start=True, stop=True,
                                    tile_position=(0, bp))
                            S[f"u_mm{par}"] = _inc(ins, s_t)

                    @block.vector
                    def _(v, ube=ube, ubo=ubo):
                        _wt(v, S["u_mm0"])
                        _wt(v, S["u_mm1"])
                        sa, sd = state_buf(), state_dst()
                        _scnt[0] += 1
                        vbar(v, v.tensor_tensor(SSCR[:, :], sa[:, :],
                                                WSBC[:, :], OP.mult))
                        v.tensor_tensor(sd[0:64, :], SSCR[0:64, :],
                                        banks.ap(ube)[0:64, :], OP.add)
                        ev = _inc(v.tensor_tensor(sd[64:128, :],
                                                  SSCR[64:128, :],
                                                  banks.ap(ubo)[64:128, :],
                                                  OP.add), s_v)
                        S["supd"] = ev
                        banks.release(ube, ev)
                        banks.release(ubo, ev)

                S["kfm_free"] = S["u_mm1"]
                S["vtm_free"] = S["u_mm1"]
                pump_splits(12)
                emit_save_xnb()
                emit_ln_finish()
                emit_mix(0, VEC["maak"])

            ensure_splits("wfv")
            S["split_fin"] = (s_wsp, s_wsp.v)

            for ti in range(nt):
                if not go():
                    break
                if not go():
                    break
                emit_mix(1, VEC["maav"])

                kevts = [None] * 8
                if not go():
                    break
                emit_proj_wstat("wk", 0, KFM, kevts, "kfm_free")
                vevts = [None] * 8
                if not go():
                    break
                emit_proj_xstat("wv", (MIX1H, MIX1L), lambda: S["mix_ready"][1],
                                evac_simple(VTM, "vtm_free", evts_out=vevts),
                                mix_free_key=1)
                emit_mix(0, VEC["maar"])
                revts = [None] * 8
                if not go():
                    break
                emit_proj_wstat("wr", 0, RFM, revts, "rfm_free")
                if not go():
                    break

                yn_evts = [None] * TPT

                def emit_scan_prep(ch):
                    @block.tensor
                    def _(t, ch=ch):
                        for e in kevts:
                            _wt(t, e)
                        banks.acquire(t, 6)
                        banks.acquire(t, 7)
                        ins = None
                        for cc in range(NCC):
                            bk = 6 + cc // 4
                            ins = t.transpose(
                                banks.ap(bk, (cc % 4) * 128, (cc % 4 + 1) * 128),
                                KFM[:, cc * TILE + ch * 128:
                                    cc * TILE + (ch + 1) * 128],
                                IDENT[:, :])
                            _inc(ins, s_t)
                        S["ktr"] = (s_t, s_t.v)

                    @block.vector
                    def _(v, ch=ch, ti=ti):
                        _wt(v, S["ktr"])
                        if ti == 0 and ch == 0:
                            _wt(v, S["split_fin"])
                        v.tensor_tensor(KTIL[:, 0:512], banks.ap(6),
                                        MK[:, 0:512], OP.mult)
                        ev = _inc(v.tensor_tensor(KTIL[:, 512:1024],
                                                  banks.ap(7), MK[:, 512:1024],
                                                  OP.mult), s_v)
                        banks.release(6, ev)
                        banks.release(7, ev)
                        rsv = RS[:, :].rearrange("p (c t) -> p c t", t=128)
                        rfm3 = RFM[:, :].rearrange("p (c t) -> p c t", t=TILE)
                        for e in revts:
                            _wt(v, e)
                        ev = _inc(v.tensor_tensor(
                            rsv, rfm3[:, :, ch * 128:(ch + 1) * 128],
                            MLAM[:, :].rearrange("p (c t) -> p c t", t=128),
                            OP.mult), s_v)
                        S["rs"] = ev

                    @block.tensor
                    def _(t, ch=ch):
                        for e in revts:
                            _wt(t, e)
                        for b in range(4):
                            banks.acquire(t, b)
                        ins = None
                        for h in range(H):
                            cc = h // 2
                            bp = (h % 2) * 64
                            hcol = (h % 2) * 1024 + (h // 2) * 128
                            tok = slice(cc * TILE + ch * 128,
                                        cc * TILE + (ch + 1) * 128)
                            ins = t.matmul(
                                banks.span(0, 4)[:, hcol:hcol + 128],
                                KFM[bp:bp + 64, tok],
                                RFM[bp:bp + 64, tok],
                                start=True, stop=True,
                                tile_position=(bp, 0))
                        S["amm"] = _inc(ins, s_t)

                    @block.vector
                    def _(v, ch=ch, ti=ti):
                        _wt(v, S["amm"])
                        if ti == 0 and ch == 0:
                            _wt(v, S["split_fin"])
                        v.tensor_tensor(APR[:, 0:1024], banks.span(0, 2),
                                        MWT[:, 0:1024], OP.mult)
                        ev = _inc(v.tensor_tensor(APR[:, 1024:2048],
                                                  banks.span(2, 2),
                                                  MWT[:, 1024:2048], OP.mult),
                                  s_v)
                        for b in range(4):
                            banks.release(b, ev)
                        S["apr"] = ev

                emit_scan_prep(0)
                for ch in range(TPT):
                    if not go():
                        break

                    @block.tensor
                    def _(t, ch=ch):
                        _wt(t, S["apr"])
                        _wt(t, S["supd"])
                        _wt(t, S["rs"])
                        banks.acquire(t, 4)
                        banks.acquire(t, 5)
                        sa = state_buf()
                        ins = None
                        for h in range(H):
                            bp = (h % 2) * 64
                            cc = h // 2
                            hcol = (h % 2) * 1024 + (h // 2) * 128
                            ysl = banks.span(4, 2)[:, h * 64:(h + 1) * 64]
                            ins = t.matmul(ysl, APR[:, hcol:hcol + 128],
                                           VTM[:, ch * C + h * 64:
                                               ch * C + (h + 1) * 64],
                                           start=True, stop=False)
                            ins = t.matmul(
                                ysl,
                                RS[bp:bp + 64, cc * 128:(cc + 1) * 128],
                                sa[bp:bp + 64,
                                   (h // 2) * 64:(h // 2) * 64 + 64],
                                start=False, stop=True,
                                tile_position=(bp, 0))
                        S["yread"] = _inc(ins, s_t)

                    @block.tensor
                    def _(t, ch=ch):
                        banks.acquire(t, 2)
                        banks.acquire(t, 3)
                        for par, ub in ((0, 2), (1, 3)):
                            bp = par * 64
                            ins = None
                            for hh in range(8):
                                h = 2 * hh + par
                                ins = t.matmul(
                                    PS[bp:bp + 64,
                                       ub * 512 + hh * 64:
                                       ub * 512 + hh * 64 + 64],
                                    KTIL[:, h * 64:(h + 1) * 64],
                                    VTM[:, ch * C + h * 64: ch * C + (h + 1) * 64],
                                    start=True, stop=True,
                                    tile_position=(0, bp))
                            S[f"u_mm{par}"] = _inc(ins, s_t)

                    @block.vector
                    def _(v, ch=ch):
                        _wt(v, S["u_mm0"])
                        _wt(v, S["u_mm1"])
                        sa, sd = state_buf(), state_dst()
                        _scnt[0] += 1
                        vbar(v, v.tensor_tensor(SSCR[:, :], sa[:, :],
                                                WSBC[:, :], OP.mult))
                        v.tensor_tensor(sd[0:64, :], SSCR[0:64, :],
                                        banks.ap(2)[0:64, :], OP.add)
                        ev = _inc(v.tensor_tensor(sd[64:128, :],
                                                  SSCR[64:128, :],
                                                  banks.ap(3)[64:128, :],
                                                  OP.add), s_v)
                        S["supd"] = ev
                        banks.release(2, ev)
                        banks.release(3, ev)

                    @block.vector
                    def _(v, ch=ch):
                        _wt(v, S["yread"])
                        ev = _inc(v.tensor_copy(BUFA[:, ch * C:(ch + 1) * C],
                                                banks.span(4, 2)), s_v)
                        banks.release(4, ev)
                        banks.release(5, ev)
                        yn_evts[ch] = ev

                    if ch + 1 < TPT:
                        emit_scan_prep(ch + 1)

                if not go():
                    break
                emit_mix(1, VEC["maag"])

                @block.vector
                def _(v):
                    for e in yn_evts:
                        _wt(v, e)
                    y3 = BUFA[:, :].rearrange("p (g n) -> p g n", n=NDIM)
                    _inc(v.tensor_reduce(GNS[:, 0:64], y3, AX.X, OP.add), s_v)
                mean_evt = (s_v, s_v.v)

                @block.scalar
                def _(s, mean_evt=mean_evt):
                    _wt(s, mean_evt)
                    _wt(s, S["acct_free"])
                    S["acct_free"] = None
                    _inc(s.activation(ACCT[:, :], BUFA[:, :], ACT.Square), s_s)
                sq_evt = (s_s, s_s.v)

                @block.vector
                def _(v, sq_evt=sq_evt):
                    _wt(v, sq_evt)
                    sq3 = ACCT[:, :].rearrange("p (g n) -> p g n", n=NDIM)
                    _inc(v.tensor_reduce(GNS[:, 64:128], sq3, AX.X, OP.add),
                         s_v)
                red2_evt = (s_v, s_v.v)

                @block.scalar
                def _(s, red2_evt=red2_evt):
                    _wt(s, red2_evt)
                    s.activation(GNS[:, 128:192], GNS[:, 0:64], ACT.Identity,
                                 scale=INVN[:, :])
                    s.activation(GNS[:, 0:64], GNS[:, 128:192], ACT.Square)
                    s.activation(GNS[:, 0:64], GNS[:, 0:64], ACT.Identity,
                                 bias=NEGEPSN[:, :])
                    _inc(s.activation(GNS[:, 64:128], GNS[:, 64:128],
                                      ACT.Identity), s_s)
                gst_evt = (s_s, s_s.v)

                @block.vector
                def _(v, gst_evt=gst_evt):
                    _wt(v, gst_evt)
                    _inc(v.scalar_tensor_tensor(
                        GNS[:, 192:256], GNS[:, 64:128], 1.0 / NDIM,
                        GNS[:, 0:64], OP.mult, OP.subtract), s_v)
                var_evt = (s_v, s_v.v)

                @block.scalar
                def _(s, var_evt=var_evt):
                    _wt(s, var_evt)
                    s.activation(GNS[:, 64:128], GNS[:, 192:256], ACT.Ln)
                    _inc(s.activation(GNS[:, 256:320], GNS[:, 64:128],
                                      ACT.Exp, scale=NEGHALF[:, :]), s_s)
                sd_evt = (s_s, s_s.v)

                @block.vector
                def _(v, sd_evt=sd_evt):
                    _wt(v, sd_evt)
                    mb = GNS[:, 128:192].rearrange(
                        "p (g o) -> p g o", o=1).broadcast_to([128, 64, NDIM])
                    rb = GNS[:, 256:320].rearrange(
                        "p (g o) -> p g o", o=1).broadcast_to([128, 64, NDIM])
                    y3 = BUFA[:, :].rearrange("p (g n) -> p g n", n=NDIM)
                    vbar(v, v.tensor_tensor(y3, y3, mb, OP.subtract))
                    vbar(v, v.tensor_tensor(y3, y3, rb, OP.mult))
                    yc = BUFA[:, :].rearrange("p (a c) -> p a c", c=C)
                    gg = MGAM[:, :].rearrange(
                        "p (a c) -> p a c", a=1).broadcast_to([128, TPT, C])
                    bb = MBET[:, :].rearrange(
                        "p (a c) -> p a c", a=1).broadcast_to([128, TPT, C])
                    vbar(v, v.tensor_tensor(yc, yc, gg, OP.mult))
                    ev = _inc(v.tensor_tensor(yc, yc, bb, OP.add), s_v)
                    S["yn_all"] = ev

                gevts = [None] * 8
                emit_proj_xstat("wg", (MIX1H, MIX1L),
                                lambda: S["mix_ready"][1],
                                evac_simple(KFM, "kfm_free", act=ACT.Gelu,
                                            evts_out=gevts), mix_free_key=1)

                if not go():
                    break
                @block.vector
                def _(v):
                    for e in gevts:
                        _wt(v, e)
                    _wt(v, S["yn_all"])
                    _wt(v, S["vtm_free"])
                    ev = _inc(v.tensor_tensor(VTM[:, :], BUFA[:, :], KFM[:, :],
                                              OP.mult), s_v)
                    S["z_tm"] = ev
                    S["bufa_free"] = ev
                    S["vtm_free"] = None

                zf_free = S["mix_free"][0]
                S["mix_free"][0] = None
                for p in range(0, NCC, 2):
                    tr_evt = {}
                    for cc in (p, p + 1):
                        bk = 6 + (cc % 2)

                        @block.tensor
                        def _(t, cc=cc, bk=bk):
                            _wt(t, S["z_tm"])
                            banks.acquire(t, bk)
                            for tch in range(TPT):
                                _inc(t.transpose(
                                    banks.ap(bk, tch * 128, (tch + 1) * 128),
                                    VTM[:, tch * C + cc * 128:
                                        tch * C + (cc + 1) * 128],
                                    IDENT[:, :]), s_t)
                        tr_evt[cc] = (s_t, s_t.v)

                    @block.vector
                    def _(v, p=p, tr_evt=tr_evt, zf_free=zf_free):
                        cpev = {}
                        for cc in (p, p + 1):
                            _wt(v, tr_evt[cc])
                            if cc == 0:
                                _wt(v, zf_free)
                            cpev[cc] = _inc(v.tensor_copy(
                                MIX0H[:, cc * TILE:(cc + 1) * TILE],
                                banks.ap(6 + (cc % 2))), s_v)
                        for cc in (p, p + 1):
                            _wt(v, cpev[cc])
                            ev = _inc(v.tensor_tensor(
                                MIX0L[:, cc * TILE:(cc + 1) * TILE],
                                banks.ap(6 + (cc % 2)),
                                MIX0H[:, cc * TILE:(cc + 1) * TILE],
                                OP.subtract), s_v)
                            banks.release(6 + (cc % 2), ev)
                            if cc == NCC - 1:
                                S["zf"] = ev

                if not go():
                    break
                emit_mix(1, OMR)
                xres_ld = [None] * TPT

                def emit_xres_load(tch, ti=ti):
                    sl = tch % 2

                    @block.gpsimd
                    def _(g, tch=tch, sl=sl, ti=ti):
                        _wt(g, S["xio_free"][sl])
                        row0 = te + ti * TILE + tch * 128
                        ins = g.dma_start(out=XIO[:, sl * C:(sl + 1) * C],
                                          in_=x_slab[row0:row0 + 128, :])
                        ins.then_inc(s_xr[sl].h, 16)
                        s_xr[sl].v += 16
                        S["xr_cnt"][sl] += 1
                        xres_ld[tch] = (s_xr[sl], 16 * S["xr_cnt"][sl])

                emit_xres_load(0)
                emit_xres_load(1)

                oevts = [None] * 8

                def o_evac(b, tch, dh, evt):
                    sl = tch % 2

                    @block.vector
                    def _(v, b=b, tch=tch, dh=dh, evt=evt, sl=sl):
                        _wt(v, evt)
                        _wt(v, xres_ld[tch])
                        _wt(v, S["acct_free"])
                        S["acct_free"] = None
                        ev = _inc(v.tensor_tensor(
                            ACCT[:, tch * C + dh * 512:
                                 tch * C + (dh + 1) * 512],
                            banks.ap(b),
                            XIO[:, sl * C + dh * 512: sl * C + (dh + 1) * 512],
                            OP.add), s_v)
                        banks.release(b, ev)
                        oevts[b] = ev
                        if dh == 1:
                            S["xio_free"][sl] = ev
                    if dh == 1 and tch + 2 < TPT:
                        emit_xres_load(tch + 2)

                emit_proj_xstat("wo", (MIX0H, MIX0L), lambda: S["zf"], o_evac,
                                mix_free_key=0)

                if not go():
                    break
                emit_mix(0, OMK)
                emit_save_xnb()
                if ti + 1 < nt:
                    emit_ln_stats("emit", ti + 1)

                frevts = [None] * 8

                def fr_evac(b, tch, dh, evt):
                    @block.scalar
                    def _(s, b=b, evt=evt):
                        _wt(s, evt)
                        ev = _inc(s.activation(rf_sl(b), banks.ap(b),
                                               ACT.Sigmoid), s_s)
                        banks.release(b, ev)
                        frevts[b] = ev

                emit_proj_xstat("wfr", (MIX1H, MIX1L),
                                lambda: S["mix_ready"][1], fr_evac,
                                mix_free_key=1)

                if not go():
                    break
                fkevts = [None] * 8
                emit_proj_wstat("wfk", 0, KFM, fkevts, "kfm_free",
                                evac_act=ACT.Relu)
                if ti + 1 < nt:
                    emit_ln_finish()

                if not go():
                    break

                @block.vector
                def _(v):
                    for e in fkevts:
                        _wt(v, e)
                    ev = _inc(v.tensor_tensor(KFM[:, :], KFM[:, :], KFM[:, :],
                                              OP.mult), s_v)
                    _wt(v, ev)
                    _wt(v, S["mix_free"][1])
                    vbar(v, v.tensor_copy(MIX1H[:, :], KFM[:, :]))
                    ev = vbar(v, v.tensor_tensor(MIX1L[:, :], KFM[:, :],
                                                 MIX1H[:, :], OP.subtract))
                    S["kact"] = ev
                    S["kfm_free"] = ev

                if ti + 1 < nt:
                    emit_mix(0, VEC["maak"])

                out_evts = [None] * 8

                def fv_evac(b, tch, dh, evt):
                    @block.vector
                    def _(v, b=b, tch=tch, dh=dh, evt=evt):
                        _wt(v, evt)
                        _wt(v, frevts[b])
                        _wt(v, oevts[b])
                        sl = slice(tch * C + dh * 512, tch * C + (dh + 1) * 512)
                        vbar(v, v.tensor_tensor(VTM[:, sl], banks.ap(b),
                                                rf_sl(b), OP.mult))
                        ev = _inc(v.tensor_tensor(VTM[:, sl], VTM[:, sl],
                                                  ACCT[:, sl], OP.add), s_v)
                        banks.release(b, ev)
                        out_evts[b] = ev

                emit_proj_xstat("wfv", (MIX1H, MIX1L), lambda: S["kact"],
                                fv_evac, mix_free_key=1)

                for tch in range(TPT):
                    @block.gpsimd
                    def _(g, tch=tch, ti=ti):
                        _wt(g, out_evts[2 * tch])
                        _wt(g, out_evts[2 * tch + 1])
                        row0 = ti * TILE + tch * 128
                        ins = g.dma_start(out=out_d[row0:row0 + 128, :],
                                          in_=VTM[:, tch * C:(tch + 1) * C])
                        ins.then_inc(s_gst.h, 16)
                        s_gst.v += 16

                S["acct_free"] = out_evts[7]
                S["rfm_free"] = S["rs"]
                S["vtm_free"] = (s_gst, s_gst.v)

            _BUFS = {"XNT": XNT, "BUFA": BUFA, "D": D,
                     "KFM": KFM, "VTM": VTM, "RFM": RFM,
                     "ACCT": ACCT, "KTIL": KTIL, "RS": RS, "APR": APR,
                     "S_ALL": S_ALL, "MWT": MWT, "MLAM": MLAM, "MK": MK,
                     "WSBC": WSBC, "MGAM": MGAM, "XNB": XNB, "LNW": LNW, "LNS": LNS, "XIO": XIO,
                     "WS128": WS128}
            _BF16S = {"MIX0S": (MIX0H, MIX0L), "MIX1S": (MIX1H, MIX1L),
                      "MIX0HH": (MIX0H, None), "MIX0LL": (MIX0L, None),
                      "MIX1HH": (MIX1H, None), "MIX1LL": (MIX1L, None)}
            if DBG_TAP in _BF16S:
                bh_, bl_ = _BF16S[DBG_TAP]

                @block.scalar
                def _(s):
                    _wt(s, (s_v, s_v.v))
                    _wt(s, (s_t, s_t.v))
                    _inc(s.activation(ACCT[:, :], bh_[:, :], ACT.Identity), s_s)

                if bl_ is not None:
                    @block.vector
                    def _(v):
                        _wt(v, (s_s, s_s.v))
                        _inc(v.tensor_tensor(ACCT[:, :], ACCT[:, :], bl_[:, :],
                                             OP.add), s_v)
                _BUFS[DBG_TAP] = ACCT
            if DBG_TAP is not None or s_gst.v == 0:
                @block.gpsimd
                def _(g):
                    _wt(g, (s_v, s_v.v))
                    _wt(g, (s_s, s_s.v))
                    _wt(g, (s_t, s_t.v))
                    if DBG_TAP is None:
                        ins = g.dma_start(out=out_d[0:128, 0:8], in_=XNB[:, :])
                        ins.then_inc(s_gst.h, 16)
                        s_gst.v += 16
                    else:
                        buf = _BUFS[DBG_TAP]
                        ncol = buf.shape[1]
                        nrow = min((ncol + C - 1) // C, te // 128)
                        for r in range(nrow):
                            w = min(C, ncol - r * C)
                            ins = g.dma_start(
                                out=out_d[r * 128:(r + 1) * 128, 0:w],
                                in_=buf[:, r * C:r * C + w])
                            ins.then_inc(s_gst.h, 16)
                            s_gst.v += 16

            total_st = s_gst.v

            @block.gpsimd
            def _(g):
                g.wait_ge(s_gst.h, total_st)

    return nc


def kernel(**inputs):
    ins = {k: np.asarray(v, dtype=np.float32) for k, v in inputs.items()}
    x = ins["L0_attn_x"]
    B, T, _ = x.shape

    def vec_tile(v):
        return np.ascontiguousarray(v.reshape(8, 128).T)

    common = {
        "wk": np.ascontiguousarray(ins["L0_attn_Wk"].T),
        "wv": np.ascontiguousarray(ins["L0_attn_Wv"].T),
        "wr": np.ascontiguousarray(ins["L0_attn_Wr"].T),
        "wg": np.ascontiguousarray(ins["L0_attn_Wg"].T),
        "wo": np.ascontiguousarray(ins["L0_attn_Wo"].T),
        "wfr": np.ascontiguousarray(ins["L0_ffn_W_r"].T),
        "wfk": np.ascontiguousarray(ins["L0_ffn_W_k"].T),
        "wfv": np.ascontiguousarray(ins["L0_ffn_W_v"].T),
        "maak": vec_tile(ins["L0_attn_time_maa_k"].reshape(-1)),
        "maav": vec_tile(ins["L0_attn_time_maa_v"].reshape(-1)),
        "maar": vec_tile(ins["L0_attn_time_maa_r"].reshape(-1)),
        "maag": vec_tile(ins["L0_attn_time_maa_g"].reshape(-1)),
        "mur": vec_tile(ins["L0_ffn_mu_r"].reshape(-1)),
        "muk": vec_tile(ins["L0_ffn_mu_k"].reshape(-1)),
        "dec_row": np.ascontiguousarray(ins["L0_attn_time_decay"].reshape(1, 16)),
        "faa_row": np.ascontiguousarray(ins["L0_attn_time_faaaa"].reshape(1, 16)),
        "lnw_row": np.ascontiguousarray(ins["L0_attn_ln_w"].reshape(1, C)),
        "lnb_row": np.ascontiguousarray(ins["L0_attn_ln_b"].reshape(1, C)),
    }

    in_maps = []
    for core in range(8):
        b, half = core // 2, core % 2
        if half == 0:
            slab = np.concatenate(
                [np.zeros((TE, C), np.float32), x[b, 0:TE]], axis=0)
        else:
            slab = np.ascontiguousarray(x[b, 0:2 * TE])
        m = dict(common)
        m["x_slab"] = slab
        in_maps.append(m)

    nc = build_nc(NT)
    res = run_bass_kernel_spmd(nc, in_maps, core_ids=list(range(8)))

    out = np.empty((B, T, C), np.float32)
    for core in range(8):
        b, half = core // 2, core % 2
        out[b, half * TE:(half + 1) * TE] = res.results[core]["out"]
    return out
